# version 15
# baseline (speedup 1.0000x reference)
import sys
if '/opt/trn_rl_repo' not in sys.path:
    sys.path.insert(0, '/opt/trn_rl_repo')
import math
import contextlib
import numpy as np
import concourse.bass as bass
import concourse.mybir as mybir
from concourse.bass_utils import run_bass_kernel_spmd
from concourse.alu_op_type import AluOpType as Op

AF = mybir.ActivationFunctionType
AX = mybir.AxisListType
F32 = mybir.dt.float32
BF16 = mybir.dt.bfloat16
FP16 = mybir.dt.float16
I32 = mybir.dt.int32
U16 = mybir.dt.uint16
U8 = mybir.dt.uint8

P = 128
T = 2048
H = 2048
NH = 16
NE = 16
NEA = 17
NPL = 16

DBG = False
_CTX_RC = None


def _slopes(n):
    start = 2.0 ** (-2.0 ** (-(math.log2(n) - 3)))
    return np.array([start * start ** i for i in range(n)], dtype=np.float64)


def host_consts():
    c = {}
    c["identb"] = np.eye(P, dtype=np.float32)
    c["onesfull"] = np.ones((P, P), dtype=np.float32)
    freq = (1.0 / 1024.0) ** np.linspace(0.0, 1.0, 32, dtype=np.float64)
    th = np.arange(T, dtype=np.float64)[:, None] * freq[None, :]
    cosv, sinv = np.cos(th).T, np.sin(th).T
    rotA = np.ones((P, T), dtype=np.float64)
    rotB = np.zeros((P, T), dtype=np.float64)
    rotA[0:32] = cosv
    rotB[0:32] = sinv
    rotA[64:96] = cosv
    rotB[64:96] = -sinv
    c["rotA"] = rotA.astype(np.float32)
    c["rotB"] = rotB.astype(np.float32)
    psw = np.zeros((P, P), dtype=np.float32)
    for j in range(P):
        psw[(j + 64) % P, j] = 1.0
    c["pswap"] = psw
    sl = _slopes(NH)
    r = np.exp(-sl)
    i = np.arange(128, dtype=np.float64)
    c["_r_C"] = [float(v) for v in (r ** 128.0).astype(np.float32)]
    qd = (r[:, None] ** (i[None, :] + 1.0))
    c["qdec"] = np.broadcast_to(qd[:, None, :], (NH, P, 128)).transpose(1, 0, 2) \
        .reshape(P, NH * 128).astype(np.float32).copy()
    c["kdec"] = (r[None, :] ** (127.0 - i[:, None])).astype(np.float32)
    esel = np.zeros((P, NE * 128), dtype=np.float32)
    for e_ in range(NE):
        esel[e_, e_ * 128:(e_ + 1) * 128] = 1.0
    c["esel"] = esel
    diff = i[None, :] - i[:, None]
    dm = np.where(diff >= 0, r[:, None, None] ** np.maximum(diff, 0.0)[None], 0.0)
    c["dmaskT"] = dm.transpose(1, 0, 2).reshape(P, NH * 128).astype(np.float32).copy()
    return c


def prep_inputs(ins):
    d = {}
    w_, j_, p_ = np.meshgrid(np.arange(2), np.arange(8), np.arange(128), indexing="ij")
    perm = (1024 * w_ + 8 * p_ + j_).reshape(-1)
    f32c = lambda a: np.ascontiguousarray(np.asarray(a, dtype=np.float32))
    d["wq"] = np.ascontiguousarray(f32c(ins["L0_attn_Wq"])[perm, :])
    d["wk"] = np.ascontiguousarray(f32c(ins["L0_attn_Wk"])[perm, :])
    d["wv"] = np.ascontiguousarray(f32c(ins["L0_attn_Wv"])[perm, :])
    d["wo"] = f32c(ins["L0_attn_Wo"])
    d["rmsw"] = np.ascontiguousarray(f32c(ins["L0_attn_rms_w"]).reshape(NPL, P).T)
    d["gwT"] = np.ascontiguousarray(f32c(ins["L0_ffn_gate_w"]).T[perm, :])
    gup = np.concatenate([np.asarray(ins["L0_ffn_gate_up_packed"]),
                          np.asarray(ins["L0_ffn_shared_gate_up_packed"])[None]], axis=0)
    dnp = np.concatenate([np.asarray(ins["L0_ffn_down_packed"]),
                          np.asarray(ins["L0_ffn_shared_down_packed"])[None]], axis=0)
    d["gup"] = np.ascontiguousarray(gup.astype(np.int32))
    d["dnp"] = np.ascontiguousarray(dnp.astype(np.int32))
    gus = np.concatenate([f32c(ins["L0_ffn_gate_up_scales"]),
                          f32c(ins["L0_ffn_shared_gate_up_scales"])[None]], axis=0)
    dns = np.concatenate([f32c(ins["L0_ffn_down_scales"]),
                          f32c(ins["L0_ffn_shared_down_scales"])[None]], axis=0)
    g16 = np.arange(P) // 16
    sg = np.empty((NEA, 2, P, 6144), dtype=np.float32)
    for wt in range(2):
        sg[:, wt] = np.repeat(gus[:, 8 * wt + g16, :], 2, axis=2)
    d["sgu"] = np.ascontiguousarray(sg)
    d["sd0"] = np.ascontiguousarray(np.repeat(dns[:, g16, :], 2, axis=2))
    g_t = 8 + (np.arange(P) % 64) // 16
    d["sd1"] = np.ascontiguousarray(np.repeat(dns[:, g_t, :], 2, axis=2))
    return d


class Ctx:
    pass


def inc(ctx, instr, name, n=1):
    instr.then_inc(ctx.sem[name], n)
    ctx.cnt[name] += n
    return ctx.cnt[name]


def inc_k(ctx, instr, name, key, n=1):
    v = inc(ctx, instr, name, n)
    assert ctx.t[key] == v, f"preset mismatch {key}: preset={ctx.t[key]} actual={v}"
    return v


def preset(ctx, name, keys, n=1):
    base = ctx.cnt[name]
    for i, k in enumerate(keys):
        ctx.t[k] = base + (i + 1) * n


def wait(ctx, eng, name, val):
    if val > 0:
        eng.wait_ge(ctx.sem[name], val)


def dvsync(ctx, v, instr):
    t = inc(ctx, instr, "dve")
    wait(ctx, v, "dve", t)
    return t


def emit_decode_pass(ctx, v, src, scale_fp16, dst, q2):
    sb = ctx.sb
    n = src.shape[-1]
    d1, d2 = sb["d1"][:, 0:n], sb["d2"][:, 0:n]
    f2, f3 = sb["f2"][:, 0:n], sb["f3"][:, 0:n]
    if q2 == 0:
        v.tensor_scalar(d1, src, 7, None, Op.bitwise_and)
    else:
        v.tensor_scalar(d1, src, 4 * q2, 7, Op.logical_shift_right, Op.bitwise_and)
    v.tensor_scalar(d2, d1, 512, 14336, Op.mult, Op.add)
    v.tensor_scalar(f2, d2.bitcast(FP16), 2.0, -1.0, Op.mult, Op.add)
    v.tensor_tensor(f3, d2.bitcast(FP16), f2, Op.min)
    v.tensor_scalar(d1, src, 12 - 4 * q2, 0x8000, Op.logical_shift_left,
                    Op.bitwise_and)
    v.tensor_tensor(f2.bitcast(U16), f3.bitcast(U16), d1, Op.bitwise_or)
    return v.tensor_tensor(dst, f2, scale_fp16, Op.mult)


def build_nc():
    nc = bass.Bass("TRN2", target_bir_lowering=False, debug=False,
                   detect_race_conditions=False)
    stack = contextlib.ExitStack()
    ctx = Ctx()
    ctx.nc = nc
    ctx.sem, ctx.cnt, ctx.t = {}, {}, {}
    ctx.r_C = _CTX_RC

    dr = {}
    def DR(name, shape, dt, kind):
        dr[name] = nc.dram_tensor(name, shape, dt, kind=kind).ap()
    scr = "ExternalOutput" if DBG else "Internal"
    DR("x", [T, H], F32, "ExternalInput")
    DR("out", [T, H], F32, "ExternalOutput")
    for n in ["wq", "wk", "wv", "wo"]:
        DR(n, [H, H], F32, "ExternalInput")
    DR("rmsw", [P, NPL], F32, "ExternalInput")
    DR("gwT", [H, NE], F32, "ExternalInput")
    DR("gup", [NEA, 256, 3072], I32, "ExternalInput")
    DR("dnp", [NEA, 192, 2048], I32, "ExternalInput")
    DR("sgu", [NEA, 2, P, 6144], F32, "ExternalInput")
    DR("sd0", [NEA, P, 4096], F32, "ExternalInput")
    DR("sd1", [NEA, P, 4096], F32, "ExternalInput")
    for n, shape in [("identb", [P, P]), ("onesfull", [P, P]),
                     ("rotA", [P, T]), ("rotB", [P, T]), ("pswap", [P, P]),
                     ("qdec", [P, NH * 128]),
                     ("kdec", [P, NH]), ("dmaskT", [P, NH * 128]),
                     ("esel", [P, NE * 128])]:
        DR(n, shape, F32, "ExternalInput")
    DR("act_scr", [NEA, 12, P, T], BF16, scr)
    DR("dgu_scr", [NEA, P, 16, 2048], BF16, "Internal")
    DR("dgt_scr", [NEA, P, 16, 1024], BF16, "Internal")
    DR("ddn_scr", [NEA, 8, P, 4096], BF16, "Internal")
    DR("s_scr", [T, H], BF16, scr)
    DR("wos_scr", [NPL, P, H], BF16, scr)
    DR("at_spill", [P, 16384], BF16, "Internal")
    if DBG:
        DR("xnT_dbg", [P, NPL * T], F32, "ExternalOutput")
        DR("comb_dbg", [P, NPL * NE], F32, "ExternalOutput")
        DR("Y_dbg", [2, P, NPL * 1024], F32, "ExternalOutput")
    ctx.dr = dr

    for n in ["pe", "dve", "act", "gp", "dvp",
              "ld_c1", "ld_c2", "ld_rot", "ld_wo", "ld_x0", "ld_x1", "ld_pk", "ld_sc",
              "ld_wx", "ld_wos0", "ld_wos1", "ld_actin",
              "ld_ex0", "ld_ex1", "ld_dm", "ld_dt", "ld_dn",
              "st_wos", "st_act", "st_s0", "st_s1", "st_dec0", "st_dec1",
              "sto0", "sto1", "st_dbg"]:
        ctx.sem[n] = stack.enter_context(nc.semaphore(n))
        ctx.cnt[n] = 0

    sb = {}
    def S(name, shape, dt):
        sb[name] = stack.enter_context(nc.sbuf_tensor("s_" + name, shape, dt))
    S("AR1", [P, 65536], U8)
    S("AR2", [P, 45568], U8)
    S("AR3", [P, 65536], U8)
    S("identb", [P, P], BF16)
    S("identf", [P, P], F32)
    S("onesfull", [P, P], BF16)
    S("onescolb", [P, 1], BF16)
    S("rotA", [P, T], BF16)
    S("rotB", [P, T], BF16)
    S("pswap", [P, P], BF16)
    S("qdec", [P, NH * 128], BF16)
    S("kdec", [P, NH], F32)
    S("dmaskT", [P, NH * 128], BF16)
    S("rmsw", [P, NPL], F32)
    S("gwTp", [P, NPL * NE], BF16)
    S("combp", [P, NPL * NE], F32)
    S("d1", [P, 1024], U16)
    S("d2", [P, 1024], U16)
    S("f1", [P, 1024], FP16)
    S("f2", [P, 1024], FP16)
    S("f3", [P, 1024], FP16)
    S("lnst", [P, 16], F32)
    S("mx", [P, 8], F32)
    S("exw", [P, NE], F32)
    S("cp4", [P, NE], F32)
    S("cp4b", [P, NE], F32)
    S("mskb", [P, NE], F32)
    S("rstdtm", [P, NPL], F32)
    S("sg0", [P, 512], FP16)
    S("sg1", [P, 512], FP16)
    S("uc0", [P, 512], FP16)
    S("uc1", [P, 512], FP16)
    S("qdh", [P, 128], BF16)
    S("smh", [P, 128], BF16)
    S("smh2", [P, 128], BF16)
    S("kvt", [P, 128], F32)
    S("etmp", [P, 128], F32)
    ctx.sb = sb
    ctx.xnT = sb["AR1"][:].bitcast(BF16)

    ps = {}
    for n in ["pA", "pB", "pC", "pD", "pE", "pF"]:
        ps[n] = stack.enter_context(nc.psum_tensor(n, [P, 512], F32))
    ps["pT1"] = stack.enter_context(nc.psum_tensor("pT1", [P, 1024], BF16))
    ps["pT2"] = stack.enter_context(nc.psum_tensor("pT2", [P, 1024], BF16))
    ctx.ps = ps

    with nc.Block() as blk:
        emit_consts(ctx, blk)
        emit_boot_pre(ctx, blk)
        emit_A(ctx, blk)
        emit_boot_mid(ctx, blk)
        emit_B(ctx, blk)
        emit_boot(ctx, blk)
        emit_C(ctx, blk)
        emit_C2(ctx, blk)
        emit_DE(ctx, blk)
        emit_end(ctx, blk)

    stack.close()
    return ctx, nc


def emit_consts(ctx, blk):
    dr, sb = ctx.dr, ctx.sb

    def f(g):
        for n in ["identb", "onesfull", "pswap", "qdec", "dmaskT"]:
            inc(ctx, g.dma_start(out=sb[n][:], in_=dr[n]), "ld_c1", 16)
        inc(ctx, g.dma_start(out=sb["rotA"][:], in_=dr["esel"]), "ld_c1", 16)
        inc(ctx, g.dma_start(out=sb["onescolb"][:], in_=dr["onesfull"][0:P, 0:1],
                             allow_slow_non_contiguous=True), "ld_c1", 16)
        inc(ctx, g.dma_start(
            out=sb["gwTp"][:].rearrange("p (pl e) -> p pl e", pl=NPL),
            in_=dr["gwT"].rearrange("(pl p) e -> p pl e", pl=NPL, p=P)), "ld_c1", 16)
    blk.gpsimd(f)

    def f(s):
        inc(ctx, s.dma_start(out=sb["kdec"][:], in_=dr["kdec"]), "ld_c2", 16)
        inc(ctx, s.dma_start(out=sb["rmsw"][:], in_=dr["rmsw"]), "ld_c2", 16)
        inc(ctx, s.dma_start(out=sb["identf"][:], in_=dr["identb"]), "ld_c2", 16)
    blk.sync(f)
    ctx.t["c_ldg"] = ctx.cnt["ld_c1"]
    ctx.t["c_ldw"] = ctx.cnt["ld_c2"]


def emit_A(ctx, blk):
    dr, sb, ps = ctx.dr, ctx.sb, ctx.ps
    ar3 = sb["AR3"][:]
    xt = [ar3[:, 0:8192].bitcast(F32), ar3[:, 8192:16384].bitcast(F32)]
    sqb = ar3[:, 16384:24576].bitcast(F32)
    xnb = ar3[:, 24576:28672].bitcast(BF16)
    woin = ar3[:, 28672:36864].bitcast(F32)
    wob = ar3[:, 36864:40960].bitcast(BF16)

    preset(ctx, "dve", [("wosc", k) for k in range(NPL)])
    preset(ctx, "st_wos", [("wost", k) for k in range(NPL)], n=16)

    def f(s):
        wait(ctx, s, "ld_c2", ctx.t["c_ldw"])
        for k in range(NPL):
            if k >= 1:
                wait(ctx, s, "dve", ctx.t[("wosc", k - 1)])
            inc(ctx, s.dma_start(out=woin, in_=dr["wo"][k * P:(k + 1) * P, :]), "ld_wo", 16)
            ctx.t[("wol", k)] = ctx.cnt["ld_wo"]
    blk.sync(f)

    def f(v):
        for k in range(NPL):
            wait(ctx, v, "ld_wo", ctx.t[("wol", k)])
            if k >= 1:
                wait(ctx, v, "st_wos", ctx.t[("wost", k - 1)])
            inc_k(ctx, v.tensor_scalar(wob, woin, sb["rmsw"][:, k:k + 1], None, Op.mult),
                  "dve", ("wosc", k))
    blk.vector(f)

    def f(s):
        for k in range(NPL):
            wait(ctx, s, "dve", ctx.t[("wosc", k)])
            v = inc(ctx, s.dma_start(out=dr["wos_scr"][k], in_=wob), "st_wos", 16)
            assert v == ctx.t[("wost", k)]
    blk.scalar(f)

    for tt in range(16):
        def f(s, tt=tt):
            if tt >= 2:
                wait(ctx, s, "dve", ctx.t[("xnw", tt - 2)])
            sem = "ld_x0" if tt % 2 == 0 else "ld_x1"
            inc(ctx, s.dma_start(out=xt[tt % 2], in_=dr["x"][tt * P:(tt + 1) * P, :]), sem, 16)
            ctx.t[("xl", tt)] = ctx.cnt[sem]
        blk.sync(f)

        def f(v, tt=tt):
            wait(ctx, v, "ld_x0" if tt % 2 == 0 else "ld_x1", ctx.t[("xl", tt)])
            dvsync(ctx, v, v.reduce_sum(sb["lnst"][:, 0:1], xt[tt % 2], axis=AX.X))
            ctx.t[("mean", tt)] = dvsync(ctx, v, v.tensor_scalar(
                sb["lnst"][:, 1:2], sb["lnst"][:, 0:1], 1.0 / H, None, Op.mult))
        blk.vector(f)

        def f(sc, tt=tt):
            wait(ctx, sc, "ld_x0" if tt % 2 == 0 else "ld_x1", ctx.t[("xl", tt)])
            if tt >= 1:
                wait(ctx, sc, "dve", ctx.t[("var", tt - 1)])
            ctx.t[("sq", tt)] = inc(ctx, sc.activation(
                sqb, xt[tt % 2], AF.Square, accum_out=sb["lnst"][:, 2:3]), "act")
        blk.scalar(f)

        def f(v, tt=tt):
            wait(ctx, v, "act", ctx.t[("sq", tt)])
            v.tensor_scalar(sb["lnst"][:, 3:4], sb["lnst"][:, 2:3], 1.0 / H, 1e-5,
                            Op.mult, Op.add)
            dvsync(ctx, v, v.tensor_tensor(
                sb["lnst"][:, 4:5], sb["lnst"][:, 1:2], sb["lnst"][:, 1:2], Op.mult))
            ctx.t[("var", tt)] = inc(ctx, v.tensor_tensor(
                sb["lnst"][:, 5:6], sb["lnst"][:, 3:4], sb["lnst"][:, 4:5], Op.subtract), "dve")
        blk.vector(f)

        def f(sc, tt=tt):
            wait(ctx, sc, "dve", ctx.t[("var", tt)])
            ctx.t[("std", tt)] = inc(ctx, sc.activation(
                sb["lnst"][:, 6:7], sb["lnst"][:, 5:6], AF.Sqrt), "act")
        blk.scalar(f)

        def f(v, tt=tt):
            wait(ctx, v, "act", ctx.t[("std", tt)])
            if tt >= 1:
                wait(ctx, v, "pe", ctx.t[("tr", tt - 1, NPL - 1)])
            dvsync(ctx, v, v.reciprocal(sb["lnst"][:, 7:8], sb["lnst"][:, 6:7]))
            ctx.t[("xnw", tt)] = inc(ctx, v.tensor_scalar(
                xnb, xt[tt % 2], sb["lnst"][:, 1:2], sb["lnst"][:, 7:8],
                Op.subtract, Op.mult), "dve")
        blk.vector(f)

        preset(ctx, "dve", [("trev", tt, ti) for ti in range(NPL)])

        def f(te, tt=tt):
            if tt == 0:
                wait(ctx, te, "ld_c1", ctx.t["c_ldg"])
            wait(ctx, te, "dve", ctx.t[("xnw", tt)])
            for ti in range(NPL):
                gi = tt * NPL + ti
                if gi >= 2:
                    ptt, pti = divmod(gi - 2, NPL)
                    wait(ctx, te, "dve", ctx.t[("trev", ptt, pti)])
                w, j = ti // 8, ti % 8
                st = 1024 * w + j
                pT = ps["pT1"] if ti % 2 == 0 else ps["pT2"]
                ctx.t[("tr", tt, ti)] = inc(
                    ctx, te.transpose(pT[:, 0:128], xnb[:, st:st + 1017:8], sb["identb"][:]), "pe")
        blk.tensor(f)

        def f(v, tt=tt):
            for ti in range(NPL):
                wait(ctx, v, "pe", ctx.t[("tr", tt, ti)])
                pT = ps["pT1"] if ti % 2 == 0 else ps["pT2"]
                dst = ctx.xnT[:, ti * T + tt * P: ti * T + tt * P + P]
                inc_k(ctx, v.tensor_copy(out=dst, in_=pT[:, 0:128]), "dve", ("trev", tt, ti))
            for p_ in ctx.boot_passes[tt * 10:(tt + 1) * 10]:
                p_(v)
        blk.vector(f)
    ctx.t["xnT_done"] = ctx.cnt["dve"]

    if DBG:
        def f(g):
            wait(ctx, g, "dve", ctx.t["xnT_done"])
            inc(ctx, g.dma_start(out=ctx.dr["xnT_dbg"], in_=ctx.xnT), "st_dbg", 16)
        blk.gpsimd(f)


def emit_B(ctx, blk):
    sb, ps = ctx.sb, ctx.ps

    for tt in range(16):
        def f(te, tt=tt):
            if tt == 0:
                wait(ctx, te, "dve", ctx.t["xnT_done"])
                wait(ctx, te, "ld_c1", ctx.t["c_ldg"])
            pb = ps["pA"] if tt % 2 == 0 else ps["pB"]
            if tt >= 2:
                wait(ctx, te, "act", ctx.t[("ex", tt - 2)])
            for ti in range(NPL):
                mm = te.matmul(pb[:, 0:NE],
                               ctx.xnT[:, ti * T + tt * P: ti * T + tt * P + P],
                               sb["gwTp"][:, ti * NE:(ti + 1) * NE],
                               start=(ti == 0), stop=(ti == NPL - 1))
            ctx.t[("lg", tt)] = inc(ctx, mm, "pe")
        blk.tensor(f)

        def f(v, tt=tt):
            pb = ps["pA"] if tt % 2 == 0 else ps["pB"]
            wait(ctx, v, "pe", ctx.t[("lg", tt)])
            dvsync(ctx, v, v.reduce_max(sb["mx"][:, 0:1], pb[:, 0:NE], axis=AX.X))
            ctx.t[("nmx", tt)] = inc(ctx, v.tensor_scalar(
                sb["mx"][:, 1:2], sb["mx"][:, 0:1], -1.0, None, Op.mult), "dve")
        blk.vector(f)

        def f(sc, tt=tt):
            pb = ps["pA"] if tt % 2 == 0 else ps["pB"]
            wait(ctx, sc, "dve", ctx.t[("nmx", tt)])
            ctx.t[("ex", tt)] = inc(ctx, sc.activation(
                sb["exw"][:], pb[:, 0:NE], AF.Exp, bias=sb["mx"][:, 1:2]), "act")
        blk.scalar(f)

        def f(v, tt=tt):
            wait(ctx, v, "act", ctx.t[("ex", tt)])
            ex, mx = sb["exw"][:], sb["mx"]
            cps = [sb["cp4"][:], sb["cp4b"][:]]
            dvsync(ctx, v, v.tensor_copy(out=cps[0], in_=ex))
            for r_ in range(4):
                dvsync(ctx, v, v.reduce_max(mx[:, 2 + r_:3 + r_], cps[r_ % 2], axis=AX.X))
                if r_ < 3:
                    dvsync(ctx, v, v.tensor_scalar(sb["mskb"][:], cps[r_ % 2],
                                                   mx[:, 2 + r_:3 + r_], None, Op.is_ge))
                    dvsync(ctx, v, v.scalar_tensor_tensor(cps[(r_ + 1) % 2], sb["mskb"][:],
                                                          -2.0, cps[r_ % 2], Op.mult, Op.add))
            dvsync(ctx, v, v.tensor_tensor(sb["lnst"][:, 8:9], mx[:, 2:3], mx[:, 3:4], Op.add))
            dvsync(ctx, v, v.tensor_tensor(sb["lnst"][:, 9:10], mx[:, 4:5], mx[:, 5:6], Op.add))
            dvsync(ctx, v, v.tensor_tensor(sb["lnst"][:, 10:11], sb["lnst"][:, 8:9],
                                           sb["lnst"][:, 9:10], Op.add))
            dvsync(ctx, v, v.reciprocal(sb["lnst"][:, 11:12], sb["lnst"][:, 10:11]))
            dvsync(ctx, v, v.tensor_scalar(sb["mskb"][:], ex, mx[:, 5:6], None, Op.is_ge))
            dvsync(ctx, v, v.tensor_tensor(sb["cp4"][:], sb["mskb"][:], ex, Op.mult))
            ctx.t[("cmb", tt)] = inc(ctx, v.tensor_scalar(
                sb["combp"][:, tt * NE:(tt + 1) * NE], sb["cp4"][:],
                sb["lnst"][:, 11:12], None, Op.mult), "dve")
        blk.vector(f)
    ctx.t["comb_done"] = ctx.cnt["dve"]

    if DBG:
        def f(g):
            wait(ctx, g, "dve", ctx.t["comb_done"])
            inc(ctx, g.dma_start(out=ctx.dr["comb_dbg"], in_=sb["combp"][:]), "st_dbg", 16)
        blk.gpsimd(f)


PROD_NPASS = 80


def producer_passes(ctx, e):
    dr, sb = ctx.dr, ctx.sb
    ar2 = sb["AR2"][:]
    pkgu16 = ar2[:, 0:24576].bitcast(U16)
    pk0_16 = ar2[:, 0:8192].bitcast(U16)
    pk1_16 = ar2[:, 8192:16384].bitcast(U16)
    scbuf = ar2[:, 32768:40960].bitcast(FP16)
    stg = [ar2[:, 41472:43520].bitcast(BF16), ar2[:, 43520:45568].bitcast(BF16)]

    descs = []
    for wt in range(2):
        for q2 in range(4):
            for gu in range(2):
                for sub in range(2):
                    soff = wt * 6144 + gu * 3072 + sub * 1024
                    descs.append((wt, pkgu16[:, soff:soff + 1024],
                                  scbuf[:, gu * 2048 + sub * 1024:
                                        gu * 2048 + (sub + 1) * 1024],
                                  dr["dgu_scr"][e, :, (wt * 4 + q2) * 2 + gu,
                                                sub * 1024:(sub + 1) * 1024],
                                  q2, False))
    for wt in range(2):
        for q2 in range(4):
            for gu in range(2):
                soff = wt * 6144 + 2048 + gu * 3072
                descs.append((2 + wt, pkgu16[:, soff:soff + 1024],
                              scbuf[:, gu * 2048:gu * 2048 + 1024],
                              dr["dgt_scr"][e, :, (wt * 4 + q2) * 2 + gu, :],
                              q2, True))
    for hh in range(2):
        for wt2 in range(2):
            for q2 in range(4):
                for sub in range(2):
                    pksrc = pk0_16 if wt2 == 0 else pk1_16
                    descs.append((4 + hh,
                                  pksrc[:, hh * 2048 + sub * 1024:
                                        hh * 2048 + (sub + 1) * 1024],
                                  scbuf[:, wt2 * 2048 + sub * 1024:
                                        wt2 * 2048 + (sub + 1) * 1024],
                                  dr["ddn_scr"][e, wt2 * 4 + q2][
                                      :, hh * 2048 + sub * 1024:
                                      hh * 2048 + (sub + 1) * 1024],
                                  q2, False))
    assert len(descs) == PROD_NPASS

    if not hasattr(ctx, "plan"):
        ctx.plan = {"dvp": 0, "sd0": 0, "sd1": 0, "ld_pk": 0}
    ctx.t[("pkl", e)] = ctx.plan["ld_pk"] + 32
    ctx.t[("pkdl", e)] = ctx.plan["ld_pk"] + 80
    ctx.plan["ld_pk"] += 80
    dvb_ = ctx.plan["dvp"]
    for i in range(PROD_NPASS):
        ctx.t[("dvp", e, i)] = dvb_ + i + 1
    ctx.plan["dvp"] = dvb_ + PROD_NPASS
    b0, b1 = ctx.plan["sd0"], ctx.plan["sd1"]
    for i in range(PROD_NPASS):
        p = i % 2
        nprev = sum(1 for j in range(i + 1) if j % 2 == p)
        ctx.t[("sdec", e, i)] = (b0 if p == 0 else b1) + nprev * 16
    ctx.plan["sd0"] = b0 + (PROD_NPASS // 2) * 16
    ctx.plan["sd1"] = b1 + (PROD_NPASS // 2) * 16
    ctx.t[("pgu_done", e)] = ctx.t[("dvp", e, 47)]
    ctx.t[("pdone", e)] = ctx.t[("dvp", e, PROD_NPASS - 1)]

    prev_stg = (ctx.t.get(("stgfree", 0)), ctx.t.get(("stgfree", 1)))

    def mk_pass(i, d):
        grp, srcap, scap, dstdram, q2, tailperm = d

        def closure(v):
            if i == 0 or descs[i - 1][0] != grp:
                wait(ctx, v, "ld_sc", ctx.t[("pscl", e, grp)])
                if grp == 0:
                    wait(ctx, v, "ld_pk", ctx.t[("pkl", e)])
                if grp == 4:
                    wait(ctx, v, "ld_pk", ctx.t[("pkdl", e)])
                    wait(ctx, v, "gp", ctx.t[("pkz", e)])
            p = i % 2
            if i >= 2:
                wait(ctx, v, "st_dec0" if p == 0 else "st_dec1",
                     ctx.t[("sdec", e, i - 2)])
            elif prev_stg[p] is not None:
                wait(ctx, v, "st_dec0" if p == 0 else "st_dec1", prev_stg[p])
            dstap = stg[p][:, 0:1024]
            if tailperm:
                dstap = dstap.rearrange("p (jj m y) -> p m jj y", jj=2, m=64, y=8)
            inc_k(ctx, emit_decode_pass(ctx, v, srcap, scap, dstap, q2),
                  "dvp", ("dvp", e, i))
        return closure

    def mk_store(i, d):
        dstdram = d[3]

        def closure(s):
            wait(ctx, s, "dvp", ctx.t[("dvp", e, i)])
            p = i % 2
            sem = "st_dec0" if p == 0 else "st_dec1"
            v = inc(ctx, s.dma_start(out=dstdram, in_=stg[p][:, 0:1024]), sem, 16)
            assert v == ctx.t[("sdec", e, i)], (v, ctx.t[("sdec", e, i)])
        return closure

    passes = [mk_pass(i, d) for i, d in enumerate(descs)]
    stores = [mk_store(i, d) for i, d in enumerate(descs)]
    ctx.t[("stgfree", 0)] = ctx.t[("sdec", e, PROD_NPASS - 2)]
    ctx.t[("stgfree", 1)] = ctx.t[("sdec", e, PROD_NPASS - 1)]
    return passes, stores


def emit_producer_gp(ctx, blk, e, stores):

    emit_producer_gp_gu(ctx, blk, e, stores, ctx.t.get(("scfree2",)))
    ctx.t[("scfree2",)] = ctx.t[("dvp", e, 47)]
    emit_producer_gp_dn(ctx, blk, e, stores, ctx.t[("scfree2",)])
    ctx.t[("scfree2",)] = ctx.t[("dvp", e, PROD_NPASS - 1)]


def emit_producer_gp_gu(ctx, blk, e, stores, prev_dvp):
    dr, sb = ctx.dr, ctx.sb
    ar2 = sb["AR2"][:]
    scbuf = ar2[:, 32768:40960].bitcast(FP16)
    grp_begin = {0: 0, 1: 16, 2: 32, 3: 40, 4: 48}

    pkgu = ar2[:, 0:24576].bitcast(I32)

    def f(g, e=e):
        for grp in range(4):
            if grp > 0:
                for i in range(grp_begin[grp - 1], grp_begin[grp]):
                    stores[i](g)
                wait(ctx, g, "dvp", ctx.t[("dvp", e, {1: 15, 2: 31, 3: 39}[grp])])
            else:
                if prev_dvp is not None:
                    wait(ctx, g, "dvp", prev_dvp)
                inc(ctx, g.dma_start(out=pkgu[:, 0:3072],
                                     in_=dr["gup"][e, 0:128, :]), "ld_pk", 16)
                v = inc(ctx, g.dma_start(out=pkgu[:, 3072:6144],
                                         in_=dr["gup"][e, 128:256, :]), "ld_pk", 16)
                assert v == ctx.t[("pkl", e)], (v, ctx.t[("pkl", e)])
            if grp < 2:
                for gu in range(2):
                    inc(ctx, g.dma_start(out=scbuf[:, gu * 2048:gu * 2048 + 2048],
                                         in_=dr["sgu"][e, grp][:, gu * 3072:gu * 3072 + 2048]),
                        "ld_sc", 16)
            else:
                for gu in range(2):
                    inc(ctx, g.dma_start(out=scbuf[:, gu * 2048:gu * 2048 + 1024],
                                         in_=dr["sgu"][e, grp - 2][:, 2048 + gu * 3072:
                                                                   2048 + gu * 3072 + 1024]),
                        "ld_sc", 16)
            ctx.t[("pscl", e, grp)] = ctx.cnt["ld_sc"]
        for i in range(grp_begin[3], grp_begin[4]):
            stores[i](g)
    blk.gpsimd(f)


def emit_producer_gp_dn(ctx, blk, e, stores, prev_dvp):
    dr, sb = ctx.dr, ctx.sb
    ar2 = sb["AR2"][:]
    pk1_16 = ar2[:, 8192:16384].bitcast(U16)
    scbuf = ar2[:, 32768:40960].bitcast(FP16)

    pkd0 = ar2[:, 0:8192].bitcast(I32)
    pkd1 = ar2[:, 8192:16384].bitcast(I32)

    def f(g, e=e):
        for grp in (4, 5):
            if grp == 4:
                wait(ctx, g, "dvp", prev_dvp)
                inc(ctx, g.dma_start(out=pkd0[:], in_=dr["dnp"][e, 0:128, :]), "ld_pk", 16)
                inc(ctx, g.dma_start(out=pkd1[0:64, 0:2048],
                                     in_=dr["dnp"][e, 128:192, :]), "ld_pk", 16)
                src16 = dr["dnp"][e, 128:192, :].bitcast(U16)
                v = inc(ctx, g.dma_start(out=pk1_16[64:128, 0:4095],
                                         in_=src16[:, 1:4096]), "ld_pk", 16)
                assert v == ctx.t[("pkdl", e)], (v, ctx.t[("pkdl", e)])
            else:
                for i in range(48, 64):
                    stores[i](g)
                wait(ctx, g, "dvp", ctx.t[("dvp", e, 63)])
            hh = grp - 4
            inc(ctx, g.dma_start(out=scbuf[:, 0:2048],
                                 in_=dr["sd0"][e][:, hh * 2048:(hh + 1) * 2048]),
                "ld_sc", 16)
            inc(ctx, g.dma_start(out=scbuf[:, 2048:4096],
                                 in_=dr["sd1"][e][:, hh * 2048:(hh + 1) * 2048]),
                "ld_sc", 16)
            if hh == 0:
                ctx.t[("pkz", e)] = inc(ctx, g.memset(pk1_16[64:128, 4095:4096], 0), "gp")
            ctx.t[("pscl", e, grp)] = ctx.cnt["ld_sc"]
        for i in range(64, PROD_NPASS):
            stores[i](g)
    blk.gpsimd(f)


def emit_crep(ctx, blk, e):
    sb, ps = ctx.sb, ctx.ps
    ar2 = sb["AR2"][:]
    crep = ar2[:, 24576:28672].bitcast(FP16)
    esel = sb["rotA"][:]
    combT = sb["rotB"][:]
    preset(ctx, "pe", [("bct", e, q) for q in range(4)])
    preset(ctx, "act", [("bcev4", e, q) for q in range(4)])

    cps = [ps["pE"][:, 0:512], ps["pF"][:, 0:512],
           ps["pT1"][:, 0:1024].bitcast(F32), ps["pT2"][:, 0:1024].bitcast(F32)]

    def f(te, e=e):
        if e == 0:
            wait(ctx, te, "dve", ctx.t["combT_done"])
            wait(ctx, te, "ld_c1", ctx.t["c_ldg"])
        for q in range(4):
            gi = e * 4 + q
            if gi >= 4:
                pe_, pq_ = divmod(gi - 4, 4)
                wait(ctx, te, "act", ctx.t[("bcev4", pe_, pq_)])
            inc_k(ctx, te.matmul(cps[q],
                                 esel[0:16, e * 128:(e + 1) * 128],
                                 combT[0:16, q * 512:(q + 1) * 512],
                                 start=True, stop=True),
                  "pe", ("bct", e, q))
    blk.tensor(f)

    def f(sc, e=e):
        if e > 0:
            wait(ctx, sc, "dve", ctx.t[("gu_ev", e - 1, 1, 15)])
        for q in range(4):
            wait(ctx, sc, "pe", ctx.t[("bct", e, q)])
            inc_k(ctx, sc.activation(crep[:, q * 512:(q + 1) * 512],
                                     cps[q], AF.Copy),
                  "act", ("bcev4", e, q))
    blk.scalar(f)
    ctx.t[("bcev", e, 15)] = ctx.t[("bcev4", e, 3)]
    ctx.t[("crep_mm_last",)] = ctx.t[("bct", e, 3)]


def emit_boot_pre(ctx, blk):
    ctx.boot_passes = []
    for e in (0, 1):
        passes, stores = producer_passes(ctx, e)
        emit_producer_gp(ctx, blk, e, stores)
        ctx.boot_passes += passes
    assert len(ctx.boot_passes) == 160


def emit_boot_mid(ctx, blk):
    pass


def emit_boot(ctx, blk):
    sb, ps = ctx.sb, ctx.ps
    combT = sb["rotB"][:]
    preset(ctx, "pe", [("cbt", tt) for tt in range(16)])
    preset(ctx, "dve", [("cbte", tt) for tt in range(16)])

    def f(te):
        wait(ctx, te, "dve", ctx.t["comb_done"])
        for tt in range(16):
            if tt >= 2:
                wait(ctx, te, "dve", ctx.t[("cbte", tt - 2)])
            pT = ps["pE"] if tt % 2 == 0 else ps["pF"]
            inc_k(ctx, te.transpose(pT[0:16, 0:128],
                                    sb["combp"][:, tt * NE:(tt + 1) * NE],
                                    sb["identf"][:]),
                  "pe", ("cbt", tt))
    blk.tensor(f)

    def f(v):
        for tt in range(16):
            wait(ctx, v, "pe", ctx.t[("cbt", tt)])
            pT = ps["pE"] if tt % 2 == 0 else ps["pF"]
            inc_k(ctx, v.tensor_copy(out=combT[0:16, tt * 128:(tt + 1) * 128],
                                     in_=pT[0:16, 0:128]),
                  "dve", ("cbte", tt))
    blk.vector(f)
    ctx.t["combT_done"] = ctx.cnt["dve"]
    emit_crep(ctx, blk, 0)


def emit_C(ctx, blk):
    dr, sb, ps = ctx.dr, ctx.sb, ctx.ps
    ar2 = sb["AR2"][:]
    crep = ar2[:, 24576:28672].bitcast(FP16)
    actst = ar2[:, 28672:32768].bitcast(BF16)
    bcb = [ar2[:, 40960:41216].bitcast(BF16), ar2[:, 41216:41472].bitcast(BF16)]
    dec = sb["AR3"][:].bitcast(BF16)

    def dchunk(wt, q2, gu, tail):
        idx = (wt * 4 + q2) * 2 + gu
        return dec[:, idx * 1024:(idx + 1) * 1024] if tail else \
            dec[:, idx * 2048:(idx + 1) * 2048]

    GU_PS = [(ps["pA"], ps["pB"]), (ps["pC"], ps["pD"])]
    ctx.t[("gu_ev_par", 0)] = 0
    ctx.t[("gu_ev_par", 1)] = 0
    ctx.t[("actst_par", 0)] = 0

    for e in range(NEA):
        if e + 2 < NEA:
            prod_passes, prod_stores = producer_passes(ctx, e + 2)
            emit_producer_gp(ctx, blk, e + 2, prod_stores)
        else:
            prod_passes, prod_stores = [], []
        n_main_p = (len(prod_passes) * 2) // 3

        def f(s, e=e):
            if e == 0:
                wait(ctx, s, "st_dec0", ctx.t[("sdec", 0, 30)])
                wait(ctx, s, "st_dec1", ctx.t[("sdec", 0, 31)])
                wait(ctx, s, "dve", ctx.t[("xnw", 15)])
                wait(ctx, s, "pe", ctx.t[("tr", 15, NPL - 1)])
                wait(ctx, s, "st_wos", ctx.t[("wost", NPL - 1)])
                wait(ctx, s, "act", ctx.t[("sq", 15)])
                inc(ctx, s.dma_start(out=dec[:, 0:32768],
                                     in_=dr["dgu_scr"][0].rearrange("p c n -> p (c n)")),
                    "ld_dm", 16)
                ctx.t[("ldm", 0)] = ctx.cnt["ld_dm"]
        blk.sync(f)

        if 0 < e < NEA - 1:
            emit_crep(ctx, blk, e)

        pass_cursor = [0]

        for cg in range(2):
            tail = cg == 1
            pts = list(range(4 if tail else 8))
            nq = len(pts) * 4

            preset(ctx, "dve", [("gu_ev", e, cg, q) for q in range(nq)])

            def f(te, e=e, cg=cg, tail=tail, pts=pts):
                wait(ctx, te, "ld_dm" if not tail else "ld_dt",
                     ctx.t[("ldm", e) if not tail else ("ldt", e)])
                if e == 0 and cg == 0:
                    wait(ctx, te, "act", ctx.t[("ex", 15)])

                for pi, pt in enumerate(pts):
                    for tc in range(4):
                        q = pi * 4 + tc
                        par = q % 2
                        psG, psU = GU_PS[par]
                        if q >= 2:
                            wait(ctx, te, "dve", ctx.t[("gu_ev", e, cg, q - 2)])
                        else:
                            wait(ctx, te, "dve", ctx.t[("gu_ev_par", par)])
                        for gu, pp in ((0, psG), (1, psU)):
                            kk = 0
                            for w in range(2):
                                for hlfH in range(2):
                                    for q2 in range(4):
                                        j = 4 * hlfH + q2
                                        ti = w * 8 + j
                                        ch = dchunk(w, q2, gu, tail)
                                        if not tail:
                                            lhsT = ch[:, 2 * pt + hlfH:
                                                      2 * pt + hlfH + 16 * 127 + 1:16]
                                        else:
                                            lhsT = ch[:, 2 * pt + hlfH:
                                                      2 * pt + hlfH + 8 * 127 + 1:8]
                                        mm = te.matmul(pp[:, 0:512], lhsT,
                                                       ctx.xnT[:, ti * T + tc * 512:
                                                               ti * T + tc * 512 + 512],
                                                       start=(kk == 0), stop=(kk == 15))
                                        kk += 1
                            ctx.t[("gu_mm", e, cg, q, gu)] = inc(ctx, mm, "pe")
                ctx.t[("mmm_last" if not tail else "mmt_last", e)] = ctx.cnt["pe"]
            blk.tensor(f)

            stbase = ctx.cnt["st_act"]
            dvb = ctx.cnt["dve"]
            prev_actst_par = ctx.t[("actst_par", 0)]
            for i_, pt in enumerate(pts):
                ctx.t[("actst", e, cg, pt)] = stbase + (i_ + 1) * 16
                ctx.t[("actw_pred", e, cg, pt)] = dvb + (i_ + 1) * 4

            def f(sc, e=e, cg=cg, tail=tail, pts=pts):
                for pi, pt in enumerate(pts):
                    for tc in range(4):
                        q = pi * 4 + tc
                        par = q % 2
                        psG, _ = GU_PS[par]
                        wait(ctx, sc, "pe", ctx.t[("gu_mm", e, cg, q, 0)])
                        sgb = sb["sg0"] if par == 0 else sb["sg1"]
                        ctx.t[("sg", e, cg, q)] = inc(
                            ctx, sc.activation(sgb[:], psG[:, 0:512], AF.Silu), "act")
                    pidx = (8 + pt) if tail else pt
                    wait(ctx, sc, "dve", ctx.t[("actw_pred", e, cg, pt)])
                    v = inc(ctx, sc.dma_start(out=dr["act_scr"][e, pidx], in_=actst[:]),
                            "st_act", 16)
                    assert v == ctx.t[("actst", e, cg, pt)]
                    ctx.t[("actst_par", 0)] = v
            blk.scalar(f)

            if not tail:
                def f(s, e=e):
                    wait(ctx, s, "pe", ctx.t[("mmm_last", e)])
                    wait(ctx, s, "st_dec0", ctx.t[("sdec", e, 46)])
                    wait(ctx, s, "st_dec1", ctx.t[("sdec", e, 47)])
                    inc(ctx, s.dma_start(out=dec[:, 0:16384],
                                         in_=dr["dgt_scr"][e].rearrange("p c n -> p (c n)")),
                        "ld_dt", 16)
                    ctx.t[("ldt", e)] = ctx.cnt["ld_dt"]
                    if e + 1 < NEA:
                        wait(ctx, s, "st_dec0", ctx.t[("sdec", e + 1, 30)])
                        wait(ctx, s, "st_dec1", ctx.t[("sdec", e + 1, 31)])
                        inc(ctx, s.dma_start(
                            out=dec[:, 16384:32768],
                            in_=dr["dgu_scr"][e + 1, :, 8:16, :].rearrange(
                                "p c n -> p (c n)")), "ld_dm", 16)
                blk.sync(f)

            def f(v, e=e, cg=cg, tail=tail, pts=pts, prev_actst_par=prev_actst_par,
                  prod_passes=prod_passes, pass_cursor=pass_cursor, n_main_p=n_main_p,
                  nq=nq):
                ntot = len(prod_passes)
                tgt_end = n_main_p if not tail else ntot
                tgt_beg = 0 if not tail else n_main_p
                if e < NEA - 1:
                    wait(ctx, v, "act", ctx.t[("bcev", e, 15)])
                for pi, pt in enumerate(pts):
                    for tc in range(4):
                        q = pi * 4 + tc
                        par = q % 2
                        psG, psU = GU_PS[par]
                        wait(ctx, v, "pe", ctx.t[("gu_mm", e, cg, q, 1)])
                        wait(ctx, v, "act", ctx.t[("sg", e, cg, q)])
                        if tc == 0:
                            if pt >= 1:
                                wait(ctx, v, "st_act", ctx.t[("actst", e, cg, pt - 1)])
                            else:
                                wait(ctx, v, "st_act", prev_actst_par)
                        sgb = sb["sg0"] if par == 0 else sb["sg1"]
                        ucb = sb["uc0"] if par == 0 else sb["uc1"]
                        if e < NEA - 1:
                            v.tensor_tensor(ucb[:], psU[:, 0:512],
                                            crep[:, tc * 512:(tc + 1) * 512], Op.mult)
                        else:
                            v.tensor_copy(out=ucb[:], in_=psU[:, 0:512])
                        inc_k(ctx, v.tensor_tensor(actst[:, tc * 512:(tc + 1) * 512],
                                                   sgb[:], ucb[:], Op.mult),
                              "dve", ("gu_ev", e, cg, q))
                        want = tgt_beg + ((q + 1) * (tgt_end - tgt_beg)) // nq
                        while pass_cursor[0] < want:
                            prod_passes[pass_cursor[0]](v)
                            pass_cursor[0] += 1
                    ctx.t[("actw", e, cg, pt)] = ctx.cnt["dve"]
                    assert ctx.t[("actw", e, cg, pt)] == ctx.t[("actw_pred", e, cg, pt)]
                ctx.t[("gu_ev_par", 0)] = ctx.t[("gu_ev", e, cg, nq - 2)]
                ctx.t[("gu_ev_par", 1)] = ctx.t[("gu_ev", e, cg, nq - 1)]
            blk.vector(f)

            if tail:
                def f(s, e=e):
                    if e + 1 < NEA:
                        wait(ctx, s, "pe", ctx.t[("mmt_last", e)])
                        inc(ctx, s.dma_start(
                            out=dec[:, 0:16384],
                            in_=dr["dgu_scr"][e + 1, :, 0:8, :].rearrange(
                                "p c n -> p (c n)")), "ld_dm", 16)
                        ctx.t[("ldm", e + 1)] = ctx.cnt["ld_dm"]
                blk.sync(f)

    ctx.t["C_done_pe"] = ctx.cnt["pe"]
    ctx.t["C_done_sta"] = ctx.cnt["st_act"]
    ctx.t["C_done_dve"] = ctx.cnt["dve"]


def emit_C2(ctx, blk):
    dr, sb, ps = ctx.dr, ctx.sb, ctx.ps
    r_C = ctx.r_C
    ar1, ar2, ar3 = sb["AR1"][:], sb["AR2"][:], sb["AR3"][:]
    attnT = ar2[:, 0:32768].bitcast(BF16)
    wxh = ar3[:, 0:12288].bitcast(BF16)
    qTh = ar3[:, 12288:14336].bitcast(BF16)
    kTh = ar3[:, 14336:16384].bitcast(BF16)
    vTh = ar3[:, 16384:18432].bitcast(BF16)
    vnh = ar3[:, 18432:20480].bitcast(BF16)
    knh = ar3[:, 20480:22528].bitcast(BF16)
    kvpp = [ar3[:, 22528:30720].bitcast(F32), ar3[:, 30720:38912].bitcast(F32)]
    kvb = ar3[:, 38912:43008].bitcast(BF16)
    sqb = [ar3[:, 43008:45056].bitcast(BF16), ar3[:, 45056:47104].bitcast(BF16)]
    stile = [ar3[:, 47104:51200].bitcast(BF16), ar3[:, 51200:55296].bitcast(BF16)]
    rrow = ar3[0:1, 55296:59392].bitcast(F32)
    rrow2 = ar3[0:1, 59392:63488].bitcast(F32)
    qdF = vTh
    wosR = ar1.bitcast(BF16)

    def f(g):
        wait(ctx, g, "pe", ctx.t["C_done_pe"])
        g.memset(kvpp[0][:], 0.0)
        ctx.t["kv0"] = inc(ctx, g.memset(kvb[:], 0.0), "gp")
    blk.gpsimd(f)

    def f(g):
        wait(ctx, g, "pe", ctx.t[("crep_mm_last",)])
        inc(ctx, g.dma_start(out=sb["rotA"][:], in_=dr["rotA"]), "ld_rot", 16)
        inc(ctx, g.dma_start(out=sb["rotB"][:], in_=dr["rotB"]), "ld_rot", 16)
        ctx.t["rot_ld"] = ctx.cnt["ld_rot"]
    blk.gpsimd(f)

    QKV_PS = [(ps["pA"], ps["pB"]), (ps["pC"], ps["pD"])]
    PO = [ps["pE"], ps["pA"]]
    PSC = [ps["pF"], ps["pD"]]
    PKX = [ps["pC"], ps["pB"]]
    ctx.t[("qkv_prev_ev", 0)] = 0
    ctx.t[("qkv_prev_ev", 1)] = 0

    for tau in range(2):
        for h in range(NH):
            hi = tau * NH + h

            def f(g, tau=tau, h=h, hi=hi):
                if hi > 0:
                    wait(ctx, g, "pe", ctx.t[("wx_cons", hi - 1)])
                else:
                    wait(ctx, g, "pe", ctx.t["C_done_pe"])
                for xi, wn in enumerate(["wq", "wk", "wv"]):
                    src = dr[wn].rearrange("(ti p) e -> p ti e", ti=NPL, p=P)[
                        :, :, h * 128:(h + 1) * 128]
                    inc(ctx, g.dma_start(
                        out=wxh[:, xi * 2048:(xi + 1) * 2048].rearrange(
                            "p (ti e) -> p ti e", ti=NPL),
                        in_=src), "ld_wx", 16)
                ctx.t[("wxl", hi)] = ctx.cnt["ld_wx"]
            blk.gpsimd(f)

            for xi, dstb in ((0, qTh), (1, kTh), (2, vTh)):
                par = xi % 2
                p0, p1 = QKV_PS[par]

                def f(te, tau=tau, h=h, hi=hi, xi=xi, p0=p0, p1=p1, par=par):
                    if xi == 0:
                        wait(ctx, te, "ld_wx", ctx.t[("wxl", hi)])
                        if hi == 0:
                            wait(ctx, te, "dve", ctx.t["C_done_dve"])
                            wait(ctx, te, "gp", ctx.t["kv0"])
                        if hi > 0:
                            wait(ctx, te, "dve", ctx.t[("kvd", hi - 1, 7)])
                            wait(ctx, te, "act", ctx.t[("aev", hi - 1, 7)])
                            wait(ctx, te, "dve", ctx.t[("sm", hi - 1, 7)])
                    if xi == 2:
                        wait(ctx, te, "dve", ctx.t[("rot", hi, 0)])
                    else:
                        wait(ctx, te, "dve", ctx.t[("qkv_prev_ev", par)])
                    for nc_ in range(2):
                        pp = (p0, p1)[nc_]
                        for ti in range(NPL):
                            mm = te.matmul(pp[:, 0:512],
                                           wxh[:, xi * 2048 + ti * 128:
                                               xi * 2048 + (ti + 1) * 128],
                                           ctx.xnT[:, ti * T + tau * 1024 + nc_ * 512:
                                                   ti * T + tau * 1024 + nc_ * 512 + 512],
                                           start=(ti == 0), stop=(ti == NPL - 1))
                    ctx.t[("qkvmm", hi, xi)] = inc(ctx, mm, "pe")
                    if xi == 2:
                        ctx.t[("wx_cons", hi)] = ctx.cnt["pe"]
                blk.tensor(f)

                def f(v, tau=tau, h=h, hi=hi, xi=xi, dstb=dstb, p0=p0, p1=p1, par=par):
                    wait(ctx, v, "pe", ctx.t[("qkvmm", hi, xi)])
                    if hi == 0 and xi == 0:
                        wait(ctx, v, "st_act", ctx.t["C_done_sta"])
                        wait(ctx, v, "dvp", ctx.t[("pdone", NEA - 1)])
                    v.tensor_copy(out=dstb[:, 0:512], in_=p0[:, 0:512])
                    ctx.t[("qkvev", hi, xi)] = inc(
                        ctx, v.tensor_copy(out=dstb[:, 512:1024], in_=p1[:, 0:512]), "dve")
                    ctx.t[("qkv_prev_ev", par)] = ctx.t[("qkvev", hi, xi)]
                blk.vector(f)

                if xi < 2:
                    def f(te, tau=tau, hi=hi, xi=xi, dstb=dstb, p0=p0, p1=p1):
                        wait(ctx, te, "dve", ctx.t[("qkvev", hi, xi)])
                        for nc_ in range(2):
                            pp = (p0, p1)[nc_]
                            mm = te.matmul(pp[:, 0:512], sb["pswap"][:],
                                           dstb[:, nc_ * 512:(nc_ + 1) * 512],
                                           start=True, stop=True)
                        ctx.t[("swmm", hi, xi)] = inc(ctx, mm, "pe")
                    blk.tensor(f)

                    def f(v, tau=tau, hi=hi, xi=xi, dstb=dstb, p0=p0, p1=p1, par=par):
                        wait(ctx, v, "pe", ctx.t[("swmm", hi, xi)])
                        if ("rot_waited",) not in ctx.t:
                            wait(ctx, v, "ld_rot", ctx.t["rot_ld"])
                            ctx.t[("rot_waited",)] = 1
                        t0 = tau * 1024
                        tA = sb["f1"][:].bitcast(BF16)
                        tB = sb["f2"][:].bitcast(BF16)
                        v.tensor_tensor(tA, dstb[:], sb["rotA"][:, t0:t0 + 1024], Op.mult)
                        v.tensor_tensor(tB[:, 0:512], p0[:, 0:512],
                                        sb["rotB"][:, t0:t0 + 512], Op.mult)
                        v.tensor_tensor(tB[:, 512:1024], p1[:, 0:512],
                                        sb["rotB"][:, t0 + 512:t0 + 1024], Op.mult)
                        ctx.t[("rot", hi, xi)] = inc(
                            ctx, v.tensor_tensor(dstb[:], tA, tB, Op.add), "dve")
                        ctx.t[("qkv_prev_ev", par)] = ctx.t[("rot", hi, xi)]
                    blk.vector(f)

            preset(ctx, "dve", [("tnev", hi * 16 + c) for c in range(16)])

            def f(te, tau=tau, h=h, hi=hi):
                wait(ctx, te, "dve", ctx.t[("rot", hi, 1)])
                wait(ctx, te, "dve", ctx.t[("qkvev", hi, 2)])
                for c in range(16):
                    gi = hi * 16 + c
                    if gi >= 2:
                        wait(ctx, te, "dve", ctx.t[("tnev", gi - 2)])
                    else:
                        wait(ctx, te, "act", ctx.t[("bcev", NEA - 2, 15)])
                    pT = ps["pT1"] if c % 2 == 0 else ps["pT2"]
                    srcb = kTh if c < 8 else vTh
                    cc = c % 8
                    key = ("tn", hi, cc) if c < 8 else ("tv", hi, cc)
                    ctx.t[key] = inc(
                        ctx, te.transpose(pT[:, 0:128], srcb[:, cc * 128:(cc + 1) * 128],
                                          sb["identb"][:]), "pe")
            blk.tensor(f)

            def f(v, tau=tau, h=h, hi=hi):
                for c in range(8):
                    wait(ctx, v, "pe", ctx.t[("tn", hi, c)])
                    pT = ps["pT1"] if c % 2 == 0 else ps["pT2"]
                    inc_k(ctx, v.tensor_scalar(knh[:, c * 128:(c + 1) * 128], pT[:, 0:128],
                                               sb["kdec"][:, h:h + 1], None, Op.mult),
                          "dve", ("tnev", hi * 16 + c))
                for c in range(8):
                    wait(ctx, v, "pe", ctx.t[("tv", hi, c)])
                    pT = ps["pT1"] if c % 2 == 0 else ps["pT2"]
                    inc_k(ctx, v.tensor_copy(out=vnh[:, c * 128:(c + 1) * 128],
                                             in_=pT[:, 0:128]),
                          "dve", ("tnev", hi * 16 + 8 + c))
                for c in range(8):
                    last = v.tensor_tensor(qdF[:, c * 128:(c + 1) * 128],
                                           qTh[:, c * 128:(c + 1) * 128],
                                           sb["qdec"][:, h * 128:(h + 1) * 128], Op.mult)
                ctx.t[("qdw", hi)] = inc(ctx, last, "dve")
            blk.vector(f)

            for c in range(8):
                cp = c % 2

                def f(te, tau=tau, h=h, hi=hi, c=c, cp=cp):
                    if c == 0:
                        wait(ctx, te, "dve", ctx.t[("qdw", hi)])
                    if c >= 2:
                        wait(ctx, te, "dve", ctx.t[("sm", hi, c - 2)])
                    elif hi > 0:
                        wait(ctx, te, "dve", ctx.t[("sm", hi - 1, c + 6)])
                    ctx.t[("scm", hi, c)] = inc(ctx, te.matmul(
                        PSC[cp][:, 0:128], kTh[:, c * 128:(c + 1) * 128],
                        qTh[:, c * 128:(c + 1) * 128], start=True, stop=True), "pe")
                blk.tensor(f)

                def f(v, tau=tau, h=h, hi=hi, c=c, cp=cp):
                    wait(ctx, v, "pe", ctx.t[("scm", hi, c)])
                    ctx.t[("sm", hi, c)] = inc(ctx, v.tensor_tensor(
                        sb["smh" if cp == 0 else "smh2"][:], PSC[cp][:, 0:128],
                        sb["dmaskT"][:, h * 128:(h + 1) * 128], Op.mult), "dve")
                blk.vector(f)

                def f(te, tau=tau, h=h, hi=hi, c=c, cp=cp):
                    gc = hi * 8 + c
                    if gc > 0:
                        wait(ctx, te, "dve", ctx.t[("kvd_d", gc - 1)])
                    if c >= 2:
                        wait(ctx, te, "act", ctx.t[("aev", hi, c - 2)])
                    elif hi > 0:
                        wait(ctx, te, "act", ctx.t[("aev", hi - 1, c + 6)])
                    te.matmul(PO[cp][:, 0:128], kvb[:, h * 128:(h + 1) * 128],
                              qdF[:, c * 128:(c + 1) * 128], start=True, stop=False)
                    wait(ctx, te, "dve", ctx.t[("sm", hi, c)])
                    ctx.t[("ao", hi, c)] = inc(ctx, te.matmul(
                        PO[cp][:, 0:128], vnh[:, c * 128:(c + 1) * 128],
                        sb["smh" if cp == 0 else "smh2"][:], start=False, stop=True), "pe")
                    ctx.t[("kvx", hi, c)] = inc(ctx, te.matmul(
                        PKX[cp][:, 0:128], knh[:, c * 128:(c + 1) * 128],
                        vnh[:, c * 128:(c + 1) * 128], start=True, stop=True), "pe")
                blk.tensor(f)

                def f(sc, tau=tau, h=h, hi=hi, c=c, cp=cp):
                    wait(ctx, sc, "pe", ctx.t[("ao", hi, c)])
                    if hi == NH and h == 0 and c == 0 and tau == 1:
                        pass
                    if tau == 1 and h == 0 and c == 0:
                        wait(ctx, sc, "st_s0", ctx.t[("atspill",)])
                    ctx.t[("aev", hi, c)] = inc(ctx, sc.copy(
                        out=attnT[:, h * 1024 + c * 128: h * 1024 + (c + 1) * 128],
                        in_=PO[cp][:, 0:128]), "act")
                blk.scalar(f)

                def f(v, tau=tau, h=h, hi=hi, c=c, cp=cp):
                    gc = hi * 8 + c
                    wait(ctx, v, "pe", ctx.t[("kvx", hi, c)])
                    kv_c = kvpp[gc % 2][:, h * 128:(h + 1) * 128]
                    kv_n = kvpp[(gc + 1) % 2][:, h * 128:(h + 1) * 128]
                    v.scalar_tensor_tensor(kv_n, kv_c, r_C[h], PKX[cp][:, 0:128],
                                           Op.mult, Op.add)
                    ctx.t[("kvd", hi, c)] = inc(ctx, v.tensor_copy(
                        out=kvb[:, h * 128:(h + 1) * 128], in_=kv_n), "dve")
                    ctx.t[("kvd_d", gc)] = ctx.t[("kvd", hi, c)]
                blk.vector(f)

        if tau == 0:
            def f(sc, tau=tau):
                for k in range(NPL):
                    wait(ctx, sc, "act", ctx.t[("aev", tau * NH + k, 7)])
                inc(ctx, sc.dma_start(out=dr["at_spill"], in_=attnT[:]), "st_s0", 16)
                ctx.t[("atspill",)] = ctx.cnt["st_s0"]
            blk.scalar(f)

    def f(s):
        wait(ctx, s, "pe", ctx.t[("wx_cons", 2 * NH - 1)])
        wait(ctx, s, "st_wos", ctx.t[("wost", NPL - 1)])
        inc(ctx, s.dma_start(out=wosR.rearrange("p (k n) -> p k n", k=NPL),
                             in_=dr["wos_scr"].rearrange("k p n -> p k n")), "ld_wos0", 16)
        ctx.t["wosl_all"] = ctx.cnt["ld_wos0"]
    blk.sync(f)

    for tau in (1, 0):
        if tau == 0:
            def f(s):
                wait(ctx, s, "pe", ctx.t[("smm", 1, 7)])
                wait(ctx, s, "st_s0", ctx.t[("atspill",)])
                inc(ctx, s.dma_start(out=attnT[:], in_=dr["at_spill"]), "ld_wos0", 16)
                ctx.t["atload"] = ctx.cnt["ld_wos0"]
            blk.sync(f)

        pebase = ctx.cnt["pe"]
        for k in range(NPL):
            ctx.t[("ssq", tau, k)] = pebase + k + 1

        def f(sc, tau=tau):
            for k in range(NPL):
                if tau == 1:
                    wait(ctx, sc, "act", ctx.t[("aev", tau * NH + k, 7)])
                    wait(ctx, sc, "dve", ctx.t[("kvd", tau * NH + k, 7)])
                elif k == 0:
                    wait(ctx, sc, "ld_wos0", ctx.t["atload"])
                if k >= 2:
                    wait(ctx, sc, "pe", ctx.t[("ssq", tau, k - 2)])
                ctx.t[("sqw", tau, k)] = inc(ctx, sc.activation(
                    sqb[k % 2][:], attnT[:, k * 1024:(k + 1) * 1024], AF.Square), "act")
        blk.scalar(f)

        def f(te, tau=tau):
            pv = ctx.t.get(("rstdev", 1 if tau == 0 else -1))
            if pv is not None:
                wait(ctx, te, "dve", pv)
            for k in range(NPL):
                wait(ctx, te, "act", ctx.t[("sqw", tau, k)])
                for nc_ in range(2):
                    pp = (ps["pE"], ps["pF"])[nc_]
                    mm = te.matmul(pp[0:1, 0:512], sb["onescolb"][:],
                                   sqb[k % 2][:, nc_ * 512:(nc_ + 1) * 512],
                                   start=(k == 0), stop=(k == NPL - 1))
                inc_k(ctx, mm, "pe", ("ssq", tau, k))
        blk.tensor(f)

        def f(v, tau=tau):
            wait(ctx, v, "pe", ctx.t[("ssq", tau, NPL - 1)])
            dvsync(ctx, v, v.tensor_scalar(rrow[0:1, 0:512], ps["pE"][0:1, 0:512],
                                           1.0 / H, 1e-5, Op.mult, Op.add))
            ctx.t[("vrow", tau)] = inc(ctx, v.tensor_scalar(
                rrow[0:1, 512:1024], ps["pF"][0:1, 0:512], 1.0 / H, 1e-5,
                Op.mult, Op.add), "dve")
        blk.vector(f)

        def f(sc, tau=tau):
            wait(ctx, sc, "dve", ctx.t[("vrow", tau)])
            ctx.t[("srow", tau)] = inc(ctx, sc.activation(
                rrow2[0:1, :], rrow[0:1, :], AF.Sqrt), "act")
        blk.scalar(f)

        def f(v, tau=tau):
            wait(ctx, v, "act", ctx.t[("srow", tau)])
            ctx.t[("rrec", tau)] = inc(ctx, v.reciprocal(rrow[0:1, :], rrow2[0:1, :]), "dve")
        blk.vector(f)

        def f(te, tau=tau):
            wait(ctx, te, "dve", ctx.t[("rrec", tau)])
            for c in range(8):
                mm = te.matmul(ps["pE"][:, c:c + 1], rrow[0:1, c * 128:(c + 1) * 128],
                               sb["identf"][0:1, 0:1], start=True, stop=True)
            ctx.t[("rtm", tau)] = inc(ctx, mm, "pe")
        blk.tensor(f)

        def f(v, tau=tau):
            wait(ctx, v, "pe", ctx.t[("rtm", tau)])
            ctx.t[("rstdev", tau)] = inc(ctx, v.tensor_copy(
                out=sb["rstdtm"][:, tau * 8:tau * 8 + 8], in_=ps["pE"][:, 0:8]), "dve")
        blk.vector(f)

        for tc in range(8):
            def f(te, tau=tau, tc=tc):
                pv = ctx.t.get(("sev", tau, tc - 1))
                if pv is None:
                    pv = ctx.t.get(("sev", 1, 7)) if tau == 0 else None
                if pv is not None:
                    wait(ctx, te, "dve", pv)
                wait(ctx, te, "dve", ctx.t[("rstdev", tau)])
                if tau == 1 and tc == 0:
                    wait(ctx, te, "ld_wos0", ctx.t["wosl_all"])
                for k in range(NPL):
                    for nc_ in range(4):
                        pp = (ps["pA"], ps["pB"], ps["pC"], ps["pD"])[nc_]
                        mm = te.matmul(pp[:, 0:512],
                                       attnT[:, k * 1024 + tc * 128: k * 1024 + (tc + 1) * 128],
                                       wosR[:, k * 2048 + nc_ * 512:
                                            k * 2048 + (nc_ + 1) * 512],
                                       start=(k == 0), stop=(k == NPL - 1))
                ctx.t[("smm", tau, tc)] = inc(ctx, mm, "pe")
            blk.tensor(f)

            def f(v, tau=tau, tc=tc):
                wait(ctx, v, "pe", ctx.t[("smm", tau, tc)])
                stk = ("sst", tau, tc - 2)
                if stk in ctx.t:
                    wait(ctx, v, "st_s0" if tc % 2 == 0 else "st_s1", ctx.t[stk])
                sp = stile[tc % 2]
                for nc_ in range(4):
                    pp = (ps["pA"], ps["pB"], ps["pC"], ps["pD"])[nc_]
                    ts_ = v.tensor_scalar(sp[:, nc_ * 512:(nc_ + 1) * 512], pp[:, 0:512],
                                          sb["rstdtm"][:, tau * 8 + tc:tau * 8 + tc + 1],
                                          None, Op.mult)
                ctx.t[("sev", tau, tc)] = inc(ctx, ts_, "dve")
            blk.vector(f)

            def f(sc, tau=tau, tc=tc):
                wait(ctx, sc, "dve", ctx.t[("sev", tau, tc)])
                r0 = tau * 1024 + tc * 128
                sem = "st_s0" if tc % 2 == 0 else "st_s1"
                inc(ctx, sc.dma_start(out=dr["s_scr"][r0:r0 + 128, :],
                                      in_=stile[tc % 2][:]), sem, 16)
                ctx.t[("sst", tau, tc)] = ctx.cnt[sem]
            blk.scalar(f)

    ctx.t["C2_done_pe"] = ctx.cnt["pe"]
    ctx.t["C2_done_dve"] = ctx.cnt["dve"]
    ctx.t["C2_done_sts"] = (ctx.cnt["st_s0"], ctx.cnt["st_s1"])


def emit_DE(ctx, blk):
    dr, sb, ps = ctx.dr, ctx.sb, ctx.ps
    ar1, ar2, ar3 = sb["AR1"][:], sb["AR2"][:], sb["AR3"][:]
    Y = ar1.bitcast(F32)
    actin = ar2[:, 0:24576].bitcast(BF16)
    decs = [ar3[:, 0:32768].bitcast(BF16), ar3[:, 32768:65536].bitcast(BF16)]
    g_ = ar3[:, 0:32768].bitcast(BF16)
    xrel = [ar2[:, 0:8192].bitcast(F32), ar2[:, 8192:16384].bitcast(F32)]
    srel = [ar2[:, 16384:20480].bitcast(BF16), ar2[:, 20480:24576].bitcast(BF16)]
    outb = [ar2[:, 24576:32768].bitcast(F32), ar2[:, 32768:40960].bitcast(F32)]

    def dch(par_, wt2, q2):
        return decs[par_][:, (wt2 * 4 + q2) * 2048:(wt2 * 4 + q2 + 1) * 2048]

    PS4 = [ps["pA"], ps["pB"], ps["pC"], ps["pD"]]
    for k in range(4):
        ctx.t[("yadd_psum", k)] = 0
    ctx.t[("eev_par", 0)] = 0
    ctx.t[("eev_par", 1)] = 0

    for tau in range(2):
        for e in range(NEA):
            ei = tau * NEA + e

            def f(s, tau=tau, e=e, ei=ei):
                k2 = ("dmm_q_done", ei - 1, 0)
                if k2 in ctx.t:
                    wait(ctx, s, "pe", ctx.t[k2])
                elif tau == 0:
                    wait(ctx, s, "pe", ctx.t["C2_done_pe"])
                if tau == 1 and e == 0:
                    wait(ctx, s, "dve", ctx.t[("ee", 0, 7)])
                    wait(ctx, s, "sto0", ctx.t[("outst", 0, 6)])
                    wait(ctx, s, "sto1", ctx.t[("outst", 0, 7)])
                inc(ctx, s.dma_start(
                    out=actin[:, 0:6144].rearrange("p (pl t) -> p pl t", pl=12),
                    in_=dr["act_scr"][e].rearrange("pl p t -> p pl t")[
                        :, :, tau * 1024:tau * 1024 + 512]),
                    "ld_actin", 16)
                ctx.t[("actl", ei, 0)] = ctx.cnt["ld_actin"]
            blk.sync(f)

            for hh in range(2):
                gidx = ei * 2 + hh
                par_ = gidx % 2

                def f(s, tau=tau, e=e, ei=ei, hh=hh, gidx=gidx, par_=par_):
                    if gidx < 2:
                        wait(ctx, s, "pe", ctx.t["C2_done_pe"])
                        wait(ctx, s, "st_s0", ctx.t["C2_done_sts"][0])
                        wait(ctx, s, "st_s1", ctx.t["C2_done_sts"][1])
                        wait(ctx, s, "st_dec0", ctx.t[("sdec", NEA - 1, PROD_NPASS - 2)])
                        wait(ctx, s, "st_dec1", ctx.t[("sdec", NEA - 1, PROD_NPASS - 1)])
                    else:
                        wait(ctx, s, "pe", ctx.t[("dchunk_free", gidx - 2)])
                    if tau == 1 and e == 0 and hh == 0:
                        wait(ctx, s, "pe", ctx.t[("etr", 0, 7, 15)])
                    inc(ctx, s.dma_start(
                        out=decs[par_][:].rearrange("p (c n) -> p c n", c=8),
                        in_=dr["ddn_scr"][e].rearrange("c p n -> p c n")[
                            :, :, hh * 2048:(hh + 1) * 2048]),
                        "ld_dn", 16)
                    ctx.t[("dnl", gidx)] = ctx.cnt["ld_dn"]
                blk.sync(f)

            def f(s, ei=ei):
                k2 = ("dmm_q_done", ei - 1, 1)
                if k2 in ctx.t:
                    wait(ctx, s, "pe", ctx.t[k2])
                inc(ctx, s.dma_start(
                    out=actin[:, 6144:12288].rearrange("p (pl t) -> p pl t", pl=12),
                    in_=dr["act_scr"][ei % NEA].rearrange("pl p t -> p pl t")[
                        :, :, (ei // NEA) * 1024 + 512:(ei // NEA) * 1024 + 1024]),
                    "ld_actin", 16)
                ctx.t[("actl", ei, 1)] = ctx.cnt["ld_actin"]
            blk.sync(f)

            preset(ctx, "dve", [("yadd", ei, q, hb)
                                for q in range(2) for hb in range(16)])

            def f(te, tau=tau, e=e, ei=ei):
                for q in range(2):
                    wait(ctx, te, "ld_actin", ctx.t[("actl", ei, q)])
                    ah = actin[:, q * 6144:(q + 1) * 6144]
                    if ei == 0 and q == 0:
                        wait(ctx, te, "dve", ctx.t[("sev", 1, 7)])
                    for hh in range(2):
                        gidx = ei * 2 + hh
                        par_ = gidx % 2
                        if q == 0:
                            wait(ctx, te, "ld_dn", ctx.t[("dnl", gidx)])
                        for hbp in range(8):
                            hb = hh * 8 + hbp
                            pk = (ei * 32 + q * 16 + hb) % 4
                            pp = PS4[pk]
                            wait(ctx, te, "dve", ctx.t[("yadd_psum", pk)])
                            kk = 0
                            for wt2 in range(2):
                                for q2 in range(4):
                                    for hlfI in ((0, 1) if wt2 == 0 else (0,)):
                                        pl = (4 * hlfI + q2) if wt2 == 0 else (8 + q2)
                                        ch = dch(par_, wt2, q2)
                                        lhsT = ch[:, 256 * hbp + hlfI:
                                                  256 * hbp + hlfI + 255:2]
                                        mm = te.matmul(pp[:, 0:512], lhsT,
                                                       ah[:, pl * 512:(pl + 1) * 512],
                                                       start=(kk == 0), stop=(kk == 11))
                                        kk += 1
                            ctx.t[("dmm", ei, q, hb)] = inc(ctx, mm, "pe")
                            ctx.t[("yadd_psum", pk)] = ctx.t[("yadd", ei, q, hb)]
                        if q == 1:
                            ctx.t[("dchunk_free", gidx)] = ctx.cnt["pe"]
                    ctx.t[("dmm_q_done", ei, q)] = ctx.cnt["pe"]
                ctx.t[("dmm_last", ei)] = ctx.cnt["pe"]
            blk.tensor(f)

            def f(v, tau=tau, e=e, ei=ei):
                for q in range(2):
                    for hh in range(2):
                        for hbp in range(8):
                            hb = hh * 8 + hbp
                            pk = (ei * 32 + q * 16 + hb) % 4
                            pp = PS4[pk]
                            wait(ctx, v, "pe", ctx.t[("dmm", ei, q, hb)])
                            if tau == 1 and e == 0:
                                wait(ctx, v, "act", ctx.t[("gw", 0, hb)])
                            ysl = Y[:, hb * 1024 + q * 512:
                                    hb * 1024 + q * 512 + 512]
                            if e == 0:
                                last = v.tensor_copy(out=ysl, in_=pp[:, 0:512])
                            else:
                                last = v.tensor_tensor(ysl, ysl, pp[:, 0:512], Op.add)
                            inc_k(ctx, last, "dve", ("yadd", ei, q, hb))
            blk.vector(f)

        ei_last = tau * NEA + NEA - 1

        def f(sc, tau=tau, ei_last=ei_last):
            for hb in range(16):
                wait(ctx, sc, "dve", ctx.t[("yadd", ei_last, 1, hb)])
                if hb == 0:
                    wait(ctx, sc, "pe", ctx.t[("dmm_last", ei_last)])
                ctx.t[("gw", tau, hb)] = inc(ctx, sc.activation(
                    g_[:, hb * 1024:(hb + 1) * 1024],
                    Y[:, hb * 1024:(hb + 1) * 1024], AF.Silu), "act")
        blk.scalar(f)

        if DBG:
            def f(g, tau=tau, ei_last=ei_last):
                for hb in range(16):
                    wait(ctx, g, "dve", ctx.t[("yadd", ei_last, 1, hb)])
                inc(ctx, g.dma_start(out=ctx.dr["Y_dbg"][tau], in_=Y[:]), "st_dbg", 16)
            blk.gpsimd(f)

        for tc in range(8):
            def f(s, tau=tau, tc=tc, ei_last=ei_last):
                pk = ("outst", tau, tc - 2)
                if pk in ctx.t:
                    wait(ctx, s, "sto0" if tc % 2 == 0 else "sto1", ctx.t[pk])
                pk = ("ee", tau, tc - 2)
                if pk in ctx.t:
                    wait(ctx, s, "dve", ctx.t[pk])
                if tc < 2:
                    wait(ctx, s, "pe", ctx.t[("dmm_last", ei_last)])
                r0 = tau * 1024 + tc * 128
                sem = "ld_ex0" if tc % 2 == 0 else "ld_ex1"
                inc(ctx, s.dma_start(out=xrel[tc % 2], in_=dr["x"][r0:r0 + 128, :]), sem, 16)
                inc(ctx, s.dma_start(out=srel[tc % 2], in_=dr["s_scr"][r0:r0 + 128, :]),
                    sem, 16)
                ctx.t[("exl", tau, tc)] = ctx.cnt[sem]
            blk.sync(f)

            preset(ctx, "dve", [("eev", tau, tc, hb) for hb in range(16)])

            def f(te, tau=tau, tc=tc):
                for hb in range(16):
                    wait(ctx, te, "act", ctx.t[("gw", tau, hb)])
                    if hb >= 2:
                        wait(ctx, te, "dve", ctx.t[("eev", tau, tc, hb - 2)])
                    else:
                        wait(ctx, te, "dve", ctx.t[("eev_par", hb % 2)])
                    pT = ps["pT1"] if hb % 2 == 0 else ps["pT2"]
                    ctx.t[("etr", tau, tc, hb)] = inc(
                        ctx, te.transpose(pT[:, 0:128],
                                          g_[:, hb * 1024 + tc * 128:
                                             hb * 1024 + (tc + 1) * 128],
                                          sb["identb"][:]), "pe")
            blk.tensor(f)

            def f(v, tau=tau, tc=tc):
                wait(ctx, v, "ld_ex0" if tc % 2 == 0 else "ld_ex1", ctx.t[("exl", tau, tc)])
                for hb in range(16):
                    wait(ctx, v, "pe", ctx.t[("etr", tau, tc, hb)])
                    pT = ps["pT1"] if hb % 2 == 0 else ps["pT2"]
                    v.tensor_tensor(sb["etmp"][:], pT[:, 0:128],
                                    srel[tc % 2][:, hb * 128:(hb + 1) * 128], Op.mult)
                    inc_k(ctx, v.tensor_tensor(outb[tc % 2][:, hb * 128:(hb + 1) * 128],
                                               sb["etmp"][:],
                                               xrel[tc % 2][:, hb * 128:(hb + 1) * 128],
                                               Op.add),
                          "dve", ("eev", tau, tc, hb))
                    ctx.t[("eev_par", hb % 2)] = ctx.t[("eev", tau, tc, hb)]
                ctx.t[("ee", tau, tc)] = ctx.cnt["dve"]
            blk.vector(f)

            def f(sc, tau=tau, tc=tc):
                wait(ctx, sc, "dve", ctx.t[("ee", tau, tc)])
                r0 = tau * 1024 + tc * 128
                sem = "sto0" if tc % 2 == 0 else "sto1"
                inc(ctx, sc.dma_start(out=dr["out"][r0:r0 + 128, :],
                                      in_=outb[tc % 2][:]), sem, 16)
                ctx.t[("outst", tau, tc)] = ctx.cnt[sem]
            blk.scalar(f)


def emit_end(ctx, blk):
    def f(g):
        g.wait_ge(ctx.sem["sto0"], ctx.cnt["sto0"])
        g.wait_ge(ctx.sem["sto1"], ctx.cnt["sto1"])
        if ctx.cnt["st_dbg"]:
            g.wait_ge(ctx.sem["st_dbg"], ctx.cnt["st_dbg"])
    blk.gpsimd(f)


_NC = None


def kernel(**inputs):
    global _NC, _CTX_RC
    cons = host_consts()
    prep = prep_inputs(inputs)
    if _NC is None:
        _CTX_RC = cons["_r_C"]
        _ctx, _NC = build_nc()
    x = np.ascontiguousarray(np.asarray(inputs["L0_attn_x"], dtype=np.float32))
    base = {}
    base.update(prep)
    base.update({k: v for k, v in cons.items() if not k.startswith("_")})
    in_maps = [dict(base, x=x[b]) for b in range(8)]
    res = run_bass_kernel_spmd(_NC, in_maps, core_ids=list(range(8)))
    return np.stack([res.results[b]["out"] for b in range(8)], axis=0)


# version 16 (final)
# speedup vs baseline: 1.0009x; 1.0009x over previous
import sys
if '/opt/trn_rl_repo' not in sys.path:
    sys.path.insert(0, '/opt/trn_rl_repo')
import math
import contextlib
import numpy as np
import concourse.bass as bass
import concourse.mybir as mybir
from concourse.bass_utils import run_bass_kernel_spmd
from concourse.alu_op_type import AluOpType as Op

AF = mybir.ActivationFunctionType
AX = mybir.AxisListType
F32 = mybir.dt.float32
BF16 = mybir.dt.bfloat16
FP16 = mybir.dt.float16
I32 = mybir.dt.int32
U16 = mybir.dt.uint16
U8 = mybir.dt.uint8

P = 128
T = 2048
H = 2048
NH = 16
NE = 16
NEA = 17
NPL = 16

DBG = False
_CTX_RC = None


def _slopes(n):
    start = 2.0 ** (-2.0 ** (-(math.log2(n) - 3)))
    return np.array([start * start ** i for i in range(n)], dtype=np.float64)


def host_consts():
    c = {}
    c["identb"] = np.eye(P, dtype=np.float32)
    c["onesfull"] = np.ones((P, P), dtype=np.float32)
    freq = (1.0 / 1024.0) ** np.linspace(0.0, 1.0, 32, dtype=np.float64)
    th = np.arange(T, dtype=np.float64)[:, None] * freq[None, :]
    cosv, sinv = np.cos(th).T, np.sin(th).T
    rotA = np.ones((P, T), dtype=np.float64)
    rotB = np.zeros((P, T), dtype=np.float64)
    rotA[0:32] = cosv
    rotB[0:32] = sinv
    rotA[64:96] = cosv
    rotB[64:96] = -sinv
    c["rotA"] = rotA.astype(np.float32)
    c["rotB"] = rotB.astype(np.float32)
    psw = np.zeros((P, P), dtype=np.float32)
    for j in range(P):
        psw[(j + 64) % P, j] = 1.0
    c["pswap"] = psw
    sl = _slopes(NH)
    r = np.exp(-sl)
    i = np.arange(128, dtype=np.float64)
    c["_r_C"] = [float(v) for v in (r ** 128.0).astype(np.float32)]
    qd = (r[:, None] ** (i[None, :] + 1.0))
    c["qdec"] = np.broadcast_to(qd[:, None, :], (NH, P, 128)).transpose(1, 0, 2) \
        .reshape(P, NH * 128).astype(np.float32).copy()
    c["kdec"] = (r[None, :] ** (127.0 - i[:, None])).astype(np.float32)
    esel = np.zeros((P, NE * 128), dtype=np.float32)
    for e_ in range(NE):
        esel[e_, e_ * 128:(e_ + 1) * 128] = 1.0
    c["esel"] = esel
    diff = i[None, :] - i[:, None]
    dm = np.where(diff >= 0, r[:, None, None] ** np.maximum(diff, 0.0)[None], 0.0)
    c["dmaskT"] = dm.transpose(1, 0, 2).reshape(P, NH * 128).astype(np.float32).copy()
    return c


def prep_inputs(ins):
    d = {}
    w_, j_, p_ = np.meshgrid(np.arange(2), np.arange(8), np.arange(128), indexing="ij")
    perm = (1024 * w_ + 8 * p_ + j_).reshape(-1)
    f32c = lambda a: np.ascontiguousarray(np.asarray(a, dtype=np.float32))
    d["wq"] = np.ascontiguousarray(f32c(ins["L0_attn_Wq"])[perm, :])
    d["wk"] = np.ascontiguousarray(f32c(ins["L0_attn_Wk"])[perm, :])
    d["wv"] = np.ascontiguousarray(f32c(ins["L0_attn_Wv"])[perm, :])
    d["wo"] = f32c(ins["L0_attn_Wo"])
    d["rmsw"] = np.ascontiguousarray(f32c(ins["L0_attn_rms_w"]).reshape(NPL, P).T)
    d["gwT"] = np.ascontiguousarray(f32c(ins["L0_ffn_gate_w"]).T[perm, :])
    gup = np.concatenate([np.asarray(ins["L0_ffn_gate_up_packed"]),
                          np.asarray(ins["L0_ffn_shared_gate_up_packed"])[None]], axis=0)
    dnp = np.concatenate([np.asarray(ins["L0_ffn_down_packed"]),
                          np.asarray(ins["L0_ffn_shared_down_packed"])[None]], axis=0)
    d["gup"] = np.ascontiguousarray(gup.astype(np.int32))
    d["dnp"] = np.ascontiguousarray(dnp.astype(np.int32))
    gus = np.concatenate([f32c(ins["L0_ffn_gate_up_scales"]),
                          f32c(ins["L0_ffn_shared_gate_up_scales"])[None]], axis=0)
    dns = np.concatenate([f32c(ins["L0_ffn_down_scales"]),
                          f32c(ins["L0_ffn_shared_down_scales"])[None]], axis=0)
    g16 = np.arange(P) // 16
    sg = np.empty((NEA, 2, P, 6144), dtype=np.float32)
    for wt in range(2):
        sg[:, wt] = np.repeat(gus[:, 8 * wt + g16, :], 2, axis=2)
    d["sgu"] = np.ascontiguousarray(sg)
    d["sd0"] = np.ascontiguousarray(np.repeat(dns[:, g16, :], 2, axis=2))
    g_t = 8 + (np.arange(P) % 64) // 16
    d["sd1"] = np.ascontiguousarray(np.repeat(dns[:, g_t, :], 2, axis=2))
    return d


class Ctx:
    pass


def inc(ctx, instr, name, n=1):
    instr.then_inc(ctx.sem[name], n)
    ctx.cnt[name] += n
    return ctx.cnt[name]


def inc_k(ctx, instr, name, key, n=1):
    v = inc(ctx, instr, name, n)
    assert ctx.t[key] == v, f"preset mismatch {key}: preset={ctx.t[key]} actual={v}"
    return v


def preset(ctx, name, keys, n=1):
    base = ctx.cnt[name]
    for i, k in enumerate(keys):
        ctx.t[k] = base + (i + 1) * n


def wait(ctx, eng, name, val):
    if val > 0:
        eng.wait_ge(ctx.sem[name], val)


def dvsync(ctx, v, instr):
    t = inc(ctx, instr, "dve")
    wait(ctx, v, "dve", t)
    return t


def emit_decode_pass(ctx, v, src, scale_fp16, dst, q2):
    sb = ctx.sb
    n = src.shape[-1]
    d1, d2 = sb["d1"][:, 0:n], sb["d2"][:, 0:n]
    f2, f3 = sb["f2"][:, 0:n], sb["f3"][:, 0:n]
    if q2 == 0:
        v.tensor_scalar(d1, src, 7, None, Op.bitwise_and)
    else:
        v.tensor_scalar(d1, src, 4 * q2, 7, Op.logical_shift_right, Op.bitwise_and)
    v.tensor_scalar(d2, d1, 512, 14336, Op.mult, Op.add)
    v.tensor_scalar(f2, d2.bitcast(FP16), 2.0, -1.0, Op.mult, Op.add)
    v.tensor_tensor(f3, d2.bitcast(FP16), f2, Op.min)
    v.tensor_scalar(d1, src, 12 - 4 * q2, 0x8000, Op.logical_shift_left,
                    Op.bitwise_and)
    v.tensor_tensor(f2.bitcast(U16), f3.bitcast(U16), d1, Op.bitwise_or)
    return v.tensor_tensor(dst, f2, scale_fp16, Op.mult)


def build_nc():
    nc = bass.Bass("TRN2", target_bir_lowering=False, debug=False,
                   detect_race_conditions=False)
    stack = contextlib.ExitStack()
    ctx = Ctx()
    ctx.nc = nc
    ctx.sem, ctx.cnt, ctx.t = {}, {}, {}
    ctx.r_C = _CTX_RC

    dr = {}
    def DR(name, shape, dt, kind):
        dr[name] = nc.dram_tensor(name, shape, dt, kind=kind).ap()
    scr = "ExternalOutput" if DBG else "Internal"
    DR("x", [T, H], F32, "ExternalInput")
    DR("out", [T, H], F32, "ExternalOutput")
    for n in ["wq", "wk", "wv", "wo"]:
        DR(n, [H, H], F32, "ExternalInput")
    DR("rmsw", [P, NPL], F32, "ExternalInput")
    DR("gwT", [H, NE], F32, "ExternalInput")
    DR("gup", [NEA, 256, 3072], I32, "ExternalInput")
    DR("dnp", [NEA, 192, 2048], I32, "ExternalInput")
    DR("sgu", [NEA, 2, P, 6144], F32, "ExternalInput")
    DR("sd0", [NEA, P, 4096], F32, "ExternalInput")
    DR("sd1", [NEA, P, 4096], F32, "ExternalInput")
    for n, shape in [("identb", [P, P]), ("onesfull", [P, P]),
                     ("rotA", [P, T]), ("rotB", [P, T]), ("pswap", [P, P]),
                     ("qdec", [P, NH * 128]),
                     ("kdec", [P, NH]), ("dmaskT", [P, NH * 128]),
                     ("esel", [P, NE * 128])]:
        DR(n, shape, F32, "ExternalInput")
    DR("act_scr", [NEA, 12, P, T], BF16, scr)
    DR("dgu_scr", [NEA, P, 16, 2048], BF16, "Internal")
    DR("dgt_scr", [NEA, P, 16, 1024], BF16, "Internal")
    DR("ddn_scr", [NEA, 8, P, 4096], BF16, "Internal")
    DR("s_scr", [T, H], BF16, scr)
    DR("wos_scr", [NPL, P, H], BF16, scr)
    DR("at_spill", [P, 16384], BF16, "Internal")
    if DBG:
        DR("xnT_dbg", [P, NPL * T], F32, "ExternalOutput")
        DR("comb_dbg", [P, NPL * NE], F32, "ExternalOutput")
        DR("Y_dbg", [2, P, NPL * 1024], F32, "ExternalOutput")
    ctx.dr = dr

    for n in ["pe", "dve", "act", "gp", "dvp",
              "ld_c1", "ld_c2", "ld_rot", "ld_wo", "ld_x0", "ld_x1", "ld_pk", "ld_sc",
              "ld_wx", "ld_wos0", "ld_wos1", "ld_actin",
              "ld_ex0", "ld_ex1", "ld_dm", "ld_dt", "ld_dn",
              "st_wos", "st_act", "st_s0", "st_s1", "st_dec0", "st_dec1",
              "sto0", "sto1", "st_dbg"]:
        ctx.sem[n] = stack.enter_context(nc.semaphore(n))
        ctx.cnt[n] = 0

    sb = {}
    def S(name, shape, dt):
        sb[name] = stack.enter_context(nc.sbuf_tensor("s_" + name, shape, dt))
    S("AR1", [P, 65536], U8)
    S("AR2", [P, 45568], U8)
    S("AR3", [P, 65536], U8)
    S("identb", [P, P], BF16)
    S("identf", [P, P], F32)
    S("onesfull", [P, P], BF16)
    S("onescolb", [P, 1], BF16)
    S("rotA", [P, T], BF16)
    S("rotB", [P, T], BF16)
    S("pswap", [P, P], BF16)
    S("qdec", [P, NH * 128], BF16)
    S("kdec", [P, NH], F32)
    S("dmaskT", [P, NH * 128], BF16)
    S("rmsw", [P, NPL], F32)
    S("gwTp", [P, NPL * NE], BF16)
    S("combp", [P, NPL * NE], F32)
    S("d1", [P, 1024], U16)
    S("d2", [P, 1024], U16)
    S("f1", [P, 1024], FP16)
    S("f2", [P, 1024], FP16)
    S("f3", [P, 1024], FP16)
    S("lnst", [P, 16], F32)
    S("mx", [P, 8], F32)
    S("exw", [P, NE], F32)
    S("cp4", [P, NE], F32)
    S("cp4b", [P, NE], F32)
    S("mskb", [P, NE], F32)
    S("rstdtm", [P, NPL], F32)
    S("sg0", [P, 512], FP16)
    S("sg1", [P, 512], FP16)
    S("uc0", [P, 512], FP16)
    S("uc1", [P, 512], FP16)
    S("qdh", [P, 128], BF16)
    S("smh", [P, 128], BF16)
    S("smh2", [P, 128], BF16)
    S("kvt", [P, 128], F32)
    S("etmp", [P, 128], F32)
    ctx.sb = sb
    ctx.xnT = sb["AR1"][:].bitcast(BF16)

    ps = {}
    for n in ["pA", "pB", "pC", "pD", "pE", "pF"]:
        ps[n] = stack.enter_context(nc.psum_tensor(n, [P, 512], F32))
    ps["pT1"] = stack.enter_context(nc.psum_tensor("pT1", [P, 1024], BF16))
    ps["pT2"] = stack.enter_context(nc.psum_tensor("pT2", [P, 1024], BF16))
    ctx.ps = ps

    with nc.Block() as blk:
        emit_consts(ctx, blk)
        emit_boot_pre(ctx, blk)
        emit_A(ctx, blk)
        emit_boot_mid(ctx, blk)
        emit_B(ctx, blk)
        emit_boot(ctx, blk)
        emit_C(ctx, blk)
        emit_C2(ctx, blk)
        emit_DE(ctx, blk)
        emit_end(ctx, blk)

    stack.close()
    return ctx, nc


def emit_consts(ctx, blk):
    dr, sb = ctx.dr, ctx.sb

    def f(g):
        for n in ["identb", "onesfull", "pswap", "qdec", "dmaskT"]:
            inc(ctx, g.dma_start(out=sb[n][:], in_=dr[n]), "ld_c1", 16)
        inc(ctx, g.dma_start(out=sb["rotA"][:], in_=dr["esel"]), "ld_c1", 16)
        inc(ctx, g.dma_start(out=sb["onescolb"][:], in_=dr["onesfull"][0:P, 0:1],
                             allow_slow_non_contiguous=True), "ld_c1", 16)
        inc(ctx, g.dma_start(
            out=sb["gwTp"][:].rearrange("p (pl e) -> p pl e", pl=NPL),
            in_=dr["gwT"].rearrange("(pl p) e -> p pl e", pl=NPL, p=P)), "ld_c1", 16)
    blk.gpsimd(f)

    def f(s):
        inc(ctx, s.dma_start(out=sb["kdec"][:], in_=dr["kdec"]), "ld_c2", 16)
        inc(ctx, s.dma_start(out=sb["rmsw"][:], in_=dr["rmsw"]), "ld_c2", 16)
        inc(ctx, s.dma_start(out=sb["identf"][:], in_=dr["identb"]), "ld_c2", 16)
    blk.sync(f)
    ctx.t["c_ldg"] = ctx.cnt["ld_c1"]
    ctx.t["c_ldw"] = ctx.cnt["ld_c2"]


def emit_A(ctx, blk):
    dr, sb, ps = ctx.dr, ctx.sb, ctx.ps
    ar3 = sb["AR3"][:]
    xt = [ar3[:, 0:8192].bitcast(F32), ar3[:, 8192:16384].bitcast(F32)]
    sqb = ar3[:, 16384:24576].bitcast(F32)
    xnb = ar3[:, 24576:28672].bitcast(BF16)
    woin = ar3[:, 28672:36864].bitcast(F32)
    wob = ar3[:, 36864:40960].bitcast(BF16)

    preset(ctx, "dve", [("wosc", k) for k in range(NPL)])
    preset(ctx, "st_wos", [("wost", k) for k in range(NPL)], n=16)

    def f(s):
        wait(ctx, s, "ld_c2", ctx.t["c_ldw"])
        for k in range(NPL):
            if k >= 1:
                wait(ctx, s, "dve", ctx.t[("wosc", k - 1)])
            inc(ctx, s.dma_start(out=woin, in_=dr["wo"][k * P:(k + 1) * P, :]), "ld_wo", 16)
            ctx.t[("wol", k)] = ctx.cnt["ld_wo"]
    blk.sync(f)

    def f(v):
        for k in range(NPL):
            wait(ctx, v, "ld_wo", ctx.t[("wol", k)])
            if k >= 1:
                wait(ctx, v, "st_wos", ctx.t[("wost", k - 1)])
            inc_k(ctx, v.tensor_scalar(wob, woin, sb["rmsw"][:, k:k + 1], None, Op.mult),
                  "dve", ("wosc", k))
    blk.vector(f)

    def f(s):
        for k in range(NPL):
            wait(ctx, s, "dve", ctx.t[("wosc", k)])
            v = inc(ctx, s.dma_start(out=dr["wos_scr"][k], in_=wob), "st_wos", 16)
            assert v == ctx.t[("wost", k)]
    blk.scalar(f)

    for tt in range(16):
        def f(s, tt=tt):
            if tt >= 2:
                wait(ctx, s, "dve", ctx.t[("xnw", tt - 2)])
            sem = "ld_x0" if tt % 2 == 0 else "ld_x1"
            inc(ctx, s.dma_start(out=xt[tt % 2], in_=dr["x"][tt * P:(tt + 1) * P, :]), sem, 16)
            ctx.t[("xl", tt)] = ctx.cnt[sem]
        blk.sync(f)

        def f(v, tt=tt):
            wait(ctx, v, "ld_x0" if tt % 2 == 0 else "ld_x1", ctx.t[("xl", tt)])
            dvsync(ctx, v, v.reduce_sum(sb["lnst"][:, 0:1], xt[tt % 2], axis=AX.X))
            ctx.t[("mean", tt)] = dvsync(ctx, v, v.tensor_scalar(
                sb["lnst"][:, 1:2], sb["lnst"][:, 0:1], 1.0 / H, None, Op.mult))
        blk.vector(f)

        def f(sc, tt=tt):
            wait(ctx, sc, "ld_x0" if tt % 2 == 0 else "ld_x1", ctx.t[("xl", tt)])
            if tt >= 1:
                wait(ctx, sc, "dve", ctx.t[("var", tt - 1)])
            ctx.t[("sq", tt)] = inc(ctx, sc.activation(
                sqb, xt[tt % 2], AF.Square, accum_out=sb["lnst"][:, 2:3]), "act")
        blk.scalar(f)

        def f(v, tt=tt):
            wait(ctx, v, "act", ctx.t[("sq", tt)])
            v.tensor_scalar(sb["lnst"][:, 3:4], sb["lnst"][:, 2:3], 1.0 / H, 1e-5,
                            Op.mult, Op.add)
            dvsync(ctx, v, v.tensor_tensor(
                sb["lnst"][:, 4:5], sb["lnst"][:, 1:2], sb["lnst"][:, 1:2], Op.mult))
            ctx.t[("var", tt)] = inc(ctx, v.tensor_tensor(
                sb["lnst"][:, 5:6], sb["lnst"][:, 3:4], sb["lnst"][:, 4:5], Op.subtract), "dve")
        blk.vector(f)

        def f(sc, tt=tt):
            wait(ctx, sc, "dve", ctx.t[("var", tt)])
            ctx.t[("std", tt)] = inc(ctx, sc.activation(
                sb["lnst"][:, 6:7], sb["lnst"][:, 5:6], AF.Sqrt), "act")
        blk.scalar(f)

        def f(v, tt=tt):
            wait(ctx, v, "act", ctx.t[("std", tt)])
            if tt >= 1:
                wait(ctx, v, "pe", ctx.t[("tr", tt - 1, NPL - 1)])
            dvsync(ctx, v, v.reciprocal(sb["lnst"][:, 7:8], sb["lnst"][:, 6:7]))
            ctx.t[("xnw", tt)] = inc(ctx, v.tensor_scalar(
                xnb, xt[tt % 2], sb["lnst"][:, 1:2], sb["lnst"][:, 7:8],
                Op.subtract, Op.mult), "dve")
        blk.vector(f)

        preset(ctx, "dve", [("trev", tt, ti) for ti in range(NPL)])

        def f(te, tt=tt):
            if tt == 0:
                wait(ctx, te, "ld_c1", ctx.t["c_ldg"])
            wait(ctx, te, "dve", ctx.t[("xnw", tt)])
            for ti in range(NPL):
                gi = tt * NPL + ti
                if gi >= 2:
                    ptt, pti = divmod(gi - 2, NPL)
                    wait(ctx, te, "dve", ctx.t[("trev", ptt, pti)])
                w, j = ti // 8, ti % 8
                st = 1024 * w + j
                pT = ps["pT1"] if ti % 2 == 0 else ps["pT2"]
                ctx.t[("tr", tt, ti)] = inc(
                    ctx, te.transpose(pT[:, 0:128], xnb[:, st:st + 1017:8], sb["identb"][:]), "pe")
        blk.tensor(f)

        def f(v, tt=tt):
            for ti in range(NPL):
                wait(ctx, v, "pe", ctx.t[("tr", tt, ti)])
                pT = ps["pT1"] if ti % 2 == 0 else ps["pT2"]
                dst = ctx.xnT[:, ti * T + tt * P: ti * T + tt * P + P]
                inc_k(ctx, v.tensor_copy(out=dst, in_=pT[:, 0:128]), "dve", ("trev", tt, ti))
            for p_ in ctx.boot_passes[tt * 10:(tt + 1) * 10]:
                p_(v)
        blk.vector(f)
    ctx.t["xnT_done"] = ctx.cnt["dve"]

    if DBG:
        def f(g):
            wait(ctx, g, "dve", ctx.t["xnT_done"])
            inc(ctx, g.dma_start(out=ctx.dr["xnT_dbg"], in_=ctx.xnT), "st_dbg", 16)
        blk.gpsimd(f)


def emit_B(ctx, blk):
    sb, ps = ctx.sb, ctx.ps

    for tt in range(16):
        def f(te, tt=tt):
            if tt == 0:
                wait(ctx, te, "dve", ctx.t["xnT_done"])
                wait(ctx, te, "ld_c1", ctx.t["c_ldg"])
            pb = ps["pA"] if tt % 2 == 0 else ps["pB"]
            if tt >= 2:
                wait(ctx, te, "act", ctx.t[("ex", tt - 2)])
            for ti in range(NPL):
                mm = te.matmul(pb[:, 0:NE],
                               ctx.xnT[:, ti * T + tt * P: ti * T + tt * P + P],
                               sb["gwTp"][:, ti * NE:(ti + 1) * NE],
                               start=(ti == 0), stop=(ti == NPL - 1))
            ctx.t[("lg", tt)] = inc(ctx, mm, "pe")
        blk.tensor(f)

        def f(v, tt=tt):
            pb = ps["pA"] if tt % 2 == 0 else ps["pB"]
            wait(ctx, v, "pe", ctx.t[("lg", tt)])
            dvsync(ctx, v, v.reduce_max(sb["mx"][:, 0:1], pb[:, 0:NE], axis=AX.X))
            ctx.t[("nmx", tt)] = inc(ctx, v.tensor_scalar(
                sb["mx"][:, 1:2], sb["mx"][:, 0:1], -1.0, None, Op.mult), "dve")
        blk.vector(f)

        def f(sc, tt=tt):
            pb = ps["pA"] if tt % 2 == 0 else ps["pB"]
            wait(ctx, sc, "dve", ctx.t[("nmx", tt)])
            ctx.t[("ex", tt)] = inc(ctx, sc.activation(
                sb["exw"][:], pb[:, 0:NE], AF.Exp, bias=sb["mx"][:, 1:2]), "act")
        blk.scalar(f)

        def f(v, tt=tt):
            wait(ctx, v, "act", ctx.t[("ex", tt)])
            ex, mx = sb["exw"][:], sb["mx"]
            cps = [sb["cp4"][:], sb["cp4b"][:]]
            dvsync(ctx, v, v.tensor_copy(out=cps[0], in_=ex))
            for r_ in range(4):
                dvsync(ctx, v, v.reduce_max(mx[:, 2 + r_:3 + r_], cps[r_ % 2], axis=AX.X))
                if r_ < 3:
                    dvsync(ctx, v, v.tensor_scalar(sb["mskb"][:], cps[r_ % 2],
                                                   mx[:, 2 + r_:3 + r_], None, Op.is_ge))
                    dvsync(ctx, v, v.scalar_tensor_tensor(cps[(r_ + 1) % 2], sb["mskb"][:],
                                                          -2.0, cps[r_ % 2], Op.mult, Op.add))
            dvsync(ctx, v, v.tensor_tensor(sb["lnst"][:, 8:9], mx[:, 2:3], mx[:, 3:4], Op.add))
            dvsync(ctx, v, v.tensor_tensor(sb["lnst"][:, 9:10], mx[:, 4:5], mx[:, 5:6], Op.add))
            dvsync(ctx, v, v.tensor_tensor(sb["lnst"][:, 10:11], sb["lnst"][:, 8:9],
                                           sb["lnst"][:, 9:10], Op.add))
            dvsync(ctx, v, v.reciprocal(sb["lnst"][:, 11:12], sb["lnst"][:, 10:11]))
            dvsync(ctx, v, v.tensor_scalar(sb["mskb"][:], ex, mx[:, 5:6], None, Op.is_ge))
            dvsync(ctx, v, v.tensor_tensor(sb["cp4"][:], sb["mskb"][:], ex, Op.mult))
            ctx.t[("cmb", tt)] = inc(ctx, v.tensor_scalar(
                sb["combp"][:, tt * NE:(tt + 1) * NE], sb["cp4"][:],
                sb["lnst"][:, 11:12], None, Op.mult), "dve")
        blk.vector(f)
    ctx.t["comb_done"] = ctx.cnt["dve"]

    if DBG:
        def f(g):
            wait(ctx, g, "dve", ctx.t["comb_done"])
            inc(ctx, g.dma_start(out=ctx.dr["comb_dbg"], in_=sb["combp"][:]), "st_dbg", 16)
        blk.gpsimd(f)


PROD_NPASS = 80


def producer_passes(ctx, e):
    dr, sb = ctx.dr, ctx.sb
    ar2 = sb["AR2"][:]
    pkgu16 = ar2[:, 0:24576].bitcast(U16)
    pk0_16 = ar2[:, 0:8192].bitcast(U16)
    pk1_16 = ar2[:, 8192:16384].bitcast(U16)
    scbuf = ar2[:, 32768:40960].bitcast(FP16)
    stg = [ar2[:, 41472:43520].bitcast(BF16), ar2[:, 43520:45568].bitcast(BF16)]

    descs = []
    for wt in range(2):
        for q2 in range(4):
            for gu in range(2):
                for sub in range(2):
                    soff = wt * 6144 + gu * 3072 + sub * 1024
                    descs.append((wt, pkgu16[:, soff:soff + 1024],
                                  scbuf[:, gu * 2048 + sub * 1024:
                                        gu * 2048 + (sub + 1) * 1024],
                                  dr["dgu_scr"][e, :, (wt * 4 + q2) * 2 + gu,
                                                sub * 1024:(sub + 1) * 1024],
                                  q2, False))
    for wt in range(2):
        for q2 in range(4):
            for gu in range(2):
                soff = wt * 6144 + 2048 + gu * 3072
                descs.append((2 + wt, pkgu16[:, soff:soff + 1024],
                              scbuf[:, gu * 2048:gu * 2048 + 1024],
                              dr["dgt_scr"][e, :, (wt * 4 + q2) * 2 + gu, :],
                              q2, True))
    for hh in range(2):
        for wt2 in range(2):
            for q2 in range(4):
                for sub in range(2):
                    pksrc = pk0_16 if wt2 == 0 else pk1_16
                    descs.append((4 + hh,
                                  pksrc[:, hh * 2048 + sub * 1024:
                                        hh * 2048 + (sub + 1) * 1024],
                                  scbuf[:, wt2 * 2048 + sub * 1024:
                                        wt2 * 2048 + (sub + 1) * 1024],
                                  dr["ddn_scr"][e, wt2 * 4 + q2][
                                      :, hh * 2048 + sub * 1024:
                                      hh * 2048 + (sub + 1) * 1024],
                                  q2, False))
    assert len(descs) == PROD_NPASS

    if not hasattr(ctx, "plan"):
        ctx.plan = {"dvp": 0, "sd0": 0, "sd1": 0, "ld_pk": 0}
    ctx.t[("pkl", e)] = ctx.plan["ld_pk"] + 32
    ctx.t[("pkdl", e)] = ctx.plan["ld_pk"] + 80
    ctx.plan["ld_pk"] += 80
    dvb_ = ctx.plan["dvp"]
    for i in range(PROD_NPASS):
        ctx.t[("dvp", e, i)] = dvb_ + i + 1
    ctx.plan["dvp"] = dvb_ + PROD_NPASS
    b0, b1 = ctx.plan["sd0"], ctx.plan["sd1"]
    for i in range(PROD_NPASS):
        p = i % 2
        nprev = sum(1 for j in range(i + 1) if j % 2 == p)
        ctx.t[("sdec", e, i)] = (b0 if p == 0 else b1) + nprev * 16
    ctx.plan["sd0"] = b0 + (PROD_NPASS // 2) * 16
    ctx.plan["sd1"] = b1 + (PROD_NPASS // 2) * 16
    ctx.t[("pgu_done", e)] = ctx.t[("dvp", e, 47)]
    ctx.t[("pdone", e)] = ctx.t[("dvp", e, PROD_NPASS - 1)]

    prev_stg = (ctx.t.get(("stgfree", 0)), ctx.t.get(("stgfree", 1)))

    def mk_pass(i, d):
        grp, srcap, scap, dstdram, q2, tailperm = d

        def closure(v):
            if i == 0 or descs[i - 1][0] != grp:
                wait(ctx, v, "ld_sc", ctx.t[("pscl", e, grp)])
                if grp == 0:
                    wait(ctx, v, "ld_pk", ctx.t[("pkl", e)])
                if grp == 4:
                    wait(ctx, v, "ld_pk", ctx.t[("pkdl", e)])
                    wait(ctx, v, "gp", ctx.t[("pkz", e)])
            p = i % 2
            if i >= 2:
                wait(ctx, v, "st_dec0" if p == 0 else "st_dec1",
                     ctx.t[("sdec", e, i - 2)])
            elif prev_stg[p] is not None:
                wait(ctx, v, "st_dec0" if p == 0 else "st_dec1", prev_stg[p])
            dstap = stg[p][:, 0:1024]
            if tailperm:
                dstap = dstap.rearrange("p (jj m y) -> p m jj y", jj=2, m=64, y=8)
            inc_k(ctx, emit_decode_pass(ctx, v, srcap, scap, dstap, q2),
                  "dvp", ("dvp", e, i))
        return closure

    def mk_store(i, d):
        dstdram = d[3]

        def closure(s):
            wait(ctx, s, "dvp", ctx.t[("dvp", e, i)])
            p = i % 2
            sem = "st_dec0" if p == 0 else "st_dec1"
            v = inc(ctx, s.dma_start(out=dstdram, in_=stg[p][:, 0:1024]), sem, 16)
            assert v == ctx.t[("sdec", e, i)], (v, ctx.t[("sdec", e, i)])
        return closure

    passes = [mk_pass(i, d) for i, d in enumerate(descs)]
    stores = [mk_store(i, d) for i, d in enumerate(descs)]
    ctx.t[("stgfree", 0)] = ctx.t[("sdec", e, PROD_NPASS - 2)]
    ctx.t[("stgfree", 1)] = ctx.t[("sdec", e, PROD_NPASS - 1)]
    return passes, stores


def emit_producer_gp(ctx, blk, e, stores):

    emit_producer_gp_gu(ctx, blk, e, stores, ctx.t.get(("scfree2",)))
    ctx.t[("scfree2",)] = ctx.t[("dvp", e, 47)]
    emit_producer_gp_dn(ctx, blk, e, stores, ctx.t[("scfree2",)])
    ctx.t[("scfree2",)] = ctx.t[("dvp", e, PROD_NPASS - 1)]


def emit_producer_gp_gu(ctx, blk, e, stores, prev_dvp):
    dr, sb = ctx.dr, ctx.sb
    ar2 = sb["AR2"][:]
    scbuf = ar2[:, 32768:40960].bitcast(FP16)
    grp_begin = {0: 0, 1: 16, 2: 32, 3: 40, 4: 48}

    pkgu = ar2[:, 0:24576].bitcast(I32)

    def f(g, e=e):
        for grp in range(4):
            if grp > 0:
                for i in range(grp_begin[grp - 1], grp_begin[grp]):
                    stores[i](g)
                wait(ctx, g, "dvp", ctx.t[("dvp", e, {1: 15, 2: 31, 3: 39}[grp])])
            else:
                if prev_dvp is not None:
                    wait(ctx, g, "dvp", prev_dvp)
                inc(ctx, g.dma_start(out=pkgu[:, 0:3072],
                                     in_=dr["gup"][e, 0:128, :]), "ld_pk", 16)
                v = inc(ctx, g.dma_start(out=pkgu[:, 3072:6144],
                                         in_=dr["gup"][e, 128:256, :]), "ld_pk", 16)
                assert v == ctx.t[("pkl", e)], (v, ctx.t[("pkl", e)])
            if grp < 2:
                for gu in range(2):
                    inc(ctx, g.dma_start(out=scbuf[:, gu * 2048:gu * 2048 + 2048],
                                         in_=dr["sgu"][e, grp][:, gu * 3072:gu * 3072 + 2048]),
                        "ld_sc", 16)
            else:
                for gu in range(2):
                    inc(ctx, g.dma_start(out=scbuf[:, gu * 2048:gu * 2048 + 1024],
                                         in_=dr["sgu"][e, grp - 2][:, 2048 + gu * 3072:
                                                                   2048 + gu * 3072 + 1024]),
                        "ld_sc", 16)
            ctx.t[("pscl", e, grp)] = ctx.cnt["ld_sc"]
        for i in range(grp_begin[3], grp_begin[4]):
            stores[i](g)
    blk.gpsimd(f)


def emit_producer_gp_dn(ctx, blk, e, stores, prev_dvp):
    dr, sb = ctx.dr, ctx.sb
    ar2 = sb["AR2"][:]
    pk1_16 = ar2[:, 8192:16384].bitcast(U16)
    scbuf = ar2[:, 32768:40960].bitcast(FP16)

    pkd0 = ar2[:, 0:8192].bitcast(I32)
    pkd1 = ar2[:, 8192:16384].bitcast(I32)

    def f(g, e=e):
        for grp in (4, 5):
            if grp == 4:
                wait(ctx, g, "dvp", prev_dvp)
                inc(ctx, g.dma_start(out=pkd0[:], in_=dr["dnp"][e, 0:128, :]), "ld_pk", 16)
                inc(ctx, g.dma_start(out=pkd1[0:64, 0:2048],
                                     in_=dr["dnp"][e, 128:192, :]), "ld_pk", 16)
                src16 = dr["dnp"][e, 128:192, :].bitcast(U16)
                v = inc(ctx, g.dma_start(out=pk1_16[64:128, 0:4095],
                                         in_=src16[:, 1:4096]), "ld_pk", 16)
                assert v == ctx.t[("pkdl", e)], (v, ctx.t[("pkdl", e)])
            else:
                for i in range(48, 64):
                    stores[i](g)
                wait(ctx, g, "dvp", ctx.t[("dvp", e, 63)])
            hh = grp - 4
            inc(ctx, g.dma_start(out=scbuf[:, 0:2048],
                                 in_=dr["sd0"][e][:, hh * 2048:(hh + 1) * 2048]),
                "ld_sc", 16)
            inc(ctx, g.dma_start(out=scbuf[:, 2048:4096],
                                 in_=dr["sd1"][e][:, hh * 2048:(hh + 1) * 2048]),
                "ld_sc", 16)
            if hh == 0:
                ctx.t[("pkz", e)] = inc(ctx, g.memset(pk1_16[64:128, 4095:4096], 0), "gp")
            ctx.t[("pscl", e, grp)] = ctx.cnt["ld_sc"]
        for i in range(64, PROD_NPASS):
            stores[i](g)
    blk.gpsimd(f)


def emit_crep(ctx, blk, e):
    sb, ps = ctx.sb, ctx.ps
    ar2 = sb["AR2"][:]
    crep = ar2[:, 24576:28672].bitcast(FP16)
    esel = sb["rotA"][:]
    combT = sb["rotB"][:]
    preset(ctx, "pe", [("bct", e, q) for q in range(4)])
    preset(ctx, "dve", [("bcev4", e, q) for q in range(4)])

    cps = [ps["pE"][:, 0:512], ps["pF"][:, 0:512],
           ps["pT1"][:, 0:1024].bitcast(F32), ps["pT2"][:, 0:1024].bitcast(F32)]

    def f(te, e=e):
        if e == 0:
            wait(ctx, te, "dve", ctx.t["combT_done"])
            wait(ctx, te, "ld_c1", ctx.t["c_ldg"])
        for q in range(4):
            gi = e * 4 + q
            if gi >= 4:
                pe_, pq_ = divmod(gi - 4, 4)
                wait(ctx, te, "dve", ctx.t[("bcev4", pe_, pq_)])
            inc_k(ctx, te.matmul(cps[q],
                                 esel[0:16, e * 128:(e + 1) * 128],
                                 combT[0:16, q * 512:(q + 1) * 512],
                                 start=True, stop=True),
                  "pe", ("bct", e, q))
    blk.tensor(f)

    def f(v, e=e):
        for q in range(4):
            wait(ctx, v, "pe", ctx.t[("bct", e, q)])
            inc_k(ctx, v.tensor_copy(out=crep[:, q * 512:(q + 1) * 512],
                                     in_=cps[q]),
                  "dve", ("bcev4", e, q))
    blk.vector(f)
    ctx.t[("bcev", e, 15)] = ctx.t[("bcev4", e, 3)]
    ctx.t[("crep_mm_last",)] = ctx.t[("bct", e, 3)]


def emit_boot_pre(ctx, blk):
    ctx.boot_passes = []
    for e in (0, 1):
        passes, stores = producer_passes(ctx, e)
        emit_producer_gp(ctx, blk, e, stores)
        ctx.boot_passes += passes
    assert len(ctx.boot_passes) == 160


def emit_boot_mid(ctx, blk):
    pass


def emit_boot(ctx, blk):
    sb, ps = ctx.sb, ctx.ps
    combT = sb["rotB"][:]
    preset(ctx, "pe", [("cbt", tt) for tt in range(16)])
    preset(ctx, "dve", [("cbte", tt) for tt in range(16)])

    def f(te):
        wait(ctx, te, "dve", ctx.t["comb_done"])
        for tt in range(16):
            if tt >= 2:
                wait(ctx, te, "dve", ctx.t[("cbte", tt - 2)])
            pT = ps["pE"] if tt % 2 == 0 else ps["pF"]
            inc_k(ctx, te.transpose(pT[0:16, 0:128],
                                    sb["combp"][:, tt * NE:(tt + 1) * NE],
                                    sb["identf"][:]),
                  "pe", ("cbt", tt))
    blk.tensor(f)

    def f(v):
        for tt in range(16):
            wait(ctx, v, "pe", ctx.t[("cbt", tt)])
            pT = ps["pE"] if tt % 2 == 0 else ps["pF"]
            inc_k(ctx, v.tensor_copy(out=combT[0:16, tt * 128:(tt + 1) * 128],
                                     in_=pT[0:16, 0:128]),
                  "dve", ("cbte", tt))
    blk.vector(f)
    ctx.t["combT_done"] = ctx.cnt["dve"]
    emit_crep(ctx, blk, 0)


def emit_C(ctx, blk):
    dr, sb, ps = ctx.dr, ctx.sb, ctx.ps
    ar2 = sb["AR2"][:]
    crep = ar2[:, 24576:28672].bitcast(FP16)
    actst = ar2[:, 28672:32768].bitcast(BF16)
    bcb = [ar2[:, 40960:41216].bitcast(BF16), ar2[:, 41216:41472].bitcast(BF16)]
    dec = sb["AR3"][:].bitcast(BF16)

    def dchunk(wt, q2, gu, tail):
        idx = (wt * 4 + q2) * 2 + gu
        return dec[:, idx * 1024:(idx + 1) * 1024] if tail else \
            dec[:, idx * 2048:(idx + 1) * 2048]

    GU_PS = [(ps["pA"], ps["pB"]), (ps["pC"], ps["pD"])]
    ctx.t[("gu_ev_par", 0)] = 0
    ctx.t[("gu_ev_par", 1)] = 0
    ctx.t[("actst_par", 0)] = 0

    for e in range(NEA):
        if e + 2 < NEA:
            prod_passes, prod_stores = producer_passes(ctx, e + 2)
            emit_producer_gp(ctx, blk, e + 2, prod_stores)
        else:
            prod_passes, prod_stores = [], []
        n_main_p = (len(prod_passes) * 2) // 3

        def f(s, e=e):
            if e == 0:
                wait(ctx, s, "st_dec0", ctx.t[("sdec", 0, 30)])
                wait(ctx, s, "st_dec1", ctx.t[("sdec", 0, 31)])
                wait(ctx, s, "dve", ctx.t[("xnw", 15)])
                wait(ctx, s, "pe", ctx.t[("tr", 15, NPL - 1)])
                wait(ctx, s, "st_wos", ctx.t[("wost", NPL - 1)])
                wait(ctx, s, "act", ctx.t[("sq", 15)])
                inc(ctx, s.dma_start(out=dec[:, 0:32768],
                                     in_=dr["dgu_scr"][0].rearrange("p c n -> p (c n)")),
                    "ld_dm", 16)
                ctx.t[("ldm", 0)] = ctx.cnt["ld_dm"]
        blk.sync(f)

        if 0 < e < NEA - 1:
            emit_crep(ctx, blk, e)

        pass_cursor = [0]

        for cg in range(2):
            tail = cg == 1
            pts = list(range(4 if tail else 8))
            nq = len(pts) * 4

            preset(ctx, "dve", [("gu_ev", e, cg, q) for q in range(nq)])

            def f(te, e=e, cg=cg, tail=tail, pts=pts):
                wait(ctx, te, "ld_dm" if not tail else "ld_dt",
                     ctx.t[("ldm", e) if not tail else ("ldt", e)])
                if e == 0 and cg == 0:
                    wait(ctx, te, "act", ctx.t[("ex", 15)])

                for pi, pt in enumerate(pts):
                    for tc in range(4):
                        q = pi * 4 + tc
                        par = q % 2
                        psG, psU = GU_PS[par]
                        if q >= 2:
                            wait(ctx, te, "dve", ctx.t[("gu_ev", e, cg, q - 2)])
                        else:
                            wait(ctx, te, "dve", ctx.t[("gu_ev_par", par)])
                        for gu, pp in ((0, psG), (1, psU)):
                            kk = 0
                            for w in range(2):
                                for hlfH in range(2):
                                    for q2 in range(4):
                                        j = 4 * hlfH + q2
                                        ti = w * 8 + j
                                        ch = dchunk(w, q2, gu, tail)
                                        if not tail:
                                            lhsT = ch[:, 2 * pt + hlfH:
                                                      2 * pt + hlfH + 16 * 127 + 1:16]
                                        else:
                                            lhsT = ch[:, 2 * pt + hlfH:
                                                      2 * pt + hlfH + 8 * 127 + 1:8]
                                        mm = te.matmul(pp[:, 0:512], lhsT,
                                                       ctx.xnT[:, ti * T + tc * 512:
                                                               ti * T + tc * 512 + 512],
                                                       start=(kk == 0), stop=(kk == 15))
                                        kk += 1
                            ctx.t[("gu_mm", e, cg, q, gu)] = inc(ctx, mm, "pe")
                ctx.t[("mmm_last" if not tail else "mmt_last", e)] = ctx.cnt["pe"]
            blk.tensor(f)

            stbase = ctx.cnt["st_act"]
            dvb = ctx.cnt["dve"]
            prev_actst_par = ctx.t[("actst_par", 0)]
            for i_, pt in enumerate(pts):
                ctx.t[("actst", e, cg, pt)] = stbase + (i_ + 1) * 16
                ctx.t[("actw_pred", e, cg, pt)] = dvb + (i_ + 1) * 4

            def f(sc, e=e, cg=cg, tail=tail, pts=pts):
                for pi, pt in enumerate(pts):
                    for tc in range(4):
                        q = pi * 4 + tc
                        par = q % 2
                        psG, _ = GU_PS[par]
                        wait(ctx, sc, "pe", ctx.t[("gu_mm", e, cg, q, 0)])
                        sgb = sb["sg0"] if par == 0 else sb["sg1"]
                        ctx.t[("sg", e, cg, q)] = inc(
                            ctx, sc.activation(sgb[:], psG[:, 0:512], AF.Silu), "act")
                    pidx = (8 + pt) if tail else pt
                    wait(ctx, sc, "dve", ctx.t[("actw_pred", e, cg, pt)])
                    v = inc(ctx, sc.dma_start(out=dr["act_scr"][e, pidx], in_=actst[:]),
                            "st_act", 16)
                    assert v == ctx.t[("actst", e, cg, pt)]
                    ctx.t[("actst_par", 0)] = v
            blk.scalar(f)

            if not tail:
                def f(s, e=e):
                    wait(ctx, s, "pe", ctx.t[("mmm_last", e)])
                    wait(ctx, s, "st_dec0", ctx.t[("sdec", e, 46)])
                    wait(ctx, s, "st_dec1", ctx.t[("sdec", e, 47)])
                    inc(ctx, s.dma_start(out=dec[:, 0:16384],
                                         in_=dr["dgt_scr"][e].rearrange("p c n -> p (c n)")),
                        "ld_dt", 16)
                    ctx.t[("ldt", e)] = ctx.cnt["ld_dt"]
                    if e + 1 < NEA:
                        wait(ctx, s, "st_dec0", ctx.t[("sdec", e + 1, 30)])
                        wait(ctx, s, "st_dec1", ctx.t[("sdec", e + 1, 31)])
                        inc(ctx, s.dma_start(
                            out=dec[:, 16384:32768],
                            in_=dr["dgu_scr"][e + 1, :, 8:16, :].rearrange(
                                "p c n -> p (c n)")), "ld_dm", 16)
                blk.sync(f)

            def f(v, e=e, cg=cg, tail=tail, pts=pts, prev_actst_par=prev_actst_par,
                  prod_passes=prod_passes, pass_cursor=pass_cursor, n_main_p=n_main_p,
                  nq=nq):
                ntot = len(prod_passes)
                tgt_end = n_main_p if not tail else ntot
                tgt_beg = 0 if not tail else n_main_p
                if e < NEA - 1:
                    wait(ctx, v, "dve", ctx.t[("bcev", e, 15)])
                for pi, pt in enumerate(pts):
                    for tc in range(4):
                        q = pi * 4 + tc
                        par = q % 2
                        psG, psU = GU_PS[par]
                        wait(ctx, v, "pe", ctx.t[("gu_mm", e, cg, q, 1)])
                        wait(ctx, v, "act", ctx.t[("sg", e, cg, q)])
                        if tc == 0:
                            if pt >= 1:
                                wait(ctx, v, "st_act", ctx.t[("actst", e, cg, pt - 1)])
                            else:
                                wait(ctx, v, "st_act", prev_actst_par)
                        sgb = sb["sg0"] if par == 0 else sb["sg1"]
                        ucb = sb["uc0"] if par == 0 else sb["uc1"]
                        if e < NEA - 1:
                            v.tensor_tensor(ucb[:], psU[:, 0:512],
                                            crep[:, tc * 512:(tc + 1) * 512], Op.mult)
                        else:
                            v.tensor_copy(out=ucb[:], in_=psU[:, 0:512])
                        inc_k(ctx, v.tensor_tensor(actst[:, tc * 512:(tc + 1) * 512],
                                                   sgb[:], ucb[:], Op.mult),
                              "dve", ("gu_ev", e, cg, q))
                        want = tgt_beg + ((q + 1) * (tgt_end - tgt_beg)) // nq
                        while pass_cursor[0] < want:
                            prod_passes[pass_cursor[0]](v)
                            pass_cursor[0] += 1
                    ctx.t[("actw", e, cg, pt)] = ctx.cnt["dve"]
                    assert ctx.t[("actw", e, cg, pt)] == ctx.t[("actw_pred", e, cg, pt)]
                ctx.t[("gu_ev_par", 0)] = ctx.t[("gu_ev", e, cg, nq - 2)]
                ctx.t[("gu_ev_par", 1)] = ctx.t[("gu_ev", e, cg, nq - 1)]
            blk.vector(f)

            if tail:
                def f(s, e=e):
                    if e + 1 < NEA:
                        wait(ctx, s, "pe", ctx.t[("mmt_last", e)])
                        inc(ctx, s.dma_start(
                            out=dec[:, 0:16384],
                            in_=dr["dgu_scr"][e + 1, :, 0:8, :].rearrange(
                                "p c n -> p (c n)")), "ld_dm", 16)
                        ctx.t[("ldm", e + 1)] = ctx.cnt["ld_dm"]
                blk.sync(f)

    ctx.t["C_done_pe"] = ctx.cnt["pe"]
    ctx.t["C_done_sta"] = ctx.cnt["st_act"]
    ctx.t["C_done_dve"] = ctx.cnt["dve"]


def emit_C2(ctx, blk):
    dr, sb, ps = ctx.dr, ctx.sb, ctx.ps
    r_C = ctx.r_C
    ar1, ar2, ar3 = sb["AR1"][:], sb["AR2"][:], sb["AR3"][:]
    attnT = ar2[:, 0:32768].bitcast(BF16)
    wxh = ar3[:, 0:12288].bitcast(BF16)
    qTh = ar3[:, 12288:14336].bitcast(BF16)
    kTh = ar3[:, 14336:16384].bitcast(BF16)
    vTh = ar3[:, 16384:18432].bitcast(BF16)
    vnh = ar3[:, 18432:20480].bitcast(BF16)
    knh = ar3[:, 20480:22528].bitcast(BF16)
    kvpp = [ar3[:, 22528:30720].bitcast(F32), ar3[:, 30720:38912].bitcast(F32)]
    kvb = ar3[:, 38912:43008].bitcast(BF16)
    sqb = [ar3[:, 43008:45056].bitcast(BF16), ar3[:, 45056:47104].bitcast(BF16)]
    stile = [ar3[:, 47104:51200].bitcast(BF16), ar3[:, 51200:55296].bitcast(BF16)]
    rrow = ar3[0:1, 55296:59392].bitcast(F32)
    rrow2 = ar3[0:1, 59392:63488].bitcast(F32)
    qdF = vTh
    wosR = ar1.bitcast(BF16)

    def f(g):
        wait(ctx, g, "pe", ctx.t["C_done_pe"])
        g.memset(kvpp[0][:], 0.0)
        ctx.t["kv0"] = inc(ctx, g.memset(kvb[:], 0.0), "gp")
    blk.gpsimd(f)

    def f(g):
        wait(ctx, g, "pe", ctx.t[("crep_mm_last",)])
        inc(ctx, g.dma_start(out=sb["rotA"][:], in_=dr["rotA"]), "ld_rot", 16)
        inc(ctx, g.dma_start(out=sb["rotB"][:], in_=dr["rotB"]), "ld_rot", 16)
        ctx.t["rot_ld"] = ctx.cnt["ld_rot"]
    blk.gpsimd(f)

    QKV_PS = [(ps["pA"], ps["pB"]), (ps["pC"], ps["pD"])]
    PO = [ps["pE"], ps["pA"]]
    PSC = [ps["pF"], ps["pD"]]
    PKX = [ps["pC"], ps["pB"]]
    ctx.t[("qkv_prev_ev", 0)] = 0
    ctx.t[("qkv_prev_ev", 1)] = 0

    for tau in range(2):
        for h in range(NH):
            hi = tau * NH + h

            def f(g, tau=tau, h=h, hi=hi):
                if hi > 0:
                    wait(ctx, g, "pe", ctx.t[("wx_cons", hi - 1)])
                else:
                    wait(ctx, g, "pe", ctx.t["C_done_pe"])
                for xi, wn in enumerate(["wq", "wk", "wv"]):
                    src = dr[wn].rearrange("(ti p) e -> p ti e", ti=NPL, p=P)[
                        :, :, h * 128:(h + 1) * 128]
                    inc(ctx, g.dma_start(
                        out=wxh[:, xi * 2048:(xi + 1) * 2048].rearrange(
                            "p (ti e) -> p ti e", ti=NPL),
                        in_=src), "ld_wx", 16)
                ctx.t[("wxl", hi)] = ctx.cnt["ld_wx"]
            blk.gpsimd(f)

            for xi, dstb in ((0, qTh), (1, kTh), (2, vTh)):
                par = xi % 2
                p0, p1 = QKV_PS[par]

                def f(te, tau=tau, h=h, hi=hi, xi=xi, p0=p0, p1=p1, par=par):
                    if xi == 0:
                        wait(ctx, te, "ld_wx", ctx.t[("wxl", hi)])
                        if hi == 0:
                            wait(ctx, te, "dve", ctx.t["C_done_dve"])
                            wait(ctx, te, "gp", ctx.t["kv0"])
                        if hi > 0:
                            wait(ctx, te, "dve", ctx.t[("kvd", hi - 1, 7)])
                            wait(ctx, te, "act", ctx.t[("aev", hi - 1, 7)])
                            wait(ctx, te, "dve", ctx.t[("sm", hi - 1, 7)])
                    if xi == 2:
                        wait(ctx, te, "dve", ctx.t[("rot", hi, 0)])
                    else:
                        wait(ctx, te, "dve", ctx.t[("qkv_prev_ev", par)])
                    for nc_ in range(2):
                        pp = (p0, p1)[nc_]
                        for ti in range(NPL):
                            mm = te.matmul(pp[:, 0:512],
                                           wxh[:, xi * 2048 + ti * 128:
                                               xi * 2048 + (ti + 1) * 128],
                                           ctx.xnT[:, ti * T + tau * 1024 + nc_ * 512:
                                                   ti * T + tau * 1024 + nc_ * 512 + 512],
                                           start=(ti == 0), stop=(ti == NPL - 1))
                    ctx.t[("qkvmm", hi, xi)] = inc(ctx, mm, "pe")
                    if xi == 2:
                        ctx.t[("wx_cons", hi)] = ctx.cnt["pe"]
                blk.tensor(f)

                def f(v, tau=tau, h=h, hi=hi, xi=xi, dstb=dstb, p0=p0, p1=p1, par=par):
                    wait(ctx, v, "pe", ctx.t[("qkvmm", hi, xi)])
                    if hi == 0 and xi == 0:
                        wait(ctx, v, "st_act", ctx.t["C_done_sta"])
                        wait(ctx, v, "dvp", ctx.t[("pdone", NEA - 1)])
                    v.tensor_copy(out=dstb[:, 0:512], in_=p0[:, 0:512])
                    ctx.t[("qkvev", hi, xi)] = inc(
                        ctx, v.tensor_copy(out=dstb[:, 512:1024], in_=p1[:, 0:512]), "dve")
                    ctx.t[("qkv_prev_ev", par)] = ctx.t[("qkvev", hi, xi)]
                blk.vector(f)

                if xi < 2:
                    def f(te, tau=tau, hi=hi, xi=xi, dstb=dstb, p0=p0, p1=p1):
                        wait(ctx, te, "dve", ctx.t[("qkvev", hi, xi)])
                        for nc_ in range(2):
                            pp = (p0, p1)[nc_]
                            mm = te.matmul(pp[:, 0:512], sb["pswap"][:],
                                           dstb[:, nc_ * 512:(nc_ + 1) * 512],
                                           start=True, stop=True)
                        ctx.t[("swmm", hi, xi)] = inc(ctx, mm, "pe")
                    blk.tensor(f)

                    def f(v, tau=tau, hi=hi, xi=xi, dstb=dstb, p0=p0, p1=p1, par=par):
                        wait(ctx, v, "pe", ctx.t[("swmm", hi, xi)])
                        if ("rot_waited",) not in ctx.t:
                            wait(ctx, v, "ld_rot", ctx.t["rot_ld"])
                            ctx.t[("rot_waited",)] = 1
                        t0 = tau * 1024
                        tA = sb["f1"][:].bitcast(BF16)
                        tB = sb["f2"][:].bitcast(BF16)
                        v.tensor_tensor(tA, dstb[:], sb["rotA"][:, t0:t0 + 1024], Op.mult)
                        v.tensor_tensor(tB[:, 0:512], p0[:, 0:512],
                                        sb["rotB"][:, t0:t0 + 512], Op.mult)
                        v.tensor_tensor(tB[:, 512:1024], p1[:, 0:512],
                                        sb["rotB"][:, t0 + 512:t0 + 1024], Op.mult)
                        ctx.t[("rot", hi, xi)] = inc(
                            ctx, v.tensor_tensor(dstb[:], tA, tB, Op.add), "dve")
                        ctx.t[("qkv_prev_ev", par)] = ctx.t[("rot", hi, xi)]
                    blk.vector(f)

            preset(ctx, "dve", [("tnev", hi * 16 + c) for c in range(16)])

            def f(te, tau=tau, h=h, hi=hi):
                wait(ctx, te, "dve", ctx.t[("rot", hi, 1)])
                wait(ctx, te, "dve", ctx.t[("qkvev", hi, 2)])
                for c in range(16):
                    gi = hi * 16 + c
                    if gi >= 2:
                        wait(ctx, te, "dve", ctx.t[("tnev", gi - 2)])
                    else:
                        wait(ctx, te, "dve", ctx.t[("bcev", NEA - 2, 15)])
                    pT = ps["pT1"] if c % 2 == 0 else ps["pT2"]
                    srcb = kTh if c < 8 else vTh
                    cc = c % 8
                    key = ("tn", hi, cc) if c < 8 else ("tv", hi, cc)
                    ctx.t[key] = inc(
                        ctx, te.transpose(pT[:, 0:128], srcb[:, cc * 128:(cc + 1) * 128],
                                          sb["identb"][:]), "pe")
            blk.tensor(f)

            def f(v, tau=tau, h=h, hi=hi):
                for c in range(8):
                    wait(ctx, v, "pe", ctx.t[("tn", hi, c)])
                    pT = ps["pT1"] if c % 2 == 0 else ps["pT2"]
                    inc_k(ctx, v.tensor_scalar(knh[:, c * 128:(c + 1) * 128], pT[:, 0:128],
                                               sb["kdec"][:, h:h + 1], None, Op.mult),
                          "dve", ("tnev", hi * 16 + c))
                for c in range(8):
                    wait(ctx, v, "pe", ctx.t[("tv", hi, c)])
                    pT = ps["pT1"] if c % 2 == 0 else ps["pT2"]
                    inc_k(ctx, v.tensor_copy(out=vnh[:, c * 128:(c + 1) * 128],
                                             in_=pT[:, 0:128]),
                          "dve", ("tnev", hi * 16 + 8 + c))
                for c in range(8):
                    last = v.tensor_tensor(qdF[:, c * 128:(c + 1) * 128],
                                           qTh[:, c * 128:(c + 1) * 128],
                                           sb["qdec"][:, h * 128:(h + 1) * 128], Op.mult)
                ctx.t[("qdw", hi)] = inc(ctx, last, "dve")
            blk.vector(f)

            for c in range(8):
                cp = c % 2

                def f(te, tau=tau, h=h, hi=hi, c=c, cp=cp):
                    if c == 0:
                        wait(ctx, te, "dve", ctx.t[("qdw", hi)])
                    if c >= 2:
                        wait(ctx, te, "dve", ctx.t[("sm", hi, c - 2)])
                    elif hi > 0:
                        wait(ctx, te, "dve", ctx.t[("sm", hi - 1, c + 6)])
                    ctx.t[("scm", hi, c)] = inc(ctx, te.matmul(
                        PSC[cp][:, 0:128], kTh[:, c * 128:(c + 1) * 128],
                        qTh[:, c * 128:(c + 1) * 128], start=True, stop=True), "pe")
                blk.tensor(f)

                def f(v, tau=tau, h=h, hi=hi, c=c, cp=cp):
                    wait(ctx, v, "pe", ctx.t[("scm", hi, c)])
                    ctx.t[("sm", hi, c)] = inc(ctx, v.tensor_tensor(
                        sb["smh" if cp == 0 else "smh2"][:], PSC[cp][:, 0:128],
                        sb["dmaskT"][:, h * 128:(h + 1) * 128], Op.mult), "dve")
                blk.vector(f)

                def f(te, tau=tau, h=h, hi=hi, c=c, cp=cp):
                    gc = hi * 8 + c
                    if gc > 0:
                        wait(ctx, te, "dve", ctx.t[("kvd_d", gc - 1)])
                    if c >= 2:
                        wait(ctx, te, "act", ctx.t[("aev", hi, c - 2)])
                    elif hi > 0:
                        wait(ctx, te, "act", ctx.t[("aev", hi - 1, c + 6)])
                    te.matmul(PO[cp][:, 0:128], kvb[:, h * 128:(h + 1) * 128],
                              qdF[:, c * 128:(c + 1) * 128], start=True, stop=False)
                    wait(ctx, te, "dve", ctx.t[("sm", hi, c)])
                    ctx.t[("ao", hi, c)] = inc(ctx, te.matmul(
                        PO[cp][:, 0:128], vnh[:, c * 128:(c + 1) * 128],
                        sb["smh" if cp == 0 else "smh2"][:], start=False, stop=True), "pe")
                    ctx.t[("kvx", hi, c)] = inc(ctx, te.matmul(
                        PKX[cp][:, 0:128], knh[:, c * 128:(c + 1) * 128],
                        vnh[:, c * 128:(c + 1) * 128], start=True, stop=True), "pe")
                blk.tensor(f)

                def f(sc, tau=tau, h=h, hi=hi, c=c, cp=cp):
                    wait(ctx, sc, "pe", ctx.t[("ao", hi, c)])
                    if hi == NH and h == 0 and c == 0 and tau == 1:
                        pass
                    if tau == 1 and h == 0 and c == 0:
                        wait(ctx, sc, "st_s0", ctx.t[("atspill",)])
                    ctx.t[("aev", hi, c)] = inc(ctx, sc.copy(
                        out=attnT[:, h * 1024 + c * 128: h * 1024 + (c + 1) * 128],
                        in_=PO[cp][:, 0:128]), "act")
                blk.scalar(f)

                def f(v, tau=tau, h=h, hi=hi, c=c, cp=cp):
                    gc = hi * 8 + c
                    wait(ctx, v, "pe", ctx.t[("kvx", hi, c)])
                    kv_c = kvpp[gc % 2][:, h * 128:(h + 1) * 128]
                    kv_n = kvpp[(gc + 1) % 2][:, h * 128:(h + 1) * 128]
                    v.scalar_tensor_tensor(kv_n, kv_c, r_C[h], PKX[cp][:, 0:128],
                                           Op.mult, Op.add)
                    ctx.t[("kvd", hi, c)] = inc(ctx, v.tensor_copy(
                        out=kvb[:, h * 128:(h + 1) * 128], in_=kv_n), "dve")
                    ctx.t[("kvd_d", gc)] = ctx.t[("kvd", hi, c)]
                blk.vector(f)

        if tau == 0:
            def f(sc, tau=tau):
                for k in range(NPL):
                    wait(ctx, sc, "act", ctx.t[("aev", tau * NH + k, 7)])
                inc(ctx, sc.dma_start(out=dr["at_spill"], in_=attnT[:]), "st_s0", 16)
                ctx.t[("atspill",)] = ctx.cnt["st_s0"]
            blk.scalar(f)

    def f(s):
        wait(ctx, s, "pe", ctx.t[("wx_cons", 2 * NH - 1)])
        wait(ctx, s, "st_wos", ctx.t[("wost", NPL - 1)])
        inc(ctx, s.dma_start(out=wosR.rearrange("p (k n) -> p k n", k=NPL),
                             in_=dr["wos_scr"].rearrange("k p n -> p k n")), "ld_wos0", 16)
        ctx.t["wosl_all"] = ctx.cnt["ld_wos0"]
    blk.sync(f)

    for tau in (1, 0):
        if tau == 0:
            def f(s):
                wait(ctx, s, "pe", ctx.t[("smm", 1, 7)])
                wait(ctx, s, "st_s0", ctx.t[("atspill",)])
                inc(ctx, s.dma_start(out=attnT[:], in_=dr["at_spill"]), "ld_wos0", 16)
                ctx.t["atload"] = ctx.cnt["ld_wos0"]
            blk.sync(f)

        pebase = ctx.cnt["pe"]
        for k in range(NPL):
            ctx.t[("ssq", tau, k)] = pebase + k + 1

        def f(sc, tau=tau):
            for k in range(NPL):
                if tau == 1:
                    wait(ctx, sc, "act", ctx.t[("aev", tau * NH + k, 7)])
                    wait(ctx, sc, "dve", ctx.t[("kvd", tau * NH + k, 7)])
                elif k == 0:
                    wait(ctx, sc, "ld_wos0", ctx.t["atload"])
                if k >= 2:
                    wait(ctx, sc, "pe", ctx.t[("ssq", tau, k - 2)])
                ctx.t[("sqw", tau, k)] = inc(ctx, sc.activation(
                    sqb[k % 2][:], attnT[:, k * 1024:(k + 1) * 1024], AF.Square), "act")
        blk.scalar(f)

        def f(te, tau=tau):
            pv = ctx.t.get(("rstdev", 1 if tau == 0 else -1))
            if pv is not None:
                wait(ctx, te, "dve", pv)
            for k in range(NPL):
                wait(ctx, te, "act", ctx.t[("sqw", tau, k)])
                for nc_ in range(2):
                    pp = (ps["pE"], ps["pF"])[nc_]
                    mm = te.matmul(pp[0:1, 0:512], sb["onescolb"][:],
                                   sqb[k % 2][:, nc_ * 512:(nc_ + 1) * 512],
                                   start=(k == 0), stop=(k == NPL - 1))
                inc_k(ctx, mm, "pe", ("ssq", tau, k))
        blk.tensor(f)

        def f(v, tau=tau):
            wait(ctx, v, "pe", ctx.t[("ssq", tau, NPL - 1)])
            dvsync(ctx, v, v.tensor_scalar(rrow[0:1, 0:512], ps["pE"][0:1, 0:512],
                                           1.0 / H, 1e-5, Op.mult, Op.add))
            ctx.t[("vrow", tau)] = inc(ctx, v.tensor_scalar(
                rrow[0:1, 512:1024], ps["pF"][0:1, 0:512], 1.0 / H, 1e-5,
                Op.mult, Op.add), "dve")
        blk.vector(f)

        def f(sc, tau=tau):
            wait(ctx, sc, "dve", ctx.t[("vrow", tau)])
            ctx.t[("srow", tau)] = inc(ctx, sc.activation(
                rrow2[0:1, :], rrow[0:1, :], AF.Sqrt), "act")
        blk.scalar(f)

        def f(v, tau=tau):
            wait(ctx, v, "act", ctx.t[("srow", tau)])
            ctx.t[("rrec", tau)] = inc(ctx, v.reciprocal(rrow[0:1, :], rrow2[0:1, :]), "dve")
        blk.vector(f)

        def f(te, tau=tau):
            wait(ctx, te, "dve", ctx.t[("rrec", tau)])
            for c in range(8):
                mm = te.matmul(ps["pE"][:, c:c + 1], rrow[0:1, c * 128:(c + 1) * 128],
                               sb["identf"][0:1, 0:1], start=True, stop=True)
            ctx.t[("rtm", tau)] = inc(ctx, mm, "pe")
        blk.tensor(f)

        def f(v, tau=tau):
            wait(ctx, v, "pe", ctx.t[("rtm", tau)])
            ctx.t[("rstdev", tau)] = inc(ctx, v.tensor_copy(
                out=sb["rstdtm"][:, tau * 8:tau * 8 + 8], in_=ps["pE"][:, 0:8]), "dve")
        blk.vector(f)

        for tc in range(8):
            def f(te, tau=tau, tc=tc):
                pv = ctx.t.get(("sev", tau, tc - 1))
                if pv is None:
                    pv = ctx.t.get(("sev", 1, 7)) if tau == 0 else None
                if pv is not None:
                    wait(ctx, te, "dve", pv)
                wait(ctx, te, "dve", ctx.t[("rstdev", tau)])
                if tau == 1 and tc == 0:
                    wait(ctx, te, "ld_wos0", ctx.t["wosl_all"])
                for k in range(NPL):
                    for nc_ in range(4):
                        pp = (ps["pA"], ps["pB"], ps["pC"], ps["pD"])[nc_]
                        mm = te.matmul(pp[:, 0:512],
                                       attnT[:, k * 1024 + tc * 128: k * 1024 + (tc + 1) * 128],
                                       wosR[:, k * 2048 + nc_ * 512:
                                            k * 2048 + (nc_ + 1) * 512],
                                       start=(k == 0), stop=(k == NPL - 1))
                ctx.t[("smm", tau, tc)] = inc(ctx, mm, "pe")
            blk.tensor(f)

            def f(v, tau=tau, tc=tc):
                wait(ctx, v, "pe", ctx.t[("smm", tau, tc)])
                stk = ("sst", tau, tc - 2)
                if stk in ctx.t:
                    wait(ctx, v, "st_s0" if tc % 2 == 0 else "st_s1", ctx.t[stk])
                sp = stile[tc % 2]
                for nc_ in range(4):
                    pp = (ps["pA"], ps["pB"], ps["pC"], ps["pD"])[nc_]
                    ts_ = v.tensor_scalar(sp[:, nc_ * 512:(nc_ + 1) * 512], pp[:, 0:512],
                                          sb["rstdtm"][:, tau * 8 + tc:tau * 8 + tc + 1],
                                          None, Op.mult)
                ctx.t[("sev", tau, tc)] = inc(ctx, ts_, "dve")
            blk.vector(f)

            def f(sc, tau=tau, tc=tc):
                wait(ctx, sc, "dve", ctx.t[("sev", tau, tc)])
                r0 = tau * 1024 + tc * 128
                sem = "st_s0" if tc % 2 == 0 else "st_s1"
                inc(ctx, sc.dma_start(out=dr["s_scr"][r0:r0 + 128, :],
                                      in_=stile[tc % 2][:]), sem, 16)
                ctx.t[("sst", tau, tc)] = ctx.cnt[sem]
            blk.scalar(f)

    ctx.t["C2_done_pe"] = ctx.cnt["pe"]
    ctx.t["C2_done_dve"] = ctx.cnt["dve"]
    ctx.t["C2_done_sts"] = (ctx.cnt["st_s0"], ctx.cnt["st_s1"])


def emit_DE(ctx, blk):
    dr, sb, ps = ctx.dr, ctx.sb, ctx.ps
    ar1, ar2, ar3 = sb["AR1"][:], sb["AR2"][:], sb["AR3"][:]
    Y = ar1.bitcast(F32)
    actin = ar2[:, 0:24576].bitcast(BF16)
    decs = [ar3[:, 0:32768].bitcast(BF16), ar3[:, 32768:65536].bitcast(BF16)]
    g_ = ar3[:, 0:32768].bitcast(BF16)
    xrel = [ar2[:, 0:8192].bitcast(F32), ar2[:, 8192:16384].bitcast(F32)]
    srel = [ar2[:, 16384:20480].bitcast(BF16), ar2[:, 20480:24576].bitcast(BF16)]
    outb = [ar2[:, 24576:32768].bitcast(F32), ar2[:, 32768:40960].bitcast(F32)]

    def dch(par_, wt2, q2):
        return decs[par_][:, (wt2 * 4 + q2) * 2048:(wt2 * 4 + q2 + 1) * 2048]

    PS4 = [ps["pA"], ps["pB"], ps["pC"], ps["pD"]]
    for k in range(4):
        ctx.t[("yadd_psum", k)] = 0
    ctx.t[("eev_par", 0)] = 0
    ctx.t[("eev_par", 1)] = 0

    for tau in range(2):
        for e in range(NEA):
            ei = tau * NEA + e

            def f(s, tau=tau, e=e, ei=ei):
                k2 = ("dmm_q_done", ei - 1, 0)
                if k2 in ctx.t:
                    wait(ctx, s, "pe", ctx.t[k2])
                elif tau == 0:
                    wait(ctx, s, "pe", ctx.t["C2_done_pe"])
                if tau == 1 and e == 0:
                    wait(ctx, s, "dve", ctx.t[("ee", 0, 7)])
                    wait(ctx, s, "sto0", ctx.t[("outst", 0, 6)])
                    wait(ctx, s, "sto1", ctx.t[("outst", 0, 7)])
                inc(ctx, s.dma_start(
                    out=actin[:, 0:6144].rearrange("p (pl t) -> p pl t", pl=12),
                    in_=dr["act_scr"][e].rearrange("pl p t -> p pl t")[
                        :, :, tau * 1024:tau * 1024 + 512]),
                    "ld_actin", 16)
                ctx.t[("actl", ei, 0)] = ctx.cnt["ld_actin"]
            blk.sync(f)

            for hh in range(2):
                gidx = ei * 2 + hh
                par_ = gidx % 2

                def f(s, tau=tau, e=e, ei=ei, hh=hh, gidx=gidx, par_=par_):
                    if gidx < 2:
                        wait(ctx, s, "pe", ctx.t["C2_done_pe"])
                        wait(ctx, s, "st_s0", ctx.t["C2_done_sts"][0])
                        wait(ctx, s, "st_s1", ctx.t["C2_done_sts"][1])
                        wait(ctx, s, "st_dec0", ctx.t[("sdec", NEA - 1, PROD_NPASS - 2)])
                        wait(ctx, s, "st_dec1", ctx.t[("sdec", NEA - 1, PROD_NPASS - 1)])
                    else:
                        wait(ctx, s, "pe", ctx.t[("dchunk_free", gidx - 2)])
                    if tau == 1 and e == 0 and hh == 0:
                        wait(ctx, s, "pe", ctx.t[("etr", 0, 7, 15)])
                    inc(ctx, s.dma_start(
                        out=decs[par_][:].rearrange("p (c n) -> p c n", c=8),
                        in_=dr["ddn_scr"][e].rearrange("c p n -> p c n")[
                            :, :, hh * 2048:(hh + 1) * 2048]),
                        "ld_dn", 16)
                    ctx.t[("dnl", gidx)] = ctx.cnt["ld_dn"]
                blk.sync(f)

            def f(s, ei=ei):
                k2 = ("dmm_q_done", ei - 1, 1)
                if k2 in ctx.t:
                    wait(ctx, s, "pe", ctx.t[k2])
                inc(ctx, s.dma_start(
                    out=actin[:, 6144:12288].rearrange("p (pl t) -> p pl t", pl=12),
                    in_=dr["act_scr"][ei % NEA].rearrange("pl p t -> p pl t")[
                        :, :, (ei // NEA) * 1024 + 512:(ei // NEA) * 1024 + 1024]),
                    "ld_actin", 16)
                ctx.t[("actl", ei, 1)] = ctx.cnt["ld_actin"]
            blk.sync(f)

            preset(ctx, "dve", [("yadd", ei, q, hb)
                                for q in range(2) for hb in range(16)])

            def f(te, tau=tau, e=e, ei=ei):
                for q in range(2):
                    wait(ctx, te, "ld_actin", ctx.t[("actl", ei, q)])
                    ah = actin[:, q * 6144:(q + 1) * 6144]
                    if ei == 0 and q == 0:
                        wait(ctx, te, "dve", ctx.t[("sev", 1, 7)])
                    for hh in range(2):
                        gidx = ei * 2 + hh
                        par_ = gidx % 2
                        if q == 0:
                            wait(ctx, te, "ld_dn", ctx.t[("dnl", gidx)])
                        for hbp in range(8):
                            hb = hh * 8 + hbp
                            pk = (ei * 32 + q * 16 + hb) % 4
                            pp = PS4[pk]
                            wait(ctx, te, "dve", ctx.t[("yadd_psum", pk)])
                            kk = 0
                            for wt2 in range(2):
                                for q2 in range(4):
                                    for hlfI in ((0, 1) if wt2 == 0 else (0,)):
                                        pl = (4 * hlfI + q2) if wt2 == 0 else (8 + q2)
                                        ch = dch(par_, wt2, q2)
                                        lhsT = ch[:, 256 * hbp + hlfI:
                                                  256 * hbp + hlfI + 255:2]
                                        mm = te.matmul(pp[:, 0:512], lhsT,
                                                       ah[:, pl * 512:(pl + 1) * 512],
                                                       start=(kk == 0), stop=(kk == 11))
                                        kk += 1
                            ctx.t[("dmm", ei, q, hb)] = inc(ctx, mm, "pe")
                            ctx.t[("yadd_psum", pk)] = ctx.t[("yadd", ei, q, hb)]
                        if q == 1:
                            ctx.t[("dchunk_free", gidx)] = ctx.cnt["pe"]
                    ctx.t[("dmm_q_done", ei, q)] = ctx.cnt["pe"]
                ctx.t[("dmm_last", ei)] = ctx.cnt["pe"]
            blk.tensor(f)

            def f(v, tau=tau, e=e, ei=ei):
                for q in range(2):
                    for hh in range(2):
                        for hbp in range(8):
                            hb = hh * 8 + hbp
                            pk = (ei * 32 + q * 16 + hb) % 4
                            pp = PS4[pk]
                            wait(ctx, v, "pe", ctx.t[("dmm", ei, q, hb)])
                            if tau == 1 and e == 0:
                                wait(ctx, v, "act", ctx.t[("gw", 0, hb)])
                            ysl = Y[:, hb * 1024 + q * 512:
                                    hb * 1024 + q * 512 + 512]
                            if e == 0:
                                last = v.tensor_copy(out=ysl, in_=pp[:, 0:512])
                            else:
                                last = v.tensor_tensor(ysl, ysl, pp[:, 0:512], Op.add)
                            inc_k(ctx, last, "dve", ("yadd", ei, q, hb))
            blk.vector(f)

        ei_last = tau * NEA + NEA - 1

        def f(sc, tau=tau, ei_last=ei_last):
            for hb in range(16):
                wait(ctx, sc, "dve", ctx.t[("yadd", ei_last, 1, hb)])
                if hb == 0:
                    wait(ctx, sc, "pe", ctx.t[("dmm_last", ei_last)])
                ctx.t[("gw", tau, hb)] = inc(ctx, sc.activation(
                    g_[:, hb * 1024:(hb + 1) * 1024],
                    Y[:, hb * 1024:(hb + 1) * 1024], AF.Silu), "act")
        blk.scalar(f)

        if DBG:
            def f(g, tau=tau, ei_last=ei_last):
                for hb in range(16):
                    wait(ctx, g, "dve", ctx.t[("yadd", ei_last, 1, hb)])
                inc(ctx, g.dma_start(out=ctx.dr["Y_dbg"][tau], in_=Y[:]), "st_dbg", 16)
            blk.gpsimd(f)

        for tc in range(8):
            def f(s, tau=tau, tc=tc, ei_last=ei_last):
                pk = ("outst", tau, tc - 2)
                if pk in ctx.t:
                    wait(ctx, s, "sto0" if tc % 2 == 0 else "sto1", ctx.t[pk])
                pk = ("ee", tau, tc - 2)
                if pk in ctx.t:
                    wait(ctx, s, "dve", ctx.t[pk])
                if tc < 2:
                    wait(ctx, s, "pe", ctx.t[("dmm_last", ei_last)])
                r0 = tau * 1024 + tc * 128
                sem = "ld_ex0" if tc % 2 == 0 else "ld_ex1"
                inc(ctx, s.dma_start(out=xrel[tc % 2], in_=dr["x"][r0:r0 + 128, :]), sem, 16)
                inc(ctx, s.dma_start(out=srel[tc % 2], in_=dr["s_scr"][r0:r0 + 128, :]),
                    sem, 16)
                ctx.t[("exl", tau, tc)] = ctx.cnt[sem]
            blk.sync(f)

            preset(ctx, "dve", [("eev", tau, tc, hb) for hb in range(16)])

            def f(te, tau=tau, tc=tc):
                for hb in range(16):
                    wait(ctx, te, "act", ctx.t[("gw", tau, hb)])
                    if hb >= 2:
                        wait(ctx, te, "dve", ctx.t[("eev", tau, tc, hb - 2)])
                    else:
                        wait(ctx, te, "dve", ctx.t[("eev_par", hb % 2)])
                    pT = ps["pT1"] if hb % 2 == 0 else ps["pT2"]
                    ctx.t[("etr", tau, tc, hb)] = inc(
                        ctx, te.transpose(pT[:, 0:128],
                                          g_[:, hb * 1024 + tc * 128:
                                             hb * 1024 + (tc + 1) * 128],
                                          sb["identb"][:]), "pe")
            blk.tensor(f)

            def f(v, tau=tau, tc=tc):
                wait(ctx, v, "ld_ex0" if tc % 2 == 0 else "ld_ex1", ctx.t[("exl", tau, tc)])
                for hb in range(16):
                    wait(ctx, v, "pe", ctx.t[("etr", tau, tc, hb)])
                    pT = ps["pT1"] if hb % 2 == 0 else ps["pT2"]
                    v.tensor_tensor(sb["etmp"][:], pT[:, 0:128],
                                    srel[tc % 2][:, hb * 128:(hb + 1) * 128], Op.mult)
                    inc_k(ctx, v.tensor_tensor(outb[tc % 2][:, hb * 128:(hb + 1) * 128],
                                               sb["etmp"][:],
                                               xrel[tc % 2][:, hb * 128:(hb + 1) * 128],
                                               Op.add),
                          "dve", ("eev", tau, tc, hb))
                    ctx.t[("eev_par", hb % 2)] = ctx.t[("eev", tau, tc, hb)]
                ctx.t[("ee", tau, tc)] = ctx.cnt["dve"]
            blk.vector(f)

            def f(sc, tau=tau, tc=tc):
                wait(ctx, sc, "dve", ctx.t[("ee", tau, tc)])
                r0 = tau * 1024 + tc * 128
                sem = "sto0" if tc % 2 == 0 else "sto1"
                inc(ctx, sc.dma_start(out=dr["out"][r0:r0 + 128, :],
                                      in_=outb[tc % 2][:]), sem, 16)
                ctx.t[("outst", tau, tc)] = ctx.cnt[sem]
            blk.scalar(f)


def emit_end(ctx, blk):
    def f(g):
        g.wait_ge(ctx.sem["sto0"], ctx.cnt["sto0"])
        g.wait_ge(ctx.sem["sto1"], ctx.cnt["sto1"])
        if ctx.cnt["st_dbg"]:
            g.wait_ge(ctx.sem["st_dbg"], ctx.cnt["st_dbg"])
    blk.gpsimd(f)


_NC = None


def kernel(**inputs):
    global _NC, _CTX_RC
    cons = host_consts()
    prep = prep_inputs(inputs)
    if _NC is None:
        _CTX_RC = cons["_r_C"]
        _ctx, _NC = build_nc()
    x = np.ascontiguousarray(np.asarray(inputs["L0_attn_x"], dtype=np.float32))
    base = {}
    base.update(prep)
    base.update({k: v for k, v in cons.items() if not k.startswith("_")})
    in_maps = [dict(base, x=x[b]) for b in range(8)]
    res = run_bass_kernel_spmd(_NC, in_maps, core_ids=list(range(8)))
    return np.stack([res.results[b]["out"] for b in range(8)], axis=0)


# version 17
# speedup vs baseline: 1.1587x; 1.1577x over previous
import sys
if '/opt/trn_rl_repo' not in sys.path:
    sys.path.insert(0, '/opt/trn_rl_repo')
import math
import contextlib
import numpy as np
import concourse.bass as bass
import concourse.mybir as mybir
from concourse.bass_utils import run_bass_kernel_spmd
from concourse.alu_op_type import AluOpType as Op

AF = mybir.ActivationFunctionType
AX = mybir.AxisListType
F32 = mybir.dt.float32
BF16 = mybir.dt.bfloat16
FP16 = mybir.dt.float16
I32 = mybir.dt.int32
U16 = mybir.dt.uint16
U8 = mybir.dt.uint8

P = 128
T = 2048
H = 2048
NH = 16
NE = 16
NEA = 17
NPL = 16

DBG = False
_CTX_RC = None


def _slopes(n):
    start = 2.0 ** (-2.0 ** (-(math.log2(n) - 3)))
    return np.array([start * start ** i for i in range(n)], dtype=np.float64)


def host_consts():
    c = {}
    c["identb"] = np.eye(P, dtype=np.float32)
    c["onesfull"] = np.ones((P, P), dtype=np.float32)
    freq = (1.0 / 1024.0) ** np.linspace(0.0, 1.0, 32, dtype=np.float64)
    th = np.arange(T, dtype=np.float64)[:, None] * freq[None, :]
    cosv, sinv = np.cos(th).T, np.sin(th).T
    rotA = np.ones((P, T), dtype=np.float64)
    rotB = np.zeros((P, T), dtype=np.float64)
    rotA[0:32] = cosv
    rotB[0:32] = sinv
    rotA[64:96] = cosv
    rotB[64:96] = -sinv
    c["rotA"] = rotA.astype(np.float32)
    c["rotB"] = rotB.astype(np.float32)
    psw = np.zeros((P, P), dtype=np.float32)
    for j in range(P):
        psw[(j + 64) % P, j] = 1.0
    c["pswap"] = psw
    sl = _slopes(NH)
    r = np.exp(-sl)
    i = np.arange(128, dtype=np.float64)
    c["_r_C"] = [float(v) for v in (r ** 128.0).astype(np.float32)]
    qd = (r[:, None] ** (i[None, :] + 1.0))
    c["qdec"] = np.broadcast_to(qd[:, None, :], (NH, P, 128)).transpose(1, 0, 2) \
        .reshape(P, NH * 128).astype(np.float32).copy()
    c["kdec"] = (r[None, :] ** (127.0 - i[:, None])).astype(np.float32)
    esel = np.zeros((P, NE * 128), dtype=np.float32)
    for e_ in range(NE):
        esel[e_, e_ * 128:(e_ + 1) * 128] = 1.0
    c["esel"] = esel
    diff = i[None, :] - i[:, None]
    dm = np.where(diff >= 0, r[:, None, None] ** np.maximum(diff, 0.0)[None], 0.0)
    c["dmaskT"] = dm.transpose(1, 0, 2).reshape(P, NH * 128).astype(np.float32).copy()
    return c


def prep_inputs(ins):
    d = {}
    w_, j_, p_ = np.meshgrid(np.arange(2), np.arange(8), np.arange(128), indexing="ij")
    perm = (1024 * w_ + 8 * p_ + j_).reshape(-1)
    f32c = lambda a: np.ascontiguousarray(np.asarray(a, dtype=np.float32))
    d["wq"] = np.ascontiguousarray(f32c(ins["L0_attn_Wq"])[perm, :])
    d["wk"] = np.ascontiguousarray(f32c(ins["L0_attn_Wk"])[perm, :])
    d["wv"] = np.ascontiguousarray(f32c(ins["L0_attn_Wv"])[perm, :])
    d["wo"] = f32c(ins["L0_attn_Wo"])
    d["rmsw"] = np.ascontiguousarray(f32c(ins["L0_attn_rms_w"]).reshape(NPL, P).T)
    d["gwT"] = np.ascontiguousarray(f32c(ins["L0_ffn_gate_w"]).T[perm, :])
    gup = np.concatenate([np.asarray(ins["L0_ffn_gate_up_packed"]),
                          np.asarray(ins["L0_ffn_shared_gate_up_packed"])[None]], axis=0)
    dnp = np.concatenate([np.asarray(ins["L0_ffn_down_packed"]),
                          np.asarray(ins["L0_ffn_shared_down_packed"])[None]], axis=0)
    d["gup"] = np.ascontiguousarray(gup.astype(np.int32))
    d["dnp"] = np.ascontiguousarray(dnp.astype(np.int32))
    gus = np.concatenate([f32c(ins["L0_ffn_gate_up_scales"]),
                          f32c(ins["L0_ffn_shared_gate_up_scales"])[None]], axis=0)
    dns = np.concatenate([f32c(ins["L0_ffn_down_scales"]),
                          f32c(ins["L0_ffn_shared_down_scales"])[None]], axis=0)
    g16 = np.arange(P) // 16
    sg = np.empty((NEA, 2, P, 6144), dtype=np.float32)
    for wt in range(2):
        sg[:, wt] = np.repeat(gus[:, 8 * wt + g16, :], 2, axis=2)
    d["sgu"] = np.ascontiguousarray(sg)
    d["sd0"] = np.ascontiguousarray(np.repeat(dns[:, g16, :], 2, axis=2))
    g_t = 8 + (np.arange(P) % 64) // 16
    d["sd1"] = np.ascontiguousarray(np.repeat(dns[:, g_t, :], 2, axis=2))
    return d


class Ctx:
    pass


def inc(ctx, instr, name, n=1):
    instr.then_inc(ctx.sem[name], n)
    ctx.cnt[name] += n
    return ctx.cnt[name]


def inc_k(ctx, instr, name, key, n=1):
    v = inc(ctx, instr, name, n)
    assert ctx.t[key] == v, f"preset mismatch {key}: preset={ctx.t[key]} actual={v}"
    return v


def preset(ctx, name, keys, n=1):
    base = ctx.cnt[name]
    for i, k in enumerate(keys):
        ctx.t[k] = base + (i + 1) * n


def wait(ctx, eng, name, val):
    if val > 0:
        eng.wait_ge(ctx.sem[name], val)


def dvsync(ctx, v, instr):
    t = inc(ctx, instr, "dve")
    wait(ctx, v, "dve", t)
    return t


def emit_decode_pass(ctx, v, src, scale_fp16, dst, q2):
    sb = ctx.sb
    n = src.shape[-1]
    d1, d2 = sb["d1"][:, 0:n], sb["d2"][:, 0:n]
    f2, f3 = sb["f2"][:, 0:n], sb["f3"][:, 0:n]
    if q2 == 0:
        v.tensor_scalar(d1, src, 7, None, Op.bitwise_and)
    else:
        v.tensor_scalar(d1, src, 4 * q2, 7, Op.logical_shift_right, Op.bitwise_and)
    v.tensor_scalar(d2, d1, 512, 14336, Op.mult, Op.add)
    v.tensor_scalar(f2, d2.bitcast(FP16), 2.0, -1.0, Op.mult, Op.add)
    v.tensor_tensor(f3, d2.bitcast(FP16), f2, Op.min)
    v.tensor_scalar(d1, src, 12 - 4 * q2, 0x8000, Op.logical_shift_left,
                    Op.bitwise_and)
    v.tensor_tensor(f2.bitcast(U16), f3.bitcast(U16), d1, Op.bitwise_or)
    return v.tensor_tensor(dst, f2, scale_fp16, Op.mult)


def build_nc():
    nc = bass.Bass("TRN2", target_bir_lowering=False, debug=False,
                   detect_race_conditions=False)
    stack = contextlib.ExitStack()
    ctx = Ctx()
    ctx.nc = nc
    ctx.sem, ctx.cnt, ctx.t = {}, {}, {}
    ctx.r_C = _CTX_RC

    dr = {}
    def DR(name, shape, dt, kind):
        dr[name] = nc.dram_tensor(name, shape, dt, kind=kind).ap()
    scr = "ExternalOutput" if DBG else "Internal"
    DR("x", [T, H], F32, "ExternalInput")
    DR("out", [T, H], F32, "ExternalOutput")
    for n in ["wq", "wk", "wv", "wo"]:
        DR(n, [H, H], F32, "ExternalInput")
    DR("rmsw", [P, NPL], F32, "ExternalInput")
    DR("gwT", [H, NE], F32, "ExternalInput")
    DR("gup", [NEA, 256, 3072], I32, "ExternalInput")
    DR("dnp", [NEA, 192, 2048], I32, "ExternalInput")
    DR("sgu", [NEA, 2, P, 6144], F32, "ExternalInput")
    DR("sd0", [NEA, P, 4096], F32, "ExternalInput")
    DR("sd1", [NEA, P, 4096], F32, "ExternalInput")
    for n, shape in [("identb", [P, P]), ("onesfull", [P, P]),
                     ("rotA", [P, T]), ("rotB", [P, T]), ("pswap", [P, P]),
                     ("qdec", [P, NH * 128]),
                     ("kdec", [P, NH]), ("dmaskT", [P, NH * 128]),
                     ("esel", [P, NE * 128])]:
        DR(n, shape, F32, "ExternalInput")
    DR("act_scr", [NEA, 12, P, T], BF16, scr)
    DR("dgu_scr", [NEA, P, 16, 2048], BF16, "Internal")
    DR("dgt_scr", [NEA, P, 16, 1024], BF16, "Internal")
    DR("ddn_scr", [NEA, 8, P, 4096], BF16, "Internal")
    DR("s_scr", [T, H], BF16, scr)
    DR("wos_scr", [NPL, P, H], BF16, scr)
    DR("at_spill", [P, 16384], BF16, "Internal")
    if DBG:
        DR("xnT_dbg", [P, NPL * T], F32, "ExternalOutput")
        DR("comb_dbg", [P, NPL * NE], F32, "ExternalOutput")
        DR("Y_dbg", [2, P, NPL * 1024], F32, "ExternalOutput")
    ctx.dr = dr

    for n in ["pe", "dve", "act", "gp", "dvp",
              "ld_c1", "ld_c2", "ld_rot", "ld_wo", "ld_x0", "ld_x1", "ld_pk", "ld_sc",
              "ld_wx", "ld_wos0", "ld_wos1", "ld_actin",
              "ld_ex0", "ld_ex1", "ld_dm", "ld_dt", "ld_dn",
              "st_wos", "st_act", "st_s0", "st_s1", "st_dec0", "st_dec1",
              "sto0", "sto1", "st_dbg"]:
        ctx.sem[n] = stack.enter_context(nc.semaphore(n))
        ctx.cnt[n] = 0

    sb = {}
    def S(name, shape, dt):
        sb[name] = stack.enter_context(nc.sbuf_tensor("s_" + name, shape, dt))
    S("AR1", [P, 65536], U8)
    S("AR2", [P, 45568], U8)
    S("AR3", [P, 65536], U8)
    S("identb", [P, P], BF16)
    S("identf", [P, P], F32)
    S("onesfull", [P, P], BF16)
    S("onescolb", [P, 1], BF16)
    S("rotA", [P, T], BF16)
    S("rotB", [P, T], BF16)
    S("pswap", [P, P], BF16)
    S("qdec", [P, NH * 128], BF16)
    S("kdec", [P, NH], F32)
    S("dmaskT", [P, NH * 128], BF16)
    S("rmsw", [P, NPL], F32)
    S("gwTp", [P, NPL * NE], BF16)
    S("combp", [P, NPL * NE], F32)
    S("d1", [P, 1024], U16)
    S("d2", [P, 1024], U16)
    S("f1", [P, 1024], FP16)
    S("f2", [P, 1024], FP16)
    S("f3", [P, 1024], FP16)
    S("lnst", [P, 16], F32)
    S("mx", [P, 8], F32)
    S("exw", [P, NE], F32)
    S("cp4", [P, NE], F32)
    S("cp4b", [P, NE], F32)
    S("mskb", [P, NE], F32)
    S("rstdtm", [P, NPL], F32)
    S("sg0", [P, 512], FP16)
    S("sg1", [P, 512], FP16)
    S("uc0", [P, 512], FP16)
    S("uc1", [P, 512], FP16)
    S("qdh", [P, 128], BF16)
    S("smh", [P, 128], BF16)
    S("smh2", [P, 128], BF16)
    S("kvt", [P, 128], F32)
    S("etmp", [P, 128], F32)
    ctx.sb = sb
    ctx.xnT = sb["AR1"][:].bitcast(BF16)

    ps = {}
    for n in ["pA", "pB", "pC", "pD", "pE", "pF"]:
        ps[n] = stack.enter_context(nc.psum_tensor(n, [P, 512], F32))
    ps["pT1"] = stack.enter_context(nc.psum_tensor("pT1", [P, 1024], BF16))
    ps["pT2"] = stack.enter_context(nc.psum_tensor("pT2", [P, 1024], BF16))
    ctx.ps = ps

    with nc.Block() as blk:
        emit_consts(ctx, blk)
        emit_boot_pre(ctx, blk)
        emit_A(ctx, blk)
        emit_boot_mid(ctx, blk)
        emit_B(ctx, blk)
        emit_boot(ctx, blk)
        emit_C(ctx, blk)
        emit_C2(ctx, blk)
        emit_DE(ctx, blk)
        emit_end(ctx, blk)

    stack.close()
    return ctx, nc


def emit_consts(ctx, blk):
    dr, sb = ctx.dr, ctx.sb

    def f(g):
        for n in ["identb", "onesfull", "pswap", "qdec", "dmaskT"]:
            inc(ctx, g.dma_start(out=sb[n][:], in_=dr[n]), "ld_c1", 16)
        inc(ctx, g.dma_start(out=sb["rotA"][:], in_=dr["esel"]), "ld_c1", 16)
        inc(ctx, g.dma_start(out=sb["onescolb"][:], in_=dr["onesfull"][0:P, 0:1],
                             allow_slow_non_contiguous=True), "ld_c1", 16)
        inc(ctx, g.dma_start(
            out=sb["gwTp"][:].rearrange("p (pl e) -> p pl e", pl=NPL),
            in_=dr["gwT"].rearrange("(pl p) e -> p pl e", pl=NPL, p=P)), "ld_c1", 16)
    blk.gpsimd(f)

    def f(s):
        inc(ctx, s.dma_start(out=sb["kdec"][:], in_=dr["kdec"]), "ld_c2", 16)
        inc(ctx, s.dma_start(out=sb["rmsw"][:], in_=dr["rmsw"]), "ld_c2", 16)
        inc(ctx, s.dma_start(out=sb["identf"][:], in_=dr["identb"]), "ld_c2", 16)
    blk.sync(f)
    ctx.t["c_ldg"] = ctx.cnt["ld_c1"]
    ctx.t["c_ldw"] = ctx.cnt["ld_c2"]


def emit_A(ctx, blk):
    dr, sb, ps = ctx.dr, ctx.sb, ctx.ps
    ar3 = sb["AR3"][:]
    xt = [ar3[:, 0:8192].bitcast(F32), ar3[:, 8192:16384].bitcast(F32)]
    sqb = ar3[:, 16384:24576].bitcast(F32)
    xnb = ar3[:, 24576:28672].bitcast(BF16)
    woin = ar3[:, 28672:36864].bitcast(F32)
    wob = ar3[:, 36864:40960].bitcast(BF16)

    preset(ctx, "dve", [("wosc", k) for k in range(NPL)])
    preset(ctx, "st_wos", [("wost", k) for k in range(NPL)], n=16)

    def f(s):
        wait(ctx, s, "ld_c2", ctx.t["c_ldw"])
        for k in range(NPL):
            if k >= 1:
                wait(ctx, s, "dve", ctx.t[("wosc", k - 1)])
            inc(ctx, s.dma_start(out=woin, in_=dr["wo"][k * P:(k + 1) * P, :]), "ld_wo", 16)
            ctx.t[("wol", k)] = ctx.cnt["ld_wo"]
    blk.sync(f)

    def f(v):
        for k in range(NPL):
            wait(ctx, v, "ld_wo", ctx.t[("wol", k)])
            if k >= 1:
                wait(ctx, v, "st_wos", ctx.t[("wost", k - 1)])
            inc_k(ctx, v.tensor_scalar(wob, woin, sb["rmsw"][:, k:k + 1], None, Op.mult),
                  "dve", ("wosc", k))
    blk.vector(f)

    def f(s):
        for k in range(NPL):
            wait(ctx, s, "dve", ctx.t[("wosc", k)])
            v = inc(ctx, s.dma_start(out=dr["wos_scr"][k], in_=wob), "st_wos", 16)
            assert v == ctx.t[("wost", k)]
    blk.scalar(f)

    for tt in range(16):
        def f(s, tt=tt):
            if tt >= 2:
                wait(ctx, s, "dve", ctx.t[("xnw", tt - 2)])
            sem = "ld_x0" if tt % 2 == 0 else "ld_x1"
            inc(ctx, s.dma_start(out=xt[tt % 2], in_=dr["x"][tt * P:(tt + 1) * P, :]), sem, 16)
            ctx.t[("xl", tt)] = ctx.cnt[sem]
        blk.sync(f)

        def f(v, tt=tt):
            wait(ctx, v, "ld_x0" if tt % 2 == 0 else "ld_x1", ctx.t[("xl", tt)])
            dvsync(ctx, v, v.reduce_sum(sb["lnst"][:, 0:1], xt[tt % 2], axis=AX.X))
            ctx.t[("mean", tt)] = dvsync(ctx, v, v.tensor_scalar(
                sb["lnst"][:, 1:2], sb["lnst"][:, 0:1], 1.0 / H, None, Op.mult))
        blk.vector(f)

        def f(sc, tt=tt):
            wait(ctx, sc, "ld_x0" if tt % 2 == 0 else "ld_x1", ctx.t[("xl", tt)])
            if tt >= 1:
                wait(ctx, sc, "dve", ctx.t[("var", tt - 1)])
            ctx.t[("sq", tt)] = inc(ctx, sc.activation(
                sqb, xt[tt % 2], AF.Square, accum_out=sb["lnst"][:, 2:3]), "act")
        blk.scalar(f)

        def f(v, tt=tt):
            for p_ in ctx.boot_passes[tt * 5:tt * 5 + 2]:
                p_(v)
            wait(ctx, v, "act", ctx.t[("sq", tt)])
            v.tensor_scalar(sb["lnst"][:, 3:4], sb["lnst"][:, 2:3], 1.0 / H, 1e-5,
                            Op.mult, Op.add)
            dvsync(ctx, v, v.tensor_tensor(
                sb["lnst"][:, 4:5], sb["lnst"][:, 1:2], sb["lnst"][:, 1:2], Op.mult))
            ctx.t[("var", tt)] = inc(ctx, v.tensor_tensor(
                sb["lnst"][:, 5:6], sb["lnst"][:, 3:4], sb["lnst"][:, 4:5], Op.subtract), "dve")
        blk.vector(f)

        def f(sc, tt=tt):
            wait(ctx, sc, "dve", ctx.t[("var", tt)])
            ctx.t[("std", tt)] = inc(ctx, sc.activation(
                sb["lnst"][:, 6:7], sb["lnst"][:, 5:6], AF.Sqrt), "act")
        blk.scalar(f)

        def f(v, tt=tt):
            for p_ in ctx.boot_passes[tt * 5 + 2:tt * 5 + 4]:
                p_(v)
            wait(ctx, v, "act", ctx.t[("std", tt)])
            if tt >= 1:
                wait(ctx, v, "pe", ctx.t[("tr", tt - 1, NPL - 1)])
            dvsync(ctx, v, v.reciprocal(sb["lnst"][:, 7:8], sb["lnst"][:, 6:7]))
            ctx.t[("xnw", tt)] = inc(ctx, v.tensor_scalar(
                xnb, xt[tt % 2], sb["lnst"][:, 1:2], sb["lnst"][:, 7:8],
                Op.subtract, Op.mult), "dve")
        blk.vector(f)

        preset(ctx, "dve", [("trev", tt, ti) for ti in range(NPL)])

        def f(te, tt=tt):
            if tt == 0:
                wait(ctx, te, "ld_c1", ctx.t["c_ldg"])
            wait(ctx, te, "dve", ctx.t[("xnw", tt)])
            for ti in range(NPL):
                gi = tt * NPL + ti
                if gi >= 2:
                    ptt, pti = divmod(gi - 2, NPL)
                    wait(ctx, te, "dve", ctx.t[("trev", ptt, pti)])
                w, j = ti // 8, ti % 8
                st = 1024 * w + j
                pT = ps["pT1"] if ti % 2 == 0 else ps["pT2"]
                ctx.t[("tr", tt, ti)] = inc(
                    ctx, te.transpose(pT[:, 0:128], xnb[:, st:st + 1017:8], sb["identb"][:]), "pe")
        blk.tensor(f)

        def f(v, tt=tt):
            for ti in range(NPL):
                wait(ctx, v, "pe", ctx.t[("tr", tt, ti)])
                pT = ps["pT1"] if ti % 2 == 0 else ps["pT2"]
                dst = ctx.xnT[:, ti * T + tt * P: ti * T + tt * P + P]
                inc_k(ctx, v.tensor_copy(out=dst, in_=pT[:, 0:128]), "dve", ("trev", tt, ti))
            for p_ in ctx.boot_passes[tt * 5 + 4:tt * 5 + 5]:
                p_(v)
        blk.vector(f)
    ctx.t["xnT_done"] = ctx.cnt["dve"]

    if DBG:
        def f(g):
            wait(ctx, g, "dve", ctx.t["xnT_done"])
            inc(ctx, g.dma_start(out=ctx.dr["xnT_dbg"], in_=ctx.xnT), "st_dbg", 16)
        blk.gpsimd(f)


def emit_B(ctx, blk):
    sb, ps = ctx.sb, ctx.ps

    for tt in range(16):
        def f(te, tt=tt):
            if tt == 0:
                wait(ctx, te, "dve", ctx.t["xnT_done"])
                wait(ctx, te, "ld_c1", ctx.t["c_ldg"])
            pb = ps["pA"] if tt % 2 == 0 else ps["pB"]
            if tt >= 2:
                wait(ctx, te, "act", ctx.t[("ex", tt - 2)])
            for ti in range(NPL):
                mm = te.matmul(pb[:, 0:NE],
                               ctx.xnT[:, ti * T + tt * P: ti * T + tt * P + P],
                               sb["gwTp"][:, ti * NE:(ti + 1) * NE],
                               start=(ti == 0), stop=(ti == NPL - 1))
            ctx.t[("lg", tt)] = inc(ctx, mm, "pe")
        blk.tensor(f)

        def f(v, tt=tt):
            pb = ps["pA"] if tt % 2 == 0 else ps["pB"]
            wait(ctx, v, "pe", ctx.t[("lg", tt)])
            dvsync(ctx, v, v.reduce_max(sb["mx"][:, 0:1], pb[:, 0:NE], axis=AX.X))
            ctx.t[("nmx", tt)] = inc(ctx, v.tensor_scalar(
                sb["mx"][:, 1:2], sb["mx"][:, 0:1], -1.0, None, Op.mult), "dve")
        blk.vector(f)

        def f(sc, tt=tt):
            pb = ps["pA"] if tt % 2 == 0 else ps["pB"]
            wait(ctx, sc, "dve", ctx.t[("nmx", tt)])
            ctx.t[("ex", tt)] = inc(ctx, sc.activation(
                sb["exw"][:], pb[:, 0:NE], AF.Exp, bias=sb["mx"][:, 1:2]), "act")
        blk.scalar(f)

        def f(v, tt=tt):
            wait(ctx, v, "act", ctx.t[("ex", tt)])
            ex, mx = sb["exw"][:], sb["mx"]
            cps = [sb["cp4"][:], sb["cp4b"][:]]
            dvsync(ctx, v, v.tensor_copy(out=cps[0], in_=ex))
            for r_ in range(4):
                dvsync(ctx, v, v.reduce_max(mx[:, 2 + r_:3 + r_], cps[r_ % 2], axis=AX.X))
                if r_ < 3:
                    dvsync(ctx, v, v.tensor_scalar(sb["mskb"][:], cps[r_ % 2],
                                                   mx[:, 2 + r_:3 + r_], None, Op.is_ge))
                    dvsync(ctx, v, v.scalar_tensor_tensor(cps[(r_ + 1) % 2], sb["mskb"][:],
                                                          -2.0, cps[r_ % 2], Op.mult, Op.add))
            dvsync(ctx, v, v.tensor_tensor(sb["lnst"][:, 8:9], mx[:, 2:3], mx[:, 3:4], Op.add))
            dvsync(ctx, v, v.tensor_tensor(sb["lnst"][:, 9:10], mx[:, 4:5], mx[:, 5:6], Op.add))
            dvsync(ctx, v, v.tensor_tensor(sb["lnst"][:, 10:11], sb["lnst"][:, 8:9],
                                           sb["lnst"][:, 9:10], Op.add))
            dvsync(ctx, v, v.reciprocal(sb["lnst"][:, 11:12], sb["lnst"][:, 10:11]))
            dvsync(ctx, v, v.tensor_scalar(sb["mskb"][:], ex, mx[:, 5:6], None, Op.is_ge))
            dvsync(ctx, v, v.tensor_tensor(sb["cp4"][:], sb["mskb"][:], ex, Op.mult))
            ctx.t[("cmb", tt)] = inc(ctx, v.tensor_scalar(
                sb["combp"][:, tt * NE:(tt + 1) * NE], sb["cp4"][:],
                sb["lnst"][:, 11:12], None, Op.mult), "dve")
        blk.vector(f)
    ctx.t["comb_done"] = ctx.cnt["dve"]

    if DBG:
        def f(g):
            wait(ctx, g, "dve", ctx.t["comb_done"])
            inc(ctx, g.dma_start(out=ctx.dr["comb_dbg"], in_=sb["combp"][:]), "st_dbg", 16)
        blk.gpsimd(f)


PROD_NPASS = 80


def producer_passes(ctx, e):
    dr, sb = ctx.dr, ctx.sb
    ar2 = sb["AR2"][:]
    pkgu16 = ar2[:, 0:24576].bitcast(U16)
    pk0_16 = ar2[:, 0:8192].bitcast(U16)
    pk1_16 = ar2[:, 8192:16384].bitcast(U16)
    scbuf = ar2[:, 32768:40960].bitcast(FP16)
    stg = [ar2[:, 41472:43520].bitcast(BF16), ar2[:, 43520:45568].bitcast(BF16)]

    descs = []
    for wt in range(2):
        for q2 in range(4):
            for gu in range(2):
                for sub in range(2):
                    soff = wt * 6144 + gu * 3072 + sub * 1024
                    descs.append((wt, pkgu16[:, soff:soff + 1024],
                                  scbuf[:, gu * 2048 + sub * 1024:
                                        gu * 2048 + (sub + 1) * 1024],
                                  dr["dgu_scr"][e, :, (wt * 4 + q2) * 2 + gu,
                                                sub * 1024:(sub + 1) * 1024],
                                  q2, False))
    for wt in range(2):
        for q2 in range(4):
            for gu in range(2):
                soff = wt * 6144 + 2048 + gu * 3072
                descs.append((2 + wt, pkgu16[:, soff:soff + 1024],
                              scbuf[:, gu * 2048:gu * 2048 + 1024],
                              dr["dgt_scr"][e, :, (wt * 4 + q2) * 2 + gu, :],
                              q2, True))
    for hh in range(2):
        for wt2 in range(2):
            for q2 in range(4):
                for sub in range(2):
                    pksrc = pk0_16 if wt2 == 0 else pk1_16
                    descs.append((4 + hh,
                                  pksrc[:, hh * 2048 + sub * 1024:
                                        hh * 2048 + (sub + 1) * 1024],
                                  scbuf[:, wt2 * 2048 + sub * 1024:
                                        wt2 * 2048 + (sub + 1) * 1024],
                                  dr["ddn_scr"][e, wt2 * 4 + q2][
                                      :, hh * 2048 + sub * 1024:
                                      hh * 2048 + (sub + 1) * 1024],
                                  q2, False))
    assert len(descs) == PROD_NPASS

    if not hasattr(ctx, "plan"):
        ctx.plan = {"dvp": 0, "sd0": 0, "sd1": 0, "ld_pk": 0}
    ctx.t[("pkl", e)] = ctx.plan["ld_pk"] + 32
    ctx.t[("pkdl", e)] = ctx.plan["ld_pk"] + 80
    ctx.plan["ld_pk"] += 80
    dvb_ = ctx.plan["dvp"]
    for i in range(PROD_NPASS):
        ctx.t[("dvp", e, i)] = dvb_ + i + 1
    ctx.plan["dvp"] = dvb_ + PROD_NPASS
    b0, b1 = ctx.plan["sd0"], ctx.plan["sd1"]
    for i in range(PROD_NPASS):
        p = i % 2
        nprev = sum(1 for j in range(i + 1) if j % 2 == p)
        ctx.t[("sdec", e, i)] = (b0 if p == 0 else b1) + nprev * 16
    ctx.plan["sd0"] = b0 + (PROD_NPASS // 2) * 16
    ctx.plan["sd1"] = b1 + (PROD_NPASS // 2) * 16
    ctx.t[("pgu_done", e)] = ctx.t[("dvp", e, 47)]
    ctx.t[("pdone", e)] = ctx.t[("dvp", e, PROD_NPASS - 1)]

    prev_stg = (ctx.t.get(("stgfree", 0)), ctx.t.get(("stgfree", 1)))

    def mk_pass(i, d):
        grp, srcap, scap, dstdram, q2, tailperm = d

        def closure(v):
            if i == 0 or descs[i - 1][0] != grp:
                wait(ctx, v, "ld_sc", ctx.t[("pscl", e, grp)])
                if grp == 0:
                    wait(ctx, v, "ld_pk", ctx.t[("pkl", e)])
                if grp == 4:
                    wait(ctx, v, "ld_pk", ctx.t[("pkdl", e)])
                    wait(ctx, v, "gp", ctx.t[("pkz", e)])
            p = i % 2
            if i >= 2:
                wait(ctx, v, "st_dec0" if p == 0 else "st_dec1",
                     ctx.t[("sdec", e, i - 2)])
            elif prev_stg[p] is not None:
                wait(ctx, v, "st_dec0" if p == 0 else "st_dec1", prev_stg[p])
            dstap = stg[p][:, 0:1024]
            if tailperm:
                dstap = dstap.rearrange("p (jj m y) -> p m jj y", jj=2, m=64, y=8)
            inc_k(ctx, emit_decode_pass(ctx, v, srcap, scap, dstap, q2),
                  "dvp", ("dvp", e, i))
        return closure

    def mk_store(i, d):
        dstdram = d[3]

        def closure(s):
            wait(ctx, s, "dvp", ctx.t[("dvp", e, i)])
            p = i % 2
            sem = "st_dec0" if p == 0 else "st_dec1"
            v = inc(ctx, s.dma_start(out=dstdram, in_=stg[p][:, 0:1024]), sem, 16)
            assert v == ctx.t[("sdec", e, i)], (v, ctx.t[("sdec", e, i)])
        return closure

    passes = [mk_pass(i, d) for i, d in enumerate(descs)]
    stores = [mk_store(i, d) for i, d in enumerate(descs)]
    ctx.t[("stgfree", 0)] = ctx.t[("sdec", e, PROD_NPASS - 2)]
    ctx.t[("stgfree", 1)] = ctx.t[("sdec", e, PROD_NPASS - 1)]
    return passes, stores


def emit_producer_gp(ctx, blk, e, stores):

    emit_producer_gp_gu(ctx, blk, e, stores, ctx.t.get(("scfree2",)))
    ctx.t[("scfree2",)] = ctx.t[("dvp", e, 47)]
    emit_producer_gp_dn(ctx, blk, e, stores, ctx.t[("scfree2",)])
    ctx.t[("scfree2",)] = ctx.t[("dvp", e, PROD_NPASS - 1)]


def emit_producer_gp_gu(ctx, blk, e, stores, prev_dvp):
    dr, sb = ctx.dr, ctx.sb
    ar2 = sb["AR2"][:]
    scbuf = ar2[:, 32768:40960].bitcast(FP16)
    grp_begin = {0: 0, 1: 16, 2: 32, 3: 40, 4: 48}

    pkgu = ar2[:, 0:24576].bitcast(I32)

    def f(g, e=e):
        for grp in range(4):
            if grp > 0:
                for i in range(grp_begin[grp - 1], grp_begin[grp]):
                    stores[i](g)
                wait(ctx, g, "dvp", ctx.t[("dvp", e, {1: 15, 2: 31, 3: 39}[grp])])
            else:
                if prev_dvp is not None:
                    wait(ctx, g, "dvp", prev_dvp)
                inc(ctx, g.dma_start(out=pkgu[:, 0:3072],
                                     in_=dr["gup"][e, 0:128, :]), "ld_pk", 16)
                v = inc(ctx, g.dma_start(out=pkgu[:, 3072:6144],
                                         in_=dr["gup"][e, 128:256, :]), "ld_pk", 16)
                assert v == ctx.t[("pkl", e)], (v, ctx.t[("pkl", e)])
            if grp < 2:
                for gu in range(2):
                    inc(ctx, g.dma_start(out=scbuf[:, gu * 2048:gu * 2048 + 2048],
                                         in_=dr["sgu"][e, grp][:, gu * 3072:gu * 3072 + 2048]),
                        "ld_sc", 16)
            else:
                for gu in range(2):
                    inc(ctx, g.dma_start(out=scbuf[:, gu * 2048:gu * 2048 + 1024],
                                         in_=dr["sgu"][e, grp - 2][:, 2048 + gu * 3072:
                                                                   2048 + gu * 3072 + 1024]),
                        "ld_sc", 16)
            ctx.t[("pscl", e, grp)] = ctx.cnt["ld_sc"]
        for i in range(grp_begin[3], grp_begin[4]):
            stores[i](g)
    blk.gpsimd(f)


def emit_producer_gp_dn(ctx, blk, e, stores, prev_dvp):
    dr, sb = ctx.dr, ctx.sb
    ar2 = sb["AR2"][:]
    pk1_16 = ar2[:, 8192:16384].bitcast(U16)
    scbuf = ar2[:, 32768:40960].bitcast(FP16)

    pkd0 = ar2[:, 0:8192].bitcast(I32)
    pkd1 = ar2[:, 8192:16384].bitcast(I32)

    def f(g, e=e):
        for grp in (4, 5):
            if grp == 4:
                wait(ctx, g, "dvp", prev_dvp)
                inc(ctx, g.dma_start(out=pkd0[:], in_=dr["dnp"][e, 0:128, :]), "ld_pk", 16)
                inc(ctx, g.dma_start(out=pkd1[0:64, 0:2048],
                                     in_=dr["dnp"][e, 128:192, :]), "ld_pk", 16)
                src16 = dr["dnp"][e, 128:192, :].bitcast(U16)
                v = inc(ctx, g.dma_start(out=pk1_16[64:128, 0:4095],
                                         in_=src16[:, 1:4096]), "ld_pk", 16)
                assert v == ctx.t[("pkdl", e)], (v, ctx.t[("pkdl", e)])
            else:
                for i in range(48, 64):
                    stores[i](g)
                wait(ctx, g, "dvp", ctx.t[("dvp", e, 63)])
            hh = grp - 4
            inc(ctx, g.dma_start(out=scbuf[:, 0:2048],
                                 in_=dr["sd0"][e][:, hh * 2048:(hh + 1) * 2048]),
                "ld_sc", 16)
            inc(ctx, g.dma_start(out=scbuf[:, 2048:4096],
                                 in_=dr["sd1"][e][:, hh * 2048:(hh + 1) * 2048]),
                "ld_sc", 16)
            if hh == 0:
                ctx.t[("pkz", e)] = inc(ctx, g.memset(pk1_16[64:128, 4095:4096], 0), "gp")
            ctx.t[("pscl", e, grp)] = ctx.cnt["ld_sc"]
        for i in range(64, PROD_NPASS):
            stores[i](g)
    blk.gpsimd(f)


def emit_crep(ctx, blk, e):
    sb, ps = ctx.sb, ctx.ps
    ar2 = sb["AR2"][:]
    crep = ar2[:, 24576:28672].bitcast(FP16)
    esel = sb["rotA"][:]
    combT = sb["rotB"][:]
    preset(ctx, "pe", [("bct", e, q) for q in range(4)])
    preset(ctx, "dve", [("bcev4", e, q) for q in range(4)])

    cps = [ps["pE"][:, 0:512], ps["pF"][:, 0:512],
           ps["pT1"][:, 0:1024].bitcast(F32), ps["pT2"][:, 0:1024].bitcast(F32)]

    def f(te, e=e):
        if e == 0:
            wait(ctx, te, "dve", ctx.t["combT_done"])
            wait(ctx, te, "ld_c1", ctx.t["c_ldg"])
        for q in range(4):
            gi = e * 4 + q
            if gi >= 4:
                pe_, pq_ = divmod(gi - 4, 4)
                wait(ctx, te, "dve", ctx.t[("bcev4", pe_, pq_)])
            inc_k(ctx, te.matmul(cps[q],
                                 esel[0:16, e * 128:(e + 1) * 128],
                                 combT[0:16, q * 512:(q + 1) * 512],
                                 start=True, stop=True),
                  "pe", ("bct", e, q))
    blk.tensor(f)

    def f(v, e=e):
        for q in range(4):
            wait(ctx, v, "pe", ctx.t[("bct", e, q)])
            inc_k(ctx, v.tensor_copy(out=crep[:, q * 512:(q + 1) * 512],
                                     in_=cps[q]),
                  "dve", ("bcev4", e, q))
    blk.vector(f)
    ctx.t[("bcev", e, 15)] = ctx.t[("bcev4", e, 3)]
    ctx.t[("crep_mm_last",)] = ctx.t[("bct", e, 3)]


def emit_boot_pre(ctx, blk):
    passes, stores = producer_passes(ctx, 0)
    emit_producer_gp(ctx, blk, 0, stores)
    ctx.boot_passes = passes
    assert len(ctx.boot_passes) == 80


def emit_boot_mid(ctx, blk):
    pass


def emit_boot(ctx, blk):
    sb, ps = ctx.sb, ctx.ps
    combT = sb["rotB"][:]
    preset(ctx, "pe", [("cbt", tt) for tt in range(16)])
    preset(ctx, "dve", [("cbte", tt) for tt in range(16)])

    def f(te):
        wait(ctx, te, "dve", ctx.t["comb_done"])
        for tt in range(16):
            if tt >= 2:
                wait(ctx, te, "dve", ctx.t[("cbte", tt - 2)])
            pT = ps["pE"] if tt % 2 == 0 else ps["pF"]
            inc_k(ctx, te.transpose(pT[0:16, 0:128],
                                    sb["combp"][:, tt * NE:(tt + 1) * NE],
                                    sb["identf"][:]),
                  "pe", ("cbt", tt))
    blk.tensor(f)

    def f(v):
        for tt in range(16):
            wait(ctx, v, "pe", ctx.t[("cbt", tt)])
            pT = ps["pE"] if tt % 2 == 0 else ps["pF"]
            inc_k(ctx, v.tensor_copy(out=combT[0:16, tt * 128:(tt + 1) * 128],
                                     in_=pT[0:16, 0:128]),
                  "dve", ("cbte", tt))
    blk.vector(f)
    ctx.t["combT_done"] = ctx.cnt["dve"]
    emit_crep(ctx, blk, 0)


def emit_C(ctx, blk):
    dr, sb, ps = ctx.dr, ctx.sb, ctx.ps
    ar2 = sb["AR2"][:]
    crep = ar2[:, 24576:28672].bitcast(FP16)
    actst = ar2[:, 28672:32768].bitcast(BF16)
    bcb = [ar2[:, 40960:41216].bitcast(BF16), ar2[:, 41216:41472].bitcast(BF16)]
    dec = sb["AR3"][:].bitcast(BF16)

    def dchunk(wt, q2, gu, tail):
        idx = (wt * 4 + q2) * 2 + gu
        return dec[:, idx * 1024:(idx + 1) * 1024] if tail else \
            dec[:, idx * 2048:(idx + 1) * 2048]

    GU_PS = [(ps["pA"], ps["pB"]), (ps["pC"], ps["pD"])]
    ctx.t[("gu_ev_par", 0)] = 0
    ctx.t[("gu_ev_par", 1)] = 0
    ctx.t[("actst_par", 0)] = 0

    for e in range(NEA):
        if e == 0:
            p1, s1 = producer_passes(ctx, 1)
            emit_producer_gp(ctx, blk, 1, s1)
            p2, s2 = producer_passes(ctx, 2)
            emit_producer_gp(ctx, blk, 2, s2)
            prod_passes = p1 + p2
        elif e + 2 < NEA:
            prod_passes, prod_stores = producer_passes(ctx, e + 2)
            emit_producer_gp(ctx, blk, e + 2, prod_stores)
        else:
            prod_passes, prod_stores = [], []
        n_main_p = (len(prod_passes) * 2) // 3

        def f(s, e=e):
            if e == 0:
                wait(ctx, s, "st_dec0", ctx.t[("sdec", 0, 30)])
                wait(ctx, s, "st_dec1", ctx.t[("sdec", 0, 31)])
                wait(ctx, s, "dve", ctx.t[("xnw", 15)])
                wait(ctx, s, "pe", ctx.t[("tr", 15, NPL - 1)])
                wait(ctx, s, "st_wos", ctx.t[("wost", NPL - 1)])
                wait(ctx, s, "act", ctx.t[("sq", 15)])
                inc(ctx, s.dma_start(out=dec[:, 0:32768],
                                     in_=dr["dgu_scr"][0].rearrange("p c n -> p (c n)")),
                    "ld_dm", 16)
                ctx.t[("ldm", 0)] = ctx.cnt["ld_dm"]
        blk.sync(f)

        if 0 < e < NEA - 1:
            emit_crep(ctx, blk, e)

        pass_cursor = [0]

        for cg in range(2):
            tail = cg == 1
            pts = list(range(4 if tail else 8))
            nq = len(pts) * 4

            preset(ctx, "dve", [("gu_ev", e, cg, q) for q in range(nq)])

            def f(te, e=e, cg=cg, tail=tail, pts=pts):
                wait(ctx, te, "ld_dm" if not tail else "ld_dt",
                     ctx.t[("ldm", e) if not tail else ("ldt", e)])
                if e == 0 and cg == 0:
                    wait(ctx, te, "act", ctx.t[("ex", 15)])

                for pi, pt in enumerate(pts):
                    for tc in range(4):
                        q = pi * 4 + tc
                        par = q % 2
                        psG, psU = GU_PS[par]
                        if q >= 2:
                            wait(ctx, te, "dve", ctx.t[("gu_ev", e, cg, q - 2)])
                        else:
                            wait(ctx, te, "dve", ctx.t[("gu_ev_par", par)])
                        for gu, pp in ((0, psG), (1, psU)):
                            kk = 0
                            for w in range(2):
                                for hlfH in range(2):
                                    for q2 in range(4):
                                        j = 4 * hlfH + q2
                                        ti = w * 8 + j
                                        ch = dchunk(w, q2, gu, tail)
                                        if not tail:
                                            lhsT = ch[:, 2 * pt + hlfH:
                                                      2 * pt + hlfH + 16 * 127 + 1:16]
                                        else:
                                            lhsT = ch[:, 2 * pt + hlfH:
                                                      2 * pt + hlfH + 8 * 127 + 1:8]
                                        mm = te.matmul(pp[:, 0:512], lhsT,
                                                       ctx.xnT[:, ti * T + tc * 512:
                                                               ti * T + tc * 512 + 512],
                                                       start=(kk == 0), stop=(kk == 15))
                                        kk += 1
                            ctx.t[("gu_mm", e, cg, q, gu)] = inc(ctx, mm, "pe")
                ctx.t[("mmm_last" if not tail else "mmt_last", e)] = ctx.cnt["pe"]
            blk.tensor(f)

            stbase = ctx.cnt["st_act"]
            dvb = ctx.cnt["dve"]
            prev_actst_par = ctx.t[("actst_par", 0)]
            for i_, pt in enumerate(pts):
                ctx.t[("actst", e, cg, pt)] = stbase + (i_ + 1) * 16
                ctx.t[("actw_pred", e, cg, pt)] = dvb + (i_ + 1) * 4

            def f(sc, e=e, cg=cg, tail=tail, pts=pts):
                for pi, pt in enumerate(pts):
                    for tc in range(4):
                        q = pi * 4 + tc
                        par = q % 2
                        psG, _ = GU_PS[par]
                        wait(ctx, sc, "pe", ctx.t[("gu_mm", e, cg, q, 0)])
                        sgb = sb["sg0"] if par == 0 else sb["sg1"]
                        ctx.t[("sg", e, cg, q)] = inc(
                            ctx, sc.activation(sgb[:], psG[:, 0:512], AF.Silu), "act")
                    pidx = (8 + pt) if tail else pt
                    wait(ctx, sc, "dve", ctx.t[("actw_pred", e, cg, pt)])
                    v = inc(ctx, sc.dma_start(out=dr["act_scr"][e, pidx], in_=actst[:]),
                            "st_act", 16)
                    assert v == ctx.t[("actst", e, cg, pt)]
                    ctx.t[("actst_par", 0)] = v
            blk.scalar(f)

            if not tail:
                def f(s, e=e):
                    wait(ctx, s, "pe", ctx.t[("mmm_last", e)])
                    wait(ctx, s, "st_dec0", ctx.t[("sdec", e, 46)])
                    wait(ctx, s, "st_dec1", ctx.t[("sdec", e, 47)])
                    inc(ctx, s.dma_start(out=dec[:, 0:16384],
                                         in_=dr["dgt_scr"][e].rearrange("p c n -> p (c n)")),
                        "ld_dt", 16)
                    ctx.t[("ldt", e)] = ctx.cnt["ld_dt"]
                    if e + 1 < NEA:
                        wait(ctx, s, "st_dec0", ctx.t[("sdec", e + 1, 30)])
                        wait(ctx, s, "st_dec1", ctx.t[("sdec", e + 1, 31)])
                        inc(ctx, s.dma_start(
                            out=dec[:, 16384:32768],
                            in_=dr["dgu_scr"][e + 1, :, 8:16, :].rearrange(
                                "p c n -> p (c n)")), "ld_dm", 16)
                blk.sync(f)

            def f(v, e=e, cg=cg, tail=tail, pts=pts, prev_actst_par=prev_actst_par,
                  prod_passes=prod_passes, pass_cursor=pass_cursor, n_main_p=n_main_p,
                  nq=nq):
                ntot = len(prod_passes)
                tgt_end = n_main_p if not tail else ntot
                tgt_beg = 0 if not tail else n_main_p
                if e < NEA - 1:
                    wait(ctx, v, "dve", ctx.t[("bcev", e, 15)])
                for pi, pt in enumerate(pts):
                    for tc in range(4):
                        q = pi * 4 + tc
                        par = q % 2
                        psG, psU = GU_PS[par]
                        wait(ctx, v, "pe", ctx.t[("gu_mm", e, cg, q, 1)])
                        wait(ctx, v, "act", ctx.t[("sg", e, cg, q)])
                        if tc == 0:
                            if pt >= 1:
                                wait(ctx, v, "st_act", ctx.t[("actst", e, cg, pt - 1)])
                            else:
                                wait(ctx, v, "st_act", prev_actst_par)
                        sgb = sb["sg0"] if par == 0 else sb["sg1"]
                        ucb = sb["uc0"] if par == 0 else sb["uc1"]
                        if e < NEA - 1:
                            v.tensor_tensor(ucb[:], psU[:, 0:512],
                                            crep[:, tc * 512:(tc + 1) * 512], Op.mult)
                        else:
                            v.tensor_copy(out=ucb[:], in_=psU[:, 0:512])
                        inc_k(ctx, v.tensor_tensor(actst[:, tc * 512:(tc + 1) * 512],
                                                   sgb[:], ucb[:], Op.mult),
                              "dve", ("gu_ev", e, cg, q))
                        want = tgt_beg + ((q + 1) * (tgt_end - tgt_beg)) // nq
                        while pass_cursor[0] < want:
                            prod_passes[pass_cursor[0]](v)
                            pass_cursor[0] += 1
                    ctx.t[("actw", e, cg, pt)] = ctx.cnt["dve"]
                    assert ctx.t[("actw", e, cg, pt)] == ctx.t[("actw_pred", e, cg, pt)]
                ctx.t[("gu_ev_par", 0)] = ctx.t[("gu_ev", e, cg, nq - 2)]
                ctx.t[("gu_ev_par", 1)] = ctx.t[("gu_ev", e, cg, nq - 1)]
            blk.vector(f)

            if tail:
                def f(s, e=e):
                    if e + 1 < NEA:
                        wait(ctx, s, "pe", ctx.t[("mmt_last", e)])
                        inc(ctx, s.dma_start(
                            out=dec[:, 0:16384],
                            in_=dr["dgu_scr"][e + 1, :, 0:8, :].rearrange(
                                "p c n -> p (c n)")), "ld_dm", 16)
                        ctx.t[("ldm", e + 1)] = ctx.cnt["ld_dm"]
                blk.sync(f)

    ctx.t["C_done_pe"] = ctx.cnt["pe"]
    ctx.t["C_done_sta"] = ctx.cnt["st_act"]
    ctx.t["C_done_dve"] = ctx.cnt["dve"]


def emit_C2(ctx, blk):
    dr, sb, ps = ctx.dr, ctx.sb, ctx.ps
    r_C = ctx.r_C
    ar1, ar2, ar3 = sb["AR1"][:], sb["AR2"][:], sb["AR3"][:]
    attnT = ar2[:, 0:32768].bitcast(BF16)
    wxh = ar3[:, 0:12288].bitcast(BF16)
    qTh = ar3[:, 12288:14336].bitcast(BF16)
    kTh = ar3[:, 14336:16384].bitcast(BF16)
    vTh = ar3[:, 16384:18432].bitcast(BF16)
    vnh = ar3[:, 18432:20480].bitcast(BF16)
    knh = ar3[:, 20480:22528].bitcast(BF16)
    kvpp = [ar3[:, 22528:30720].bitcast(F32), ar3[:, 30720:38912].bitcast(F32)]
    kvb = ar3[:, 38912:43008].bitcast(BF16)
    sqb = [ar3[:, 43008:45056].bitcast(BF16), ar3[:, 45056:47104].bitcast(BF16)]
    stile = [ar3[:, 47104:51200].bitcast(BF16), ar3[:, 51200:55296].bitcast(BF16)]
    rrow = ar3[0:1, 55296:59392].bitcast(F32)
    rrow2 = ar3[0:1, 59392:63488].bitcast(F32)
    qdF = vTh
    wosR = ar1.bitcast(BF16)

    def f(g):
        wait(ctx, g, "pe", ctx.t["C_done_pe"])
        g.memset(kvpp[0][:], 0.0)
        ctx.t["kv0"] = inc(ctx, g.memset(kvb[:], 0.0), "gp")
    blk.gpsimd(f)

    def f(g):
        wait(ctx, g, "pe", ctx.t[("crep_mm_last",)])
        inc(ctx, g.dma_start(out=sb["rotA"][:], in_=dr["rotA"]), "ld_rot", 16)
        inc(ctx, g.dma_start(out=sb["rotB"][:], in_=dr["rotB"]), "ld_rot", 16)
        ctx.t["rot_ld"] = ctx.cnt["ld_rot"]
    blk.gpsimd(f)

    QKV_PS = [(ps["pA"], ps["pB"]), (ps["pC"], ps["pD"])]
    PO = [ps["pE"], ps["pA"]]
    PSC = [ps["pF"], ps["pD"]]
    PKX = [ps["pC"], ps["pB"]]
    ctx.t[("qkv_prev_ev", 0)] = 0
    ctx.t[("qkv_prev_ev", 1)] = 0

    for tau in range(2):
        for h in range(NH):
            hi = tau * NH + h

            def f(g, tau=tau, h=h, hi=hi):
                if hi > 0:
                    wait(ctx, g, "pe", ctx.t[("wx_cons", hi - 1)])
                else:
                    wait(ctx, g, "pe", ctx.t["C_done_pe"])
                for xi, wn in enumerate(["wq", "wk", "wv"]):
                    src = dr[wn].rearrange("(ti p) e -> p ti e", ti=NPL, p=P)[
                        :, :, h * 128:(h + 1) * 128]
                    inc(ctx, g.dma_start(
                        out=wxh[:, xi * 2048:(xi + 1) * 2048].rearrange(
                            "p (ti e) -> p ti e", ti=NPL),
                        in_=src), "ld_wx", 16)
                ctx.t[("wxl", hi)] = ctx.cnt["ld_wx"]
            blk.gpsimd(f)

            for xi, dstb in ((0, qTh), (1, kTh), (2, vTh)):
                par = xi % 2
                p0, p1 = QKV_PS[par]

                def f(te, tau=tau, h=h, hi=hi, xi=xi, p0=p0, p1=p1, par=par):
                    if xi == 0:
                        wait(ctx, te, "ld_wx", ctx.t[("wxl", hi)])
                        if hi == 0:
                            wait(ctx, te, "dve", ctx.t["C_done_dve"])
                            wait(ctx, te, "gp", ctx.t["kv0"])
                        if hi > 0:
                            wait(ctx, te, "dve", ctx.t[("kvd", hi - 1, 7)])
                            wait(ctx, te, "act", ctx.t[("aev", hi - 1, 7)])
                            wait(ctx, te, "dve", ctx.t[("sm", hi - 1, 7)])
                    if xi == 2:
                        wait(ctx, te, "dve", ctx.t[("rot", hi, 0)])
                    else:
                        wait(ctx, te, "dve", ctx.t[("qkv_prev_ev", par)])
                    for nc_ in range(2):
                        pp = (p0, p1)[nc_]
                        for ti in range(NPL):
                            mm = te.matmul(pp[:, 0:512],
                                           wxh[:, xi * 2048 + ti * 128:
                                               xi * 2048 + (ti + 1) * 128],
                                           ctx.xnT[:, ti * T + tau * 1024 + nc_ * 512:
                                                   ti * T + tau * 1024 + nc_ * 512 + 512],
                                           start=(ti == 0), stop=(ti == NPL - 1))
                    ctx.t[("qkvmm", hi, xi)] = inc(ctx, mm, "pe")
                    if xi == 2:
                        ctx.t[("wx_cons", hi)] = ctx.cnt["pe"]
                blk.tensor(f)

                def f(v, tau=tau, h=h, hi=hi, xi=xi, dstb=dstb, p0=p0, p1=p1, par=par):
                    wait(ctx, v, "pe", ctx.t[("qkvmm", hi, xi)])
                    if hi == 0 and xi == 0:
                        wait(ctx, v, "st_act", ctx.t["C_done_sta"])
                        wait(ctx, v, "dvp", ctx.t[("pdone", NEA - 1)])
                    v.tensor_copy(out=dstb[:, 0:512], in_=p0[:, 0:512])
                    ctx.t[("qkvev", hi, xi)] = inc(
                        ctx, v.tensor_copy(out=dstb[:, 512:1024], in_=p1[:, 0:512]), "dve")
                    ctx.t[("qkv_prev_ev", par)] = ctx.t[("qkvev", hi, xi)]
                blk.vector(f)

                if xi < 2:
                    def f(te, tau=tau, hi=hi, xi=xi, dstb=dstb, p0=p0, p1=p1):
                        wait(ctx, te, "dve", ctx.t[("qkvev", hi, xi)])
                        for nc_ in range(2):
                            pp = (p0, p1)[nc_]
                            mm = te.matmul(pp[:, 0:512], sb["pswap"][:],
                                           dstb[:, nc_ * 512:(nc_ + 1) * 512],
                                           start=True, stop=True)
                        ctx.t[("swmm", hi, xi)] = inc(ctx, mm, "pe")
                    blk.tensor(f)

                    def f(v, tau=tau, hi=hi, xi=xi, dstb=dstb, p0=p0, p1=p1, par=par):
                        wait(ctx, v, "pe", ctx.t[("swmm", hi, xi)])
                        if ("rot_waited",) not in ctx.t:
                            wait(ctx, v, "ld_rot", ctx.t["rot_ld"])
                            ctx.t[("rot_waited",)] = 1
                        t0 = tau * 1024
                        tA = sb["f1"][:].bitcast(BF16)
                        tB = sb["f2"][:].bitcast(BF16)
                        v.tensor_tensor(tA, dstb[:], sb["rotA"][:, t0:t0 + 1024], Op.mult)
                        v.tensor_tensor(tB[:, 0:512], p0[:, 0:512],
                                        sb["rotB"][:, t0:t0 + 512], Op.mult)
                        v.tensor_tensor(tB[:, 512:1024], p1[:, 0:512],
                                        sb["rotB"][:, t0 + 512:t0 + 1024], Op.mult)
                        ctx.t[("rot", hi, xi)] = inc(
                            ctx, v.tensor_tensor(dstb[:], tA, tB, Op.add), "dve")
                        ctx.t[("qkv_prev_ev", par)] = ctx.t[("rot", hi, xi)]
                    blk.vector(f)

            preset(ctx, "dve", [("tnev", hi * 16 + c) for c in range(16)])

            def f(te, tau=tau, h=h, hi=hi):
                wait(ctx, te, "dve", ctx.t[("rot", hi, 1)])
                wait(ctx, te, "dve", ctx.t[("qkvev", hi, 2)])
                for c in range(16):
                    gi = hi * 16 + c
                    if gi >= 2:
                        wait(ctx, te, "dve", ctx.t[("tnev", gi - 2)])
                    else:
                        wait(ctx, te, "dve", ctx.t[("bcev", NEA - 2, 15)])
                    pT = ps["pT1"] if c % 2 == 0 else ps["pT2"]
                    srcb = kTh if c < 8 else vTh
                    cc = c % 8
                    key = ("tn", hi, cc) if c < 8 else ("tv", hi, cc)
                    ctx.t[key] = inc(
                        ctx, te.transpose(pT[:, 0:128], srcb[:, cc * 128:(cc + 1) * 128],
                                          sb["identb"][:]), "pe")
            blk.tensor(f)

            def f(v, tau=tau, h=h, hi=hi):
                for c in range(8):
                    wait(ctx, v, "pe", ctx.t[("tn", hi, c)])
                    pT = ps["pT1"] if c % 2 == 0 else ps["pT2"]
                    inc_k(ctx, v.tensor_scalar(knh[:, c * 128:(c + 1) * 128], pT[:, 0:128],
                                               sb["kdec"][:, h:h + 1], None, Op.mult),
                          "dve", ("tnev", hi * 16 + c))
                for c in range(8):
                    wait(ctx, v, "pe", ctx.t[("tv", hi, c)])
                    pT = ps["pT1"] if c % 2 == 0 else ps["pT2"]
                    inc_k(ctx, v.tensor_copy(out=vnh[:, c * 128:(c + 1) * 128],
                                             in_=pT[:, 0:128]),
                          "dve", ("tnev", hi * 16 + 8 + c))
                for c in range(8):
                    last = v.tensor_tensor(qdF[:, c * 128:(c + 1) * 128],
                                           qTh[:, c * 128:(c + 1) * 128],
                                           sb["qdec"][:, h * 128:(h + 1) * 128], Op.mult)
                ctx.t[("qdw", hi)] = inc(ctx, last, "dve")
            blk.vector(f)

            for c in range(8):
                cp = c % 2

                def f(te, tau=tau, h=h, hi=hi, c=c, cp=cp):
                    if c == 0:
                        wait(ctx, te, "dve", ctx.t[("qdw", hi)])
                    if c >= 2:
                        wait(ctx, te, "dve", ctx.t[("sm", hi, c - 2)])
                    elif hi > 0:
                        wait(ctx, te, "dve", ctx.t[("sm", hi - 1, c + 6)])
                    ctx.t[("scm", hi, c)] = inc(ctx, te.matmul(
                        PSC[cp][:, 0:128], kTh[:, c * 128:(c + 1) * 128],
                        qTh[:, c * 128:(c + 1) * 128], start=True, stop=True), "pe")
                blk.tensor(f)

                def f(v, tau=tau, h=h, hi=hi, c=c, cp=cp):
                    wait(ctx, v, "pe", ctx.t[("scm", hi, c)])
                    ctx.t[("sm", hi, c)] = inc(ctx, v.tensor_tensor(
                        sb["smh" if cp == 0 else "smh2"][:], PSC[cp][:, 0:128],
                        sb["dmaskT"][:, h * 128:(h + 1) * 128], Op.mult), "dve")
                blk.vector(f)

                def f(te, tau=tau, h=h, hi=hi, c=c, cp=cp):
                    gc = hi * 8 + c
                    if gc > 0:
                        wait(ctx, te, "dve", ctx.t[("kvd_d", gc - 1)])
                    if c >= 2:
                        wait(ctx, te, "act", ctx.t[("aev", hi, c - 2)])
                    elif hi > 0:
                        wait(ctx, te, "act", ctx.t[("aev", hi - 1, c + 6)])
                    te.matmul(PO[cp][:, 0:128], kvb[:, h * 128:(h + 1) * 128],
                              qdF[:, c * 128:(c + 1) * 128], start=True, stop=False)
                    wait(ctx, te, "dve", ctx.t[("sm", hi, c)])
                    ctx.t[("ao", hi, c)] = inc(ctx, te.matmul(
                        PO[cp][:, 0:128], vnh[:, c * 128:(c + 1) * 128],
                        sb["smh" if cp == 0 else "smh2"][:], start=False, stop=True), "pe")
                    ctx.t[("kvx", hi, c)] = inc(ctx, te.matmul(
                        PKX[cp][:, 0:128], knh[:, c * 128:(c + 1) * 128],
                        vnh[:, c * 128:(c + 1) * 128], start=True, stop=True), "pe")
                blk.tensor(f)

                def f(sc, tau=tau, h=h, hi=hi, c=c, cp=cp):
                    wait(ctx, sc, "pe", ctx.t[("ao", hi, c)])
                    if hi == NH and h == 0 and c == 0 and tau == 1:
                        pass
                    if tau == 1 and h == 0 and c == 0:
                        wait(ctx, sc, "st_s0", ctx.t[("atspill",)])
                    ctx.t[("aev", hi, c)] = inc(ctx, sc.copy(
                        out=attnT[:, h * 1024 + c * 128: h * 1024 + (c + 1) * 128],
                        in_=PO[cp][:, 0:128]), "act")
                blk.scalar(f)

                def f(v, tau=tau, h=h, hi=hi, c=c, cp=cp):
                    gc = hi * 8 + c
                    wait(ctx, v, "pe", ctx.t[("kvx", hi, c)])
                    kv_c = kvpp[gc % 2][:, h * 128:(h + 1) * 128]
                    kv_n = kvpp[(gc + 1) % 2][:, h * 128:(h + 1) * 128]
                    v.scalar_tensor_tensor(kv_n, kv_c, r_C[h], PKX[cp][:, 0:128],
                                           Op.mult, Op.add)
                    ctx.t[("kvd", hi, c)] = inc(ctx, v.tensor_copy(
                        out=kvb[:, h * 128:(h + 1) * 128], in_=kv_n), "dve")
                    ctx.t[("kvd_d", gc)] = ctx.t[("kvd", hi, c)]
                blk.vector(f)

        if tau == 0:
            def f(sc, tau=tau):
                for k in range(NPL):
                    wait(ctx, sc, "act", ctx.t[("aev", tau * NH + k, 7)])
                inc(ctx, sc.dma_start(out=dr["at_spill"], in_=attnT[:]), "st_s0", 16)
                ctx.t[("atspill",)] = ctx.cnt["st_s0"]
            blk.scalar(f)

    def f(s):
        wait(ctx, s, "pe", ctx.t[("wx_cons", 2 * NH - 1)])
        wait(ctx, s, "st_wos", ctx.t[("wost", NPL - 1)])
        inc(ctx, s.dma_start(out=wosR.rearrange("p (k n) -> p k n", k=NPL),
                             in_=dr["wos_scr"].rearrange("k p n -> p k n")), "ld_wos0", 16)
        ctx.t["wosl_all"] = ctx.cnt["ld_wos0"]
    blk.sync(f)

    for tau in (1, 0):
        if tau == 0:
            def f(s):
                wait(ctx, s, "pe", ctx.t[("smm", 1, 7)])
                wait(ctx, s, "st_s0", ctx.t[("atspill",)])
                inc(ctx, s.dma_start(out=attnT[:], in_=dr["at_spill"]), "ld_wos0", 16)
                ctx.t["atload"] = ctx.cnt["ld_wos0"]
            blk.sync(f)

        pebase = ctx.cnt["pe"]
        for k in range(NPL):
            ctx.t[("ssq", tau, k)] = pebase + k + 1

        def f(sc, tau=tau):
            for k in range(NPL):
                if tau == 1:
                    wait(ctx, sc, "act", ctx.t[("aev", tau * NH + k, 7)])
                    wait(ctx, sc, "dve", ctx.t[("kvd", tau * NH + k, 7)])
                elif k == 0:
                    wait(ctx, sc, "ld_wos0", ctx.t["atload"])
                if k >= 2:
                    wait(ctx, sc, "pe", ctx.t[("ssq", tau, k - 2)])
                ctx.t[("sqw", tau, k)] = inc(ctx, sc.activation(
                    sqb[k % 2][:], attnT[:, k * 1024:(k + 1) * 1024], AF.Square), "act")
        blk.scalar(f)

        def f(te, tau=tau):
            pv = ctx.t.get(("rstdev", 1 if tau == 0 else -1))
            if pv is not None:
                wait(ctx, te, "dve", pv)
            for k in range(NPL):
                wait(ctx, te, "act", ctx.t[("sqw", tau, k)])
                for nc_ in range(2):
                    pp = (ps["pE"], ps["pF"])[nc_]
                    mm = te.matmul(pp[0:1, 0:512], sb["onescolb"][:],
                                   sqb[k % 2][:, nc_ * 512:(nc_ + 1) * 512],
                                   start=(k == 0), stop=(k == NPL - 1))
                inc_k(ctx, mm, "pe", ("ssq", tau, k))
        blk.tensor(f)

        def f(v, tau=tau):
            wait(ctx, v, "pe", ctx.t[("ssq", tau, NPL - 1)])
            dvsync(ctx, v, v.tensor_scalar(rrow[0:1, 0:512], ps["pE"][0:1, 0:512],
                                           1.0 / H, 1e-5, Op.mult, Op.add))
            ctx.t[("vrow", tau)] = inc(ctx, v.tensor_scalar(
                rrow[0:1, 512:1024], ps["pF"][0:1, 0:512], 1.0 / H, 1e-5,
                Op.mult, Op.add), "dve")
        blk.vector(f)

        def f(sc, tau=tau):
            wait(ctx, sc, "dve", ctx.t[("vrow", tau)])
            ctx.t[("srow", tau)] = inc(ctx, sc.activation(
                rrow2[0:1, :], rrow[0:1, :], AF.Sqrt), "act")
        blk.scalar(f)

        def f(v, tau=tau):
            wait(ctx, v, "act", ctx.t[("srow", tau)])
            ctx.t[("rrec", tau)] = inc(ctx, v.reciprocal(rrow[0:1, :], rrow2[0:1, :]), "dve")
        blk.vector(f)

        def f(te, tau=tau):
            wait(ctx, te, "dve", ctx.t[("rrec", tau)])
            for c in range(8):
                mm = te.matmul(ps["pE"][:, c:c + 1], rrow[0:1, c * 128:(c + 1) * 128],
                               sb["identf"][0:1, 0:1], start=True, stop=True)
            ctx.t[("rtm", tau)] = inc(ctx, mm, "pe")
        blk.tensor(f)

        def f(v, tau=tau):
            wait(ctx, v, "pe", ctx.t[("rtm", tau)])
            ctx.t[("rstdev", tau)] = inc(ctx, v.tensor_copy(
                out=sb["rstdtm"][:, tau * 8:tau * 8 + 8], in_=ps["pE"][:, 0:8]), "dve")
        blk.vector(f)

        for tc in range(8):
            def f(te, tau=tau, tc=tc):
                pv = ctx.t.get(("sev", tau, tc - 1))
                if pv is None:
                    pv = ctx.t.get(("sev", 1, 7)) if tau == 0 else None
                if pv is not None:
                    wait(ctx, te, "dve", pv)
                wait(ctx, te, "dve", ctx.t[("rstdev", tau)])
                if tau == 1 and tc == 0:
                    wait(ctx, te, "ld_wos0", ctx.t["wosl_all"])
                for k in range(NPL):
                    for nc_ in range(4):
                        pp = (ps["pA"], ps["pB"], ps["pC"], ps["pD"])[nc_]
                        mm = te.matmul(pp[:, 0:512],
                                       attnT[:, k * 1024 + tc * 128: k * 1024 + (tc + 1) * 128],
                                       wosR[:, k * 2048 + nc_ * 512:
                                            k * 2048 + (nc_ + 1) * 512],
                                       start=(k == 0), stop=(k == NPL - 1))
                ctx.t[("smm", tau, tc)] = inc(ctx, mm, "pe")
            blk.tensor(f)

            def f(v, tau=tau, tc=tc):
                wait(ctx, v, "pe", ctx.t[("smm", tau, tc)])
                stk = ("sst", tau, tc - 2)
                if stk in ctx.t:
                    wait(ctx, v, "st_s0" if tc % 2 == 0 else "st_s1", ctx.t[stk])
                sp = stile[tc % 2]
                for nc_ in range(4):
                    pp = (ps["pA"], ps["pB"], ps["pC"], ps["pD"])[nc_]
                    ts_ = v.tensor_scalar(sp[:, nc_ * 512:(nc_ + 1) * 512], pp[:, 0:512],
                                          sb["rstdtm"][:, tau * 8 + tc:tau * 8 + tc + 1],
                                          None, Op.mult)
                ctx.t[("sev", tau, tc)] = inc(ctx, ts_, "dve")
            blk.vector(f)

            def f(sc, tau=tau, tc=tc):
                wait(ctx, sc, "dve", ctx.t[("sev", tau, tc)])
                r0 = tau * 1024 + tc * 128
                sem = "st_s0" if tc % 2 == 0 else "st_s1"
                inc(ctx, sc.dma_start(out=dr["s_scr"][r0:r0 + 128, :],
                                      in_=stile[tc % 2][:]), sem, 16)
                ctx.t[("sst", tau, tc)] = ctx.cnt[sem]
            blk.scalar(f)

    ctx.t["C2_done_pe"] = ctx.cnt["pe"]
    ctx.t["C2_done_dve"] = ctx.cnt["dve"]
    ctx.t["C2_done_sts"] = (ctx.cnt["st_s0"], ctx.cnt["st_s1"])


def emit_DE(ctx, blk):
    dr, sb, ps = ctx.dr, ctx.sb, ctx.ps
    ar1, ar2, ar3 = sb["AR1"][:], sb["AR2"][:], sb["AR3"][:]
    Y = ar1.bitcast(F32)
    actin = ar2[:, 0:24576].bitcast(BF16)
    decs = [ar3[:, 0:32768].bitcast(BF16), ar3[:, 32768:65536].bitcast(BF16)]
    g_ = ar3[:, 0:32768].bitcast(BF16)
    xrel = [ar2[:, 0:8192].bitcast(F32), ar2[:, 8192:16384].bitcast(F32)]
    srel = [ar2[:, 16384:20480].bitcast(BF16), ar2[:, 20480:24576].bitcast(BF16)]
    outb = [ar2[:, 24576:32768].bitcast(F32), ar2[:, 32768:40960].bitcast(F32)]

    def dch(par_, wt2, q2):
        return decs[par_][:, (wt2 * 4 + q2) * 2048:(wt2 * 4 + q2 + 1) * 2048]

    PS4 = [ps["pA"], ps["pB"], ps["pC"], ps["pD"]]
    for k in range(4):
        ctx.t[("yadd_psum", k)] = 0
    ctx.t[("eev_par", 0)] = 0
    ctx.t[("eev_par", 1)] = 0

    for tau in range(2):
        for e in range(NEA):
            ei = tau * NEA + e

            def f(s, tau=tau, e=e, ei=ei):
                k2 = ("dmm_q_done", ei - 1, 0)
                if k2 in ctx.t:
                    wait(ctx, s, "pe", ctx.t[k2])
                elif tau == 0:
                    wait(ctx, s, "pe", ctx.t["C2_done_pe"])
                if tau == 1 and e == 0:
                    wait(ctx, s, "dve", ctx.t[("ee", 0, 7)])
                    wait(ctx, s, "sto0", ctx.t[("outst", 0, 6)])
                    wait(ctx, s, "sto1", ctx.t[("outst", 0, 7)])
                inc(ctx, s.dma_start(
                    out=actin[:, 0:6144].rearrange("p (pl t) -> p pl t", pl=12),
                    in_=dr["act_scr"][e].rearrange("pl p t -> p pl t")[
                        :, :, tau * 1024:tau * 1024 + 512]),
                    "ld_actin", 16)
                ctx.t[("actl", ei, 0)] = ctx.cnt["ld_actin"]
            blk.sync(f)

            for hh in range(2):
                gidx = ei * 2 + hh
                par_ = gidx % 2

                def f(s, tau=tau, e=e, ei=ei, hh=hh, gidx=gidx, par_=par_):
                    if gidx < 2:
                        wait(ctx, s, "pe", ctx.t["C2_done_pe"])
                        wait(ctx, s, "st_s0", ctx.t["C2_done_sts"][0])
                        wait(ctx, s, "st_s1", ctx.t["C2_done_sts"][1])
                        wait(ctx, s, "st_dec0", ctx.t[("sdec", NEA - 1, PROD_NPASS - 2)])
                        wait(ctx, s, "st_dec1", ctx.t[("sdec", NEA - 1, PROD_NPASS - 1)])
                    else:
                        wait(ctx, s, "pe", ctx.t[("dchunk_free", gidx - 2)])
                    if tau == 1 and e == 0 and hh == 0:
                        wait(ctx, s, "pe", ctx.t[("etr", 0, 7, 15)])
                    inc(ctx, s.dma_start(
                        out=decs[par_][:].rearrange("p (c n) -> p c n", c=8),
                        in_=dr["ddn_scr"][e].rearrange("c p n -> p c n")[
                            :, :, hh * 2048:(hh + 1) * 2048]),
                        "ld_dn", 16)
                    ctx.t[("dnl", gidx)] = ctx.cnt["ld_dn"]
                blk.sync(f)

            def f(s, ei=ei):
                k2 = ("dmm_q_done", ei - 1, 1)
                if k2 in ctx.t:
                    wait(ctx, s, "pe", ctx.t[k2])
                inc(ctx, s.dma_start(
                    out=actin[:, 6144:12288].rearrange("p (pl t) -> p pl t", pl=12),
                    in_=dr["act_scr"][ei % NEA].rearrange("pl p t -> p pl t")[
                        :, :, (ei // NEA) * 1024 + 512:(ei // NEA) * 1024 + 1024]),
                    "ld_actin", 16)
                ctx.t[("actl", ei, 1)] = ctx.cnt["ld_actin"]
            blk.sync(f)

            preset(ctx, "dve", [("yadd", ei, q, hb)
                                for q in range(2) for hb in range(16)])

            def f(te, tau=tau, e=e, ei=ei):
                for q in range(2):
                    wait(ctx, te, "ld_actin", ctx.t[("actl", ei, q)])
                    ah = actin[:, q * 6144:(q + 1) * 6144]
                    if ei == 0 and q == 0:
                        wait(ctx, te, "dve", ctx.t[("sev", 1, 7)])
                    for hh in range(2):
                        gidx = ei * 2 + hh
                        par_ = gidx % 2
                        if q == 0:
                            wait(ctx, te, "ld_dn", ctx.t[("dnl", gidx)])
                        for hbp in range(8):
                            hb = hh * 8 + hbp
                            pk = (ei * 32 + q * 16 + hb) % 4
                            pp = PS4[pk]
                            wait(ctx, te, "dve", ctx.t[("yadd_psum", pk)])
                            kk = 0
                            for wt2 in range(2):
                                for q2 in range(4):
                                    for hlfI in ((0, 1) if wt2 == 0 else (0,)):
                                        pl = (4 * hlfI + q2) if wt2 == 0 else (8 + q2)
                                        ch = dch(par_, wt2, q2)
                                        lhsT = ch[:, 256 * hbp + hlfI:
                                                  256 * hbp + hlfI + 255:2]
                                        mm = te.matmul(pp[:, 0:512], lhsT,
                                                       ah[:, pl * 512:(pl + 1) * 512],
                                                       start=(kk == 0), stop=(kk == 11))
                                        kk += 1
                            ctx.t[("dmm", ei, q, hb)] = inc(ctx, mm, "pe")
                            ctx.t[("yadd_psum", pk)] = ctx.t[("yadd", ei, q, hb)]
                        if q == 1:
                            ctx.t[("dchunk_free", gidx)] = ctx.cnt["pe"]
                    ctx.t[("dmm_q_done", ei, q)] = ctx.cnt["pe"]
                ctx.t[("dmm_last", ei)] = ctx.cnt["pe"]
            blk.tensor(f)

            def f(v, tau=tau, e=e, ei=ei):
                for q in range(2):
                    for hh in range(2):
                        for hbp in range(8):
                            hb = hh * 8 + hbp
                            pk = (ei * 32 + q * 16 + hb) % 4
                            pp = PS4[pk]
                            wait(ctx, v, "pe", ctx.t[("dmm", ei, q, hb)])
                            if tau == 1 and e == 0:
                                wait(ctx, v, "act", ctx.t[("gw", 0, hb)])
                            ysl = Y[:, hb * 1024 + q * 512:
                                    hb * 1024 + q * 512 + 512]
                            if e == 0:
                                last = v.tensor_copy(out=ysl, in_=pp[:, 0:512])
                            else:
                                last = v.tensor_tensor(ysl, ysl, pp[:, 0:512], Op.add)
                            inc_k(ctx, last, "dve", ("yadd", ei, q, hb))
            blk.vector(f)

        ei_last = tau * NEA + NEA - 1

        def f(sc, tau=tau, ei_last=ei_last):
            for hb in range(16):
                wait(ctx, sc, "dve", ctx.t[("yadd", ei_last, 1, hb)])
                if hb == 0:
                    wait(ctx, sc, "pe", ctx.t[("dmm_last", ei_last)])
                ctx.t[("gw", tau, hb)] = inc(ctx, sc.activation(
                    g_[:, hb * 1024:(hb + 1) * 1024],
                    Y[:, hb * 1024:(hb + 1) * 1024], AF.Silu), "act")
        blk.scalar(f)

        if DBG:
            def f(g, tau=tau, ei_last=ei_last):
                for hb in range(16):
                    wait(ctx, g, "dve", ctx.t[("yadd", ei_last, 1, hb)])
                inc(ctx, g.dma_start(out=ctx.dr["Y_dbg"][tau], in_=Y[:]), "st_dbg", 16)
            blk.gpsimd(f)

        for tc in range(8):
            def f(s, tau=tau, tc=tc, ei_last=ei_last):
                pk = ("outst", tau, tc - 2)
                if pk in ctx.t:
                    wait(ctx, s, "sto0" if tc % 2 == 0 else "sto1", ctx.t[pk])
                pk = ("ee", tau, tc - 2)
                if pk in ctx.t:
                    wait(ctx, s, "dve", ctx.t[pk])
                if tc < 2:
                    wait(ctx, s, "pe", ctx.t[("dmm_last", ei_last)])
                r0 = tau * 1024 + tc * 128
                sem = "ld_ex0" if tc % 2 == 0 else "ld_ex1"
                inc(ctx, s.dma_start(out=xrel[tc % 2], in_=dr["x"][r0:r0 + 128, :]), sem, 16)
                inc(ctx, s.dma_start(out=srel[tc % 2], in_=dr["s_scr"][r0:r0 + 128, :]),
                    sem, 16)
                ctx.t[("exl", tau, tc)] = ctx.cnt[sem]
            blk.sync(f)

            preset(ctx, "dve", [("eev", tau, tc, hb) for hb in range(16)])

            def f(te, tau=tau, tc=tc):
                for hb in range(16):
                    wait(ctx, te, "act", ctx.t[("gw", tau, hb)])
                    if hb >= 2:
                        wait(ctx, te, "dve", ctx.t[("eev", tau, tc, hb - 2)])
                    else:
                        wait(ctx, te, "dve", ctx.t[("eev_par", hb % 2)])
                    pT = ps["pT1"] if hb % 2 == 0 else ps["pT2"]
                    ctx.t[("etr", tau, tc, hb)] = inc(
                        ctx, te.transpose(pT[:, 0:128],
                                          g_[:, hb * 1024 + tc * 128:
                                             hb * 1024 + (tc + 1) * 128],
                                          sb["identb"][:]), "pe")
            blk.tensor(f)

            def f(v, tau=tau, tc=tc):
                wait(ctx, v, "ld_ex0" if tc % 2 == 0 else "ld_ex1", ctx.t[("exl", tau, tc)])
                for hb in range(16):
                    wait(ctx, v, "pe", ctx.t[("etr", tau, tc, hb)])
                    pT = ps["pT1"] if hb % 2 == 0 else ps["pT2"]
                    v.tensor_tensor(sb["etmp"][:], pT[:, 0:128],
                                    srel[tc % 2][:, hb * 128:(hb + 1) * 128], Op.mult)
                    inc_k(ctx, v.tensor_tensor(outb[tc % 2][:, hb * 128:(hb + 1) * 128],
                                               sb["etmp"][:],
                                               xrel[tc % 2][:, hb * 128:(hb + 1) * 128],
                                               Op.add),
                          "dve", ("eev", tau, tc, hb))
                    ctx.t[("eev_par", hb % 2)] = ctx.t[("eev", tau, tc, hb)]
                ctx.t[("ee", tau, tc)] = ctx.cnt["dve"]
            blk.vector(f)

            def f(sc, tau=tau, tc=tc):
                wait(ctx, sc, "dve", ctx.t[("ee", tau, tc)])
                r0 = tau * 1024 + tc * 128
                sem = "sto0" if tc % 2 == 0 else "sto1"
                inc(ctx, sc.dma_start(out=dr["out"][r0:r0 + 128, :],
                                      in_=outb[tc % 2][:]), sem, 16)
                ctx.t[("outst", tau, tc)] = ctx.cnt[sem]
            blk.scalar(f)


def emit_end(ctx, blk):
    def f(g):
        g.wait_ge(ctx.sem["sto0"], ctx.cnt["sto0"])
        g.wait_ge(ctx.sem["sto1"], ctx.cnt["sto1"])
        if ctx.cnt["st_dbg"]:
            g.wait_ge(ctx.sem["st_dbg"], ctx.cnt["st_dbg"])
    blk.gpsimd(f)


_NC = None


def kernel(**inputs):
    global _NC, _CTX_RC
    cons = host_consts()
    prep = prep_inputs(inputs)
    if _NC is None:
        _CTX_RC = cons["_r_C"]
        _ctx, _NC = build_nc()
    x = np.ascontiguousarray(np.asarray(inputs["L0_attn_x"], dtype=np.float32))
    base = {}
    base.update(prep)
    base.update({k: v for k, v in cons.items() if not k.startswith("_")})
    in_maps = [dict(base, x=x[b]) for b in range(8)]
    res = run_bass_kernel_spmd(_NC, in_maps, core_ids=list(range(8)))
    return np.stack([res.results[b]["out"] for b in range(8)], axis=0)
